# version 02
# baseline (speedup 1.0000x reference)
import numpy as np
import ml_dtypes
from contextlib import ExitStack
import concourse.bass as bass
import concourse.mybir as mybir
from concourse.bass_utils import run_bass_kernel_spmd

F32 = mybir.dt.float32
BF16 = mybir.dt.bfloat16
AF = mybir.ActivationFunctionType
OP = mybir.AluOpType
AX = mybir.AxisListType
BF = ml_dtypes.bfloat16


def full_cfg():
    return dict(D=2048, I=8192, TLOC=2048, TC=512, NRANK=4, n_cores=8, eps=1e-5)


def build_nc(cfg):
    D = cfg["D"]; I = cfg["I"]; TLOC = cfg["TLOC"]; TC = cfg["TC"]
    NRANK = cfg["NRANK"]; n_cores = cfg["n_cores"]; eps = cfg["eps"]
    KCH = D // 128
    ICH = I // 128
    GC = I // 512
    NTILE = TLOC // 128
    NTC = TLOC // TC
    TPT = TC // 128
    SFULL = NRANK * TLOC
    NKB = SFULL // 512
    VCH = SFULL // 128
    FH = max(1, (TPT * D * 4) // 16384)
    FHW = D // FH
    NT = NTILE
    ARAW = max(ICH * TC, 4 * SFULL, NTILE * D, 8 * D)

    nc = bass.Bass(trn_type="TRN2", target_bir_lowering=False)

    x_d = nc.dram_tensor("x", [TLOC, D], F32, kind="ExternalInput").ap()
    wg_d = nc.dram_tensor("wg", [4, GC, 128, KCH * 128], F32, kind="ExternalInput").ap()
    wu_d = nc.dram_tensor("wu", [4, GC, 128, KCH * 128], F32, kind="ExternalInput").ap()
    wd_d = nc.dram_tensor("wd", [GC, 4, 128, D], F32, kind="ExternalInput").ap()
    wk_d = nc.dram_tensor("wk", [128, KCH * 128], F32, kind="ExternalInput").ap()
    wv_d = nc.dram_tensor("wv", [128, KCH * 128], F32, kind="ExternalInput").ap()
    wo_d = nc.dram_tensor("wo", [KCH, 128, D], F32, kind="ExternalInput").ap()
    bk_d = nc.dram_tensor("bk", [128, 1], F32, kind="ExternalInput").ap()
    bv_d = nc.dram_tensor("bv", [1, 128], F32, kind="ExternalInput").ap()
    bo_d = nc.dram_tensor("bo", [1, D], F32, kind="ExternalInput").ap()
    id_d = nc.dram_tensor("ident", [128, 128], BF16, kind="ExternalInput").ap()
    out_d = nc.dram_tensor("out", [TLOC, D], F32, kind="ExternalOutput").ap()
    kvs_d = nc.dram_tensor("kv_send", [2, 128, TLOC], BF16).ap()
    kvr_d = nc.dram_tensor("kv_recv", [NRANK, 2, 128, TLOC], BF16).ap()

    replica_groups = [[g * NRANK + i for i in range(NRANK)]
                      for g in range(n_cores // NRANK)]

    st = ExitStack()
    with st:
        X = st.enter_context(nc.sbuf_tensor("Xs", [128, NTILE * D], BF16)).ap()
        ARA = st.enter_context(nc.sbuf_tensor("ARA", [128, ARAW], BF16)).ap()
        LNF = st.enter_context(nc.sbuf_tensor("LNF", [128, KCH * TC], BF16)).ap()
        WGU = st.enter_context(nc.sbuf_tensor("WGU", [128, 4 * KCH * 128], BF16)).ap()
        WST = st.enter_context(nc.sbuf_tensor("WST", [128, 2 * KCH * 128], F32)).ap()
        WDB = st.enter_context(nc.sbuf_tensor("WDB", [128, max(2 * FHW, 4 * TC)], BF16)).ap()
        HR = st.enter_context(nc.sbuf_tensor("HRs", [128, 4 * TC], F32)).ap()
        SCR = st.enter_context(nc.sbuf_tensor("SCR", [128, 3 * TC], F32)).ap()
        WOE = st.enter_context(nc.sbuf_tensor("WOE", [128, D], BF16)).ap()
        SSTA = st.enter_context(nc.sbuf_tensor("SSTA", [128, 4 * NT], F32)).ap()
        SSTB = st.enter_context(nc.sbuf_tensor("SSTB", [128, 4 * NT], F32)).ap()
        MXB = st.enter_context(nc.sbuf_tensor("MXB", [128, NKB + 4], F32)).ap()
        IDN = st.enter_context(nc.sbuf_tensor("IDN", [128, 128], BF16)).ap()
        ONE = st.enter_context(nc.sbuf_tensor("ONEs", [1, 128], BF16)).ap()
        BKB = st.enter_context(nc.sbuf_tensor("BKB", [128, 1], F32)).ap()
        BVR = st.enter_context(nc.sbuf_tensor("BVR", [1, 128], BF16)).ap()
        BOR = st.enter_context(nc.sbuf_tensor("BOR", [1, D], BF16)).ap()
        PS = st.enter_context(nc.psum_tensor("PS", [128, 4096], F32)).ap()

        sA = st.enter_context(nc.semaphore("sA"))
        sV = st.enter_context(nc.semaphore("sV"))
        sP = st.enter_context(nc.semaphore("sP"))
        sC = st.enter_context(nc.semaphore("sC"))
        sG = st.enter_context(nc.semaphore("sG"))
        sWs = [st.enter_context(nc.semaphore(f"sW{k}")) for k in range(4)]
        sGs = [st.enter_context(nc.semaphore(f"sG{k}")) for k in range(2)]
        cnt = {}
        for s_ in [sA, sV, sP, sC, sG] + sWs + sGs:
            cnt[id(s_)] = 0

        def inc(instr, sem, amt=1):
            instr.then_inc(sem, amt)
            cnt[id(sem)] += amt
            return (sem, cnt[id(sem)])

        def now(sem):
            return (sem, cnt[id(sem)])

        def wt(eng, tok):
            if isinstance(eng, Eng):
                eng = eng.eng
            if tok and tok[1] > 0:
                eng.wait_ge(tok[0], tok[1])

        def dma(eng, out, in_, sem):
            if cnt[id(sem)] > 0:
                eng.wait_ge(sem, cnt[id(sem)])
            return inc(eng.dma_start(out=out, in_=in_), sem, 16)

        class Eng:
            def __init__(self, eng, sem):
                self.eng = eng
                self.sem = sem

            def __getattr__(self, name):
                if name in ("wait_ge",):
                    return getattr(self.eng, name)
                fn = getattr(self.eng, name)

                def call(*a, **k):
                    if cnt[id(self.sem)] > 0:
                        self.eng.wait_ge(self.sem, cnt[id(self.sem)])
                    return inc(fn(*a, **k), self.sem)
                return call

        VE = Eng(nc.vector, sV)
        SE = Eng(nc.scalar, sA)

        ARAf = ARA.bitcast(F32)
        WSTb = WST.bitcast(BF16)
        HRb = HR.bitcast(BF16)
        SCRb = SCR.bitcast(BF16)
        PSb = PS.bitcast(BF16)

        def xt(t):
            return X[:, t * D:(t + 1) * D]

        def hsp(ci):
            return ARA[:, ci * TC:(ci + 1) * TC]

        def bank(b):
            return PS[:, b * 512:(b + 1) * 512]

        def bankb(b):
            return PSb[:, b * 1024:(b + 1) * 1024]

        if KCH * TC >= SFULL and 4 * KCH * 128 >= SFULL:
            KFB = LNF[:, 0:SFULL]
            VKB = WGU[:, 0:SFULL]
        else:
            KFB = st.enter_context(nc.sbuf_tensor("KFBt", [128, SFULL], BF16)).ap()
            VKB = st.enter_context(nc.sbuf_tensor("VKBt", [128, SFULL], BF16)).ap()
        QF = WSTb[:, 0:TLOC]
        AFB = WSTb[:, TLOC:2 * TLOC]
        OUTW = [ARAf[:, (ARAW // 2) - 2 * D + s * D: (ARAW // 2) - 2 * D + (s + 1) * D]
                for s in range(2)]
        out_guard_ci = (ARAW - 4 * D) // TC

        dma(nc.sync, IDN[:, :], id_d[:, :], sWs[0])
        dma(nc.sync, BKB[:, :], bk_d[:, :], sWs[1])
        dma(nc.sync, WST[0:1, 0:128], bv_d[:, :], sWs[2])
        dma(nc.sync, WST[0:1, 128:128 + D], bo_d[:, :], sWs[3])
        tok_const = now(sWs[3])

        for s_ in sWs:
            VE.wait_ge(s_, cnt[id(s_)])
        VE.memset(ONE[:, :], 1.0)
        VE.tensor_copy(out=BVR[:, :], in_=WST[0:1, 0:128])
        tok_smallcast = VE.tensor_copy(out=BOR[:, :], in_=WST[0:1, 128:128 + D])

        WOACC = ARAf[:, 0:D]
        wo_dma = []
        wo_add = []
        wt(nc.sync, tok_smallcast)
        for c in range(KCH):
            s = c % 2
            if c >= 2:
                wt(nc.sync, wo_add[c - 2])
            wo_dma.append(dma(nc.sync, WST[:, s * D:(s + 1) * D], wo_d[c], sWs[s]))
            wt(nc.vector, wo_dma[c])
            if c == 0:
                i = VE.tensor_copy(out=WOACC, in_=WST[:, 0:D])
            else:
                i = VE.tensor_add(WOACC, WOACC, WST[:, s * D:(s + 1) * D])
            wo_add.append(i)
        tok_woe = VE.tensor_copy(out=WOE[:, :], in_=WOACC)

        def xstg(s):
            return ARAf[:, (1 + s) * D:(2 + s) * D]

        SQS = ARAf[:, 3 * D:4 * D]
        xcast = []
        for t in range(NTILE):
            s = t % 2
            if t >= 2:
                wt(nc.gpsimd, xcast[2 * (t - 2)])
            tok = dma(nc.gpsimd, xstg(s), x_d[t * 128:(t + 1) * 128, :], sGs[s])
            wt(nc.scalar, tok)
            if t == 0:
                wt(nc.scalar, tok_woe)
            i = SE.activation(out=xt(t), in_=xstg(s), func=AF.Copy,
                                     accum_out=SSTA[:, t:t + 1])
            xcast.append(i)
            i = SE.activation(out=SQS, in_=xt(t), func=AF.Square,
                                     accum_out=SSTA[:, NT + t:NT + t + 1])
            xcast.append(i)
        tok_x_loaded = now(sA)

        def ln_finalize(SST, wait_tok):
            mean = SST[:, 0:NT]; ssq = SST[:, NT:2 * NT]
            rr = SST[:, 2 * NT:3 * NT]; nmr = SST[:, 3 * NT:4 * NT]
            wt(nc.vector, wait_tok)
            VE.tensor_scalar_mul(mean, mean, 1.0 / D)
            VE.tensor_scalar_mul(ssq, ssq, 1.0 / D)
            VE.tensor_mul(rr, mean, mean)
            VE.tensor_sub(ssq, ssq, rr)
            VE.tensor_scalar_add(ssq, ssq, eps)
            tv = VE.reciprocal(rr, ssq)
            wt(nc.scalar, tv)
            ta = SE.activation(out=rr, in_=rr, func=AF.Sqrt)
            wt(nc.vector, ta)
            VE.tensor_mul(nmr, mean, rr)
            return VE.tensor_scalar_mul(nmr, nmr, -1.0)

        tok_ln1 = ln_finalize(SSTA, tok_x_loaded)

        pipe = dict(out_tok=None)

        def emit_ffn(SST, stats_tok, write_out, hr_free_tok, ara_free_tok,
                     wst_free_tok=None):
            LNF3 = LNF.rearrange("p (k t) -> p k t", k=KCH)
            spars_last = hr_free_tok
            ara_tok = ara_free_tok
            ev2_last = None
            if wst_free_tok:
                pipe['wst_free'] = wst_free_tok
            for tc in range(NTC):
                tiles = [tc * TPT + j for j in range(TPT)]
                lts = []
                tp = {}
                ev = {}
                fills = [(j, k0) for j in range(TPT) for k0 in range(0, KCH, 4)]
                for fi, (j, k0) in enumerate(fills):
                    t = tiles[j]
                    s = j % 2
                    kw = min(4, KCH - k0)
                    b = fi % 2
                    if k0 == 0:
                        if j == 0:
                            wt(nc.vector, stats_tok)
                            wt(nc.vector, spars_last)
                        if j >= 2:
                            wt(nc.vector, tp[(j - 2, 'last')])
                        i = VE.tensor_scalar(
                            out=HRb[:, s * D:(s + 1) * D], in0=xt(t),
                            scalar1=SST[:, 2 * NT + t:2 * NT + t + 1],
                            scalar2=SST[:, 3 * NT + t:3 * NT + t + 1],
                            op0=OP.mult, op1=OP.add)
                        lts.append(i)
                    wt(nc.tensor, lts[j])
                    if fi >= 2:
                        wt(nc.tensor, ev[fi - 2])
                    elif ev2_last:
                        wt(nc.tensor, ev2_last)
                    for kk in range(kw):
                        i = nc.tensor.transpose(
                            out=bankb(b)[:, kk * 128:(kk + 1) * 128],
                            in_=HRb[:, s * D + (k0 + kk) * 128: s * D + (k0 + kk + 1) * 128],
                            identity=IDN[:, :])
                    tok = inc(i, sP)
                    tp[fi] = tok
                    tp[(j, 'last')] = tok
                    wt(nc.vector, tok)
                    src = bankb(b)[:, 0:kw * 128].rearrange("p (k t) -> p k t", k=kw)
                    i = VE.tensor_copy(
                        out=LNF3[:, k0:k0 + kw, j * 128:(j + 1) * 128], in_=src)
                    ev[fi] = i
                tok_lnf = now(sV)

                dma_tok = {}
                cast_tok = {}
                silu_tok = {}
                mult_tok = {}
                g_tok = {}
                u_tok = {}
                spars_tok = {}
                HW = KCH * 128 // 2
                for si in range(GC * 4):
                    gc, r = si // 4, si % 4
                    p = si % 2
                    for it in range(4):
                        item = si * 4 + it
                        slot = item % 4
                        gu, hh = it // 2, it % 2
                        if item >= 4:
                            wt(nc.sync, cast_tok[item - 4])
                        else:
                            wt(nc.sync, pipe.get('wst_free'))
                        src = (wg_d if gu == 0 else wu_d)[r, gc]
                        dma_tok[item] = dma(nc.sync, WST[:, slot * HW:(slot + 1) * HW],
                                            src[:, hh * HW:(hh + 1) * HW], sWs[slot])
                    for it in range(4):
                        item = si * 4 + it
                        slot = item % 4
                        gu, hh = it // 2, it % 2
                        wt(nc.scalar, dma_tok[item])
                        if si >= 2 and it == 0:
                            wt(nc.scalar, u_tok[si - 2])
                        elif si < 2 and it == 0:
                            wt(nc.scalar, pipe.get('wgu_free'))
                        dst = WGU[:, (gu * 2 + p) * KCH * 128:(gu * 2 + p + 1) * KCH * 128]
                        i = SE.activation(out=dst[:, hh * HW:(hh + 1) * HW],
                                                 in_=WST[:, slot * HW:(slot + 1) * HW],
                                                 func=AF.Copy)
                        cast_tok[item] = i
                    if si == 0:
                        wt(nc.tensor, tok_lnf)
                    wt(nc.tensor, cast_tok[si * 4 + 3])
                    if si >= 2:
                        wt(nc.tensor, silu_tok[si - 2])
                        wt(nc.tensor, mult_tok[si - 2])
                    gw = WGU[:, p * KCH * 128:(p + 1) * KCH * 128]
                    uw = WGU[:, (2 + p) * KCH * 128:(2 + p + 1) * KCH * 128]
                    for k in range(KCH):
                        i = nc.tensor.matmul(out=bank(p)[:, 0:TC],
                                             lhsT=gw[:, k * 128:(k + 1) * 128],
                                             rhs=LNF[:, k * TC:(k + 1) * TC],
                                             start=(k == 0), stop=(k == KCH - 1))
                    g_tok[si] = inc(i, sP)
                    for k in range(KCH):
                        i = nc.tensor.matmul(out=bank(2 + p)[:, 0:TC],
                                             lhsT=uw[:, k * 128:(k + 1) * 128],
                                             rhs=LNF[:, k * TC:(k + 1) * TC],
                                             start=(k == 0), stop=(k == KCH - 1))
                    u_tok[si] = inc(i, sP)
                    wt(nc.scalar, g_tok[si])
                    if si >= 4:
                        wt(nc.scalar, spars_tok[si // 4 - 1])
                    i = SE.activation(out=HR[:, r * TC:(r + 1) * TC],
                                             in_=bank(p)[:, 0:TC], func=AF.Silu)
                    silu_tok[si] = i
                    wt(nc.vector, silu_tok[si])
                    wt(nc.vector, u_tok[si])
                    i = VE.tensor_mul(HR[:, r * TC:(r + 1) * TC],
                                             HR[:, r * TC:(r + 1) * TC],
                                             bank(2 + p)[:, 0:TC])
                    mult_tok[si] = i
                    if r == 3:
                        WDF = WDB.bitcast(F32)
                        wt(nc.scalar, mult_tok[si])
                        if si >= 4:
                            wt(nc.scalar, spars_tok[si // 4 - 1])
                        elif ev2_last:
                            wt(nc.scalar, ev2_last)
                        T = [PS[:, 4 * 512:4 * 512 + TC],
                             WDF[:, 0:TC],
                             PS[:, 5 * 512:5 * 512 + TC],
                             WDF[:, TC:2 * TC]]
                        for rr in range(4):
                            i = SE.activation(out=T[rr],
                                              in_=HR[:, rr * TC:(rr + 1) * TC],
                                              func=AF.Square)
                        sq_tok = i
                        wt(nc.vector, sq_tok)
                        if tc == 0 and gc == 0:
                            wt(nc.vector, ara_tok)
                        Hh = [HR[:, rr * TC:(rr + 1) * TC] for rr in range(4)]
                        S0 = SCR[:, 0:TC]; S1 = SCR[:, TC:2 * TC]; S2 = SCR[:, 2 * TC:3 * TC]
                        VE.tensor_tensor(S0, T[0], T[1], OP.max)
                        VE.tensor_tensor(S1, T[2], T[3], OP.max)
                        VE.tensor_tensor(S2, T[0], T[1], OP.min)
                        VE.tensor_tensor(S0, S0, S1, OP.min)
                        VE.tensor_tensor(S1, T[2], T[3], OP.min)
                        VE.tensor_tensor(S1, S2, S1, OP.max)
                        VE.tensor_tensor(S0, S0, S1, OP.max)
                        if write_out and (gc + 1) * 4 > out_guard_ci:
                            wt(nc.vector, pipe['out_tok'])
                            wt(nc.vector, pipe.get('out_tok2'))
                        for rr in range(4):
                            ci = gc * 4 + rr
                            VE.tensor_tensor(T[rr], T[rr], S0, OP.is_ge)
                            i = VE.tensor_mul(hsp(ci), Hh[rr], T[rr])
                        spars_tok[gc] = i
                spars_last = spars_tok[GC - 1]
                pipe['wst_free'] = cast_tok[GC * 16 - 1]
                pipe['wgu_free'] = u_tok[GC * 4 - 1]

                dd = {}
                dc = {}
                pd = {}
                DSL = min(4, max(2, (2 * KCH * 128) // FHW))
                for fh in range(FH):
                    for ci in range(ICH):
                        item = fh * ICH + ci
                        slot = item % DSL
                        gc, r = ci // 4, ci % 4
                        if item >= DSL:
                            wt(nc.sync, dc[item - DSL])
                        else:
                            wt(nc.sync, pipe.get('wst_free'))
                        dd[item] = dma(nc.sync, WST[:, slot * FHW:(slot + 1) * FHW],
                                       wd_d[gc, r][:, fh * FHW:(fh + 1) * FHW], sWs[slot])
                        wt(nc.scalar, dd[item])
                        if item >= 2:
                            wt(nc.scalar, pd.get(item - 2))
                        else:
                            wt(nc.scalar, pipe.get('wdb_free'))
                            wt(nc.scalar, spars_last)
                        i = SE.activation(out=WDB[:, (item % 2) * FHW:(item % 2 + 1) * FHW],
                                                 in_=WST[:, slot * FHW:(slot + 1) * FHW],
                                                 func=AF.Copy)
                        dc[item] = i
                        wt(nc.tensor, dc[item])
                        if ci == 0:
                            wt(nc.tensor, spars_last)
                            if fh > 0:
                                wt(nc.tensor, ev2_last)
                        wdb = WDB[:, (item % 2) * FHW:(item % 2 + 1) * FHW]
                        nsl = min(512, FHW)
                        GRP = FHW // nsl
                        for m in range(TPT):
                            for nn_ in range(GRP):
                                bb = (m * GRP + nn_) * 512
                                i = nc.tensor.matmul(
                                    out=PS[:, bb: bb + nsl],
                                    lhsT=hsp(ci)[:, m * 128:(m + 1) * 128],
                                    rhs=wdb[:, nn_ * nsl:(nn_ + 1) * nsl],
                                    start=(ci == 0), stop=(ci == ICH - 1))
                        pd[item] = inc(i, sP)
                    wt(nc.vector, pd[fh * ICH + ICH - 1])
                    nsl = min(512, FHW)
                    GRP = FHW // nsl
                    for m, t in enumerate(tiles):
                        for nn_ in range(GRP):
                            bb = (m * GRP + nn_) * 512
                            i = VE.scalar_tensor_tensor(
                                out=xt(t)[:, fh * FHW + nn_ * nsl: fh * FHW + (nn_ + 1) * nsl],
                                in0=PS[:, bb: bb + nsl],
                                scalar=0.5,
                                in1=xt(t)[:, fh * FHW + nn_ * nsl: fh * FHW + (nn_ + 1) * nsl],
                                op0=OP.mult, op1=OP.add)
                    ev2_last = i
                pipe['wst_free'] = dc[FH * ICH - 1]
                pipe['wdb_free'] = pd[FH * ICH - 1]

                if write_out:
                    oc = {}
                    oo = pipe.setdefault('oo_slot', {})
                    for m, t in enumerate(tiles):
                        s = m % 2
                        wt(nc.scalar, ev2_last)
                        wt(nc.scalar, oo.get(s))
                        i = SE.activation(out=OUTW[s], in_=xt(t), func=AF.Copy)
                        oc[m] = i
                        wt(nc.gpsimd, oc[m])
                        oo[s] = dma(nc.gpsimd, out_d[t * 128:(t + 1) * 128, :],
                                    OUTW[s], sGs[s])
                    pipe['out_tok'] = oo.get(0)
                    pipe['out_tok2'] = oo.get(1)
            return spars_last, ev2_last

        spars_last, ev2_last = emit_ffn(SSTA, tok_ln1, False, None, tok_x_loaded,
                                        wst_free_tok=tok_woe)

        wt(nc.scalar, ev2_last)
        SQ2 = ARAf[:, 0:D]
        for t in range(NTILE):
            ta = SE.activation(out=SQ2, in_=xt(t), func=AF.Square,
                                          accum_out=SSTB[:, NT + t:NT + t + 1])
            wt(nc.vector, ta)
            VE.reduce_sum(SSTB[:, t:t + 1], xt(t), axis=AX.X)
        tok_ln2 = ln_finalize(SSTB, now(sV))

        groups = []
        remaining = KCH
        k0 = 0
        for buf, width in [(LNF, KCH * TC), (WGU, 4 * KCH * 128),
                           (WSTb, 4 * KCH * 128), (WDB, 2 * FHW),
                           (SCRb, 6 * TC), (HRb, 8 * TC)]:
            nch = min(width // TLOC, remaining)
            if nch > 0:
                groups.append((buf, k0, nch))
                k0 += nch
                remaining -= nch
            if remaining == 0:
                break
        assert remaining == 0, (remaining, KCH)
        ln2f_views = []
        for buf, gk0, nch in groups:
            for q in range(nch):
                ln2f_views.append(buf[:, q * TLOC:(q + 1) * TLOC])

        l2t = []
        for t in range(NTILE):
            if t == 0:
                wt(nc.vector, tok_ln2)
            i = VE.tensor_scalar(
                out=ARA[:, t * D:(t + 1) * D], in0=xt(t),
                scalar1=SSTB[:, 2 * NT + t:2 * NT + t + 1],
                scalar2=SSTB[:, 3 * NT + t:3 * NT + t + 1],
                op0=OP.mult, op1=OP.add)
            l2t.append(i)

        ev3 = {}
        tile_done_tok = {}
        fi = 0
        for t in range(NTILE):
            for k0 in range(0, KCH, 4):
                kw = min(4, KCH - k0)
                b = fi % 2
                wt(nc.tensor, l2t[t])
                if fi >= 2:
                    wt(nc.tensor, ev3[fi - 2])
                elif ev2_last:
                    wt(nc.tensor, ev2_last)
                for kk in range(kw):
                    i = nc.tensor.transpose(
                        out=bankb(b)[:, kk * 128:(kk + 1) * 128],
                        in_=ARA[:, t * D + (k0 + kk) * 128: t * D + (k0 + kk + 1) * 128],
                        identity=IDN[:, :])
                tok = inc(i, sP)
                wt(nc.vector, tok)
                for kk in range(kw):
                    i = VE.tensor_copy(
                        out=ln2f_views[k0 + kk][:, t * 128:(t + 1) * 128],
                        in_=bankb(b)[:, kk * 128:(kk + 1) * 128])
                ev3[fi] = i
                fi += 1
            tile_done_tok[t] = now(sV)
        tok_ln2f = now(sV)

        WKc = ARA[:, 0:KCH * 128]
        WVc = ARA[:, KCH * 128:2 * KCH * 128]
        WKVST = ARAf[:, KCH * 128:2 * KCH * 128]
        need_tiles = (4 * KCH * 128 + D - 1) // D
        wt(nc.sync, tile_done_tok[min(need_tiles - 1, NTILE - 1)])
        tok = dma(nc.sync, WKVST, wk_d[:, :], sWs[0])
        wt(nc.scalar, tok)
        tok_wk = SE.activation(out=WKc, in_=WKVST, func=AF.Copy)
        wt(nc.sync, tok_wk)
        tok = dma(nc.sync, WKVST, wv_d[:, :], sWs[0])
        wt(nc.scalar, tok)
        tok_wv = SE.activation(out=WVc, in_=WKVST, func=AF.Copy)

        KLOC = ARA[:, 2 * KCH * 128:2 * KCH * 128 + TLOC]
        VLOC = ARA[:, 2 * KCH * 128 + TLOC:2 * KCH * 128 + 2 * TLOC]
        wt(nc.tensor, tok_wv)
        wt(nc.tensor, tok_ln2f)
        for k in range(KCH):
            for nn_ in range(TLOC // 512):
                i = nc.tensor.matmul(out=PS[:, nn_ * 512:(nn_ + 1) * 512],
                                     lhsT=WKc[:, k * 128:(k + 1) * 128],
                                     rhs=ln2f_views[k][:, nn_ * 512:(nn_ + 1) * 512],
                                     start=(k == 0), stop=(k == KCH - 1))
        tok_kmm = inc(i, sP)
        wt(nc.vector, tok_kmm)
        VE.tensor_scalar(out=KLOC, in0=PS[:, 0:TLOC],
                         scalar1=BKB[:, 0:1], scalar2=None, op0=OP.add)
        tok_kv = None
        for rnd in range((NTILE + 3) // 4):
            ms = list(range(rnd * 4, min(rnd * 4 + 4, NTILE)))
            if rnd > 0:
                wt(nc.tensor, tok_kv)
            for k in range(KCH):
                for m in ms:
                    i = nc.tensor.matmul(
                        out=PS[:, 2048 + (m % 4) * 512: 2048 + (m % 4) * 512 + 128],
                        lhsT=ln2f_views[k][:, m * 128:(m + 1) * 128],
                        rhs=WVc[:, k * 128:(k + 1) * 128],
                        start=(k == 0), stop=False)
            for m in ms:
                i = nc.tensor.matmul(
                    out=PS[:, 2048 + (m % 4) * 512: 2048 + (m % 4) * 512 + 128],
                    lhsT=ONE[0:1, :], rhs=BVR[0:1, :],
                    start=False, stop=True)
            tok_vr = inc(i, sP)
            wt(nc.vector, tok_vr)
            for m in ms:
                tok_kv = VE.tensor_copy(
                    out=VLOC[:, m * 128:(m + 1) * 128],
                    in_=PS[:, 2048 + (m % 4) * 512: 2048 + (m % 4) * 512 + 128])

        wt(nc.gpsimd, tok_kv)
        dma(nc.gpsimd, kvs_d[0], KLOC, sG)
        tok = dma(nc.gpsimd, kvs_d[1], VLOC, sG)
        wt(nc.gpsimd, tok)
        i = nc.gpsimd.collective_compute(
            "AllGather", OP.bypass, replica_groups=replica_groups,
            ins=[kvs_d.opt()], outs=[kvr_d.opt()])
        tok_coll = inc(i, sC)
        wt(nc.gpsimd, tok_coll)
        wt(nc.gpsimd, now(sP))
        for rk in range(NRANK):
            dma(nc.gpsimd, KFB[:, rk * TLOC:(rk + 1) * TLOC], kvr_d[rk, 0], sG)
            dma(nc.gpsimd, VKB[:, rk * TLOC:(rk + 1) * TLOC], kvr_d[rk, 1], sG)
        tok_kvfull = now(sG)

        qev = {}
        for m0 in range(0, NTILE, 4):
            b = (m0 // 4) % 2
            mw = min(4, NTILE - m0)
            if m0 == 0:
                wt(nc.tensor, tok_kv)
            if m0 >= 8:
                wt(nc.tensor, qev[m0 - 8])
            for mm in range(mw):
                i = nc.tensor.transpose(out=bankb(b)[:, mm * 128:(mm + 1) * 128],
                                        in_=VLOC[:, (m0 + mm) * 128:(m0 + mm + 1) * 128],
                                        identity=IDN[:, :])
            tok = inc(i, sP)
            wt(nc.vector, tok)
            i = VE.tensor_copy(out=QF[:, m0 * 128:(m0 + mw) * 128],
                                      in_=bankb(b)[:, 0:mw * 128])
            qev[m0] = i
        tok_qf = now(sV)

        SCb = ARAf[:, 0:SFULL]
        PB = ARA[:, 2 * SFULL:3 * SFULL]
        PT = ARA[:, 3 * SFULL:4 * SFULL]
        prev_cp = {}
        af_tok = {}
        for qt in range(NTILE):
            cp = {}
            for ib in range(NKB):
                b = ib % 4
                if qt == 0 and ib == 0:
                    wt(nc.tensor, tok_qf)
                    wt(nc.tensor, tok_kvfull)
                if ib >= 4:
                    wt(nc.tensor, cp[ib - 4])
                elif qt > 0:
                    wt(nc.tensor, prev_cp[NKB - 4 + ib])
                i = nc.tensor.matmul(out=bank(b),
                                     lhsT=QF[:, qt * 128:(qt + 1) * 128],
                                     rhs=KFB[:, ib * 512:(ib + 1) * 512],
                                     start=True, stop=True)
                tok = inc(i, sP)
                wt(nc.scalar, tok)
                i = SE.activation(out=SCb[:, ib * 512:(ib + 1) * 512],
                                         in_=bank(b), func=AF.Copy)
                cp[ib] = i
                wt(nc.vector, cp[ib])
                VE.reduce_max(MXB[:, ib:ib + 1],
                                     SCb[:, ib * 512:(ib + 1) * 512], axis=AX.X)
            prev_cp = cp
            VE.reduce_max(MXB[:, NKB:NKB + 1], MXB[:, 0:NKB], axis=AX.X)
            tok_mx = VE.tensor_scalar_mul(
                MXB[:, NKB:NKB + 1], MXB[:, NKB:NKB + 1], -1.0 / 128.0)
            wt(nc.scalar, tok_mx)
            i = SE.activation(out=PB, in_=SCb, func=AF.Exp,
                                     bias=MXB[:, NKB:NKB + 1], scale=1.0 / 128.0,
                                     accum_out=MXB[:, NKB + 1:NKB + 2])
            tok_exp = i
            wt(nc.vector, tok_exp)
            VE.reciprocal(MXB[:, NKB + 2:NKB + 3], MXB[:, NKB + 1:NKB + 2])
            tok_pn = VE.tensor_scalar(
                out=PB, in0=PB, scalar1=MXB[:, NKB + 2:NKB + 3],
                scalar2=None, op0=OP.mult)
            ptv = {}
            for ci0 in range(0, VCH, 4):
                b = 4 + (ci0 // 4) % 2
                if ci0 == 0:
                    wt(nc.tensor, tok_pn)
                if ci0 >= 8:
                    wt(nc.tensor, ptv[ci0 - 8])
                for cc in range(4):
                    i = nc.tensor.transpose(
                        out=bankb(b)[:, cc * 128:(cc + 1) * 128],
                        in_=PB[:, (ci0 + cc) * 128:(ci0 + cc + 1) * 128],
                        identity=IDN[:, :])
                tok = inc(i, sP)
                wt(nc.vector, tok)
                i = VE.tensor_copy(out=PT[:, ci0 * 128:(ci0 + 4) * 128],
                                          in_=bankb(b)[:, 0:512])
                ptv[ci0] = i
            tok_pt = now(sV)
            wt(nc.tensor, tok_pt)
            if qt >= 2:
                wt(nc.tensor, af_tok[qt - 2])
            bav = 6 + (qt % 2)
            for ci in range(VCH):
                i = nc.tensor.matmul(out=PS[:, bav * 512:bav * 512 + 128],
                                     lhsT=VKB[:, ci * 128:(ci + 1) * 128],
                                     rhs=PT[:, ci * 128:(ci + 1) * 128],
                                     start=(ci == 0), stop=(ci == VCH - 1))
            tok_av = inc(i, sP)
            wt(nc.vector, tok_av)
            i = VE.tensor_copy(out=AFB[:, qt * 128:(qt + 1) * 128],
                                      in_=PS[:, bav * 512:bav * 512 + 128])
            af_tok[qt] = i
        tok_af = now(sV)

        wo_mm = {}
        wo_ev = {}
        for t in range(NTILE):
            b0 = (t % 2) * 4
            if t == 0:
                wt(nc.tensor, tok_af)
            elif t >= 2:
                wt(nc.tensor, wo_ev[t - 2])
            for nn_ in range((D + 511) // 512):
                ww = min(512, D - nn_ * 512)
                i = nc.tensor.matmul(out=PS[:, b0 * 512 + nn_ * 512: b0 * 512 + nn_ * 512 + ww],
                                     lhsT=AFB[:, t * 128:(t + 1) * 128],
                                     rhs=WOE[:, nn_ * 512:nn_ * 512 + ww],
                                     start=True, stop=False)
                i = nc.tensor.matmul(out=PS[:, b0 * 512 + nn_ * 512: b0 * 512 + nn_ * 512 + ww],
                                     lhsT=ONE[0:1, :], rhs=BOR[0:1, nn_ * 512:nn_ * 512 + ww],
                                     start=False, stop=True)
            wo_mm[t] = inc(i, sP)
            wt(nc.vector, wo_mm[t])
            i = VE.tensor_add(xt(t), xt(t), PS[:, b0 * 512:b0 * 512 + D])
            wo_ev[t] = i
        tok_x2 = now(sV)

        SQ3 = ARAf[:, 0:D]
        for t in range(NTILE):
            wt(nc.scalar, wo_ev[t])
            ta = SE.activation(out=SQ3, in_=xt(t), func=AF.Square,
                                          accum_out=SSTA[:, NT + t:NT + t + 1])
            wt(nc.vector, ta)
            VE.reduce_sum(SSTA[:, t:t + 1], xt(t), axis=AX.X)
        tok_ln3 = ln_finalize(SSTA, now(sV))

        wt(nc.sync, wo_mm[NTILE - 1])
        pipe['wgu_free'] = wo_mm[NTILE - 1]
        pipe['wst_free'] = wo_mm[NTILE - 1]
        emit_ffn(SSTA, tok_ln3, True, (sV, tok_x2[1]), (sV, tok_x2[1]))

        for s_ in sGs:
            nc.gpsimd.wait_ge(s_, cnt[id(s_)])
        nc.gpsimd.wait_ge(sG, cnt[id(sG)])

    return nc


def shard_inputs(cfg, L0_attn_q, L0_attn_Wk, L0_attn_bk, L0_attn_Wv, L0_attn_bv,
                 L0_attn_Wo, L0_attn_bo, L0_ffn_w_gate, L0_ffn_w_up, L0_ffn_w_down):
    D = cfg["D"]; I = cfg["I"]; TLOC = cfg["TLOC"]; n_cores = cfg["n_cores"]
    KCH = D // 128; GC = I // 512

    x = np.ascontiguousarray(np.asarray(L0_attn_q, dtype=np.float32).reshape(-1, D))

    def layout_gu(w):
        w = np.asarray(w, dtype=np.float32)
        w5 = w.reshape(KCH, 128, GC, 128, 4)
        return np.ascontiguousarray(w5.transpose(4, 2, 1, 0, 3).reshape(4, GC, 128, KCH * 128))

    def layout_wd(w):
        w = np.asarray(w, dtype=np.float32)
        w4 = w.reshape(GC, 128, 4, D)
        return np.ascontiguousarray(w4.transpose(0, 2, 1, 3))

    def layout_wkv(w):
        w = np.asarray(w, dtype=np.float32)
        return np.ascontiguousarray(
            w.reshape(KCH, 128, 128).transpose(1, 0, 2).reshape(128, KCH * 128))

    base = {
        "wg": layout_gu(L0_ffn_w_gate),
        "wu": layout_gu(L0_ffn_w_up),
        "wd": layout_wd(L0_ffn_w_down),
        "wk": layout_wkv(L0_attn_Wk),
        "wv": layout_wkv(L0_attn_Wv),
        "wo": np.ascontiguousarray(np.asarray(L0_attn_Wo, dtype=np.float32).reshape(KCH, 128, D)),
        "bk": np.ascontiguousarray(np.asarray(L0_attn_bk, dtype=np.float32).reshape(128, 1)),
        "bv": np.ascontiguousarray(np.asarray(L0_attn_bv, dtype=np.float32).reshape(1, 128)),
        "bo": np.ascontiguousarray(np.asarray(L0_attn_bo, dtype=np.float32).reshape(1, D)),
        "ident": np.eye(128, dtype=BF),
    }
    in_maps = []
    for c in range(n_cores):
        m = dict(base)
        m["x"] = np.ascontiguousarray(x[c * TLOC:(c + 1) * TLOC])
        in_maps.append(m)
    return in_maps


def kernel(L0_attn_q, L0_attn_Wq, L0_attn_bq, L0_attn_Wk, L0_attn_bk,
           L0_attn_Wv, L0_attn_bv, L0_attn_Wo, L0_attn_bo,
           L0_ffn_w_gate, L0_ffn_w_up, L0_ffn_w_down):
    cfg = full_cfg()
    in_maps = shard_inputs(cfg, L0_attn_q, L0_attn_Wk, L0_attn_bk, L0_attn_Wv,
                           L0_attn_bv, L0_attn_Wo, L0_attn_bo,
                           L0_ffn_w_gate, L0_ffn_w_up, L0_ffn_w_down)
    nc = build_nc(cfg)
    res = run_bass_kernel_spmd(nc, in_maps, core_ids=list(range(cfg["n_cores"])))
    out = np.concatenate([res.results[c]["out"] for c in range(cfg["n_cores"])], axis=0)
    return out.reshape(2, 8192, cfg["D"]).astype(np.float32)


# version 03
# speedup vs baseline: 1.1243x; 1.1243x over previous
import numpy as np
import ml_dtypes
from contextlib import ExitStack
import concourse.bass as bass
import concourse.mybir as mybir
from concourse.bass_utils import run_bass_kernel_spmd

F32 = mybir.dt.float32
BF16 = mybir.dt.bfloat16
AF = mybir.ActivationFunctionType
OP = mybir.AluOpType
AX = mybir.AxisListType
BF = ml_dtypes.bfloat16


def full_cfg():
    return dict(D=2048, I=8192, TLOC=2048, TC=512, NRANK=4, n_cores=8, eps=1e-5)


def build_nc(cfg):
    D = cfg["D"]; I = cfg["I"]; TLOC = cfg["TLOC"]; TC = cfg["TC"]
    NRANK = cfg["NRANK"]; n_cores = cfg["n_cores"]; eps = cfg["eps"]
    KCH = D // 128
    ICH = I // 128
    GC = I // 512
    NTILE = TLOC // 128
    NTC = TLOC // TC
    TPT = TC // 128
    SFULL = NRANK * TLOC
    NKB = SFULL // 512
    VCH = SFULL // 128
    FH = max(1, (TPT * D * 4) // 16384)
    FHW = D // FH
    NT = NTILE
    ARAW = max(ICH * TC, 4 * SFULL, NTILE * D, 8 * D)

    nc = bass.Bass(trn_type="TRN2", target_bir_lowering=False)

    x_d = nc.dram_tensor("x", [TLOC, D], F32, kind="ExternalInput").ap()
    wg_d = nc.dram_tensor("wg", [4, GC, 128, KCH * 128], F32, kind="ExternalInput").ap()
    wu_d = nc.dram_tensor("wu", [4, GC, 128, KCH * 128], F32, kind="ExternalInput").ap()
    wd_d = nc.dram_tensor("wd", [GC, 4, 128, D], F32, kind="ExternalInput").ap()
    wk_d = nc.dram_tensor("wk", [128, KCH * 128], F32, kind="ExternalInput").ap()
    wv_d = nc.dram_tensor("wv", [128, KCH * 128], F32, kind="ExternalInput").ap()
    wo_d = nc.dram_tensor("wo", [KCH, 128, D], F32, kind="ExternalInput").ap()
    bk_d = nc.dram_tensor("bk", [128, 1], F32, kind="ExternalInput").ap()
    bv_d = nc.dram_tensor("bv", [1, 128], F32, kind="ExternalInput").ap()
    bo_d = nc.dram_tensor("bo", [1, D], F32, kind="ExternalInput").ap()
    id_d = nc.dram_tensor("ident", [128, 128], BF16, kind="ExternalInput").ap()
    out_d = nc.dram_tensor("out", [TLOC, D], F32, kind="ExternalOutput").ap()
    kvs_d = nc.dram_tensor("kv_send", [2, 128, TLOC], BF16).ap()
    kvr_d = nc.dram_tensor("kv_recv", [NRANK, 2, 128, TLOC], BF16).ap()
    wgc_d = nc.dram_tensor("wg_c", [4, GC, 128, KCH * 128], BF16).ap()
    wuc_d = nc.dram_tensor("wu_c", [4, GC, 128, KCH * 128], BF16).ap()
    wdc_d = nc.dram_tensor("wd_c", [GC, 4, 128, D], BF16).ap()

    replica_groups = [[g * NRANK + i for i in range(NRANK)]
                      for g in range(n_cores // NRANK)]

    st = ExitStack()
    with st:
        X = st.enter_context(nc.sbuf_tensor("Xs", [128, NTILE * D], BF16)).ap()
        ARA = st.enter_context(nc.sbuf_tensor("ARA", [128, ARAW], BF16)).ap()
        LNF = st.enter_context(nc.sbuf_tensor("LNF", [128, KCH * TC], BF16)).ap()
        WGU = st.enter_context(nc.sbuf_tensor("WGU", [128, 4 * KCH * 128], BF16)).ap()
        WST = st.enter_context(nc.sbuf_tensor("WST", [128, 2 * KCH * 128], F32)).ap()
        WDB = st.enter_context(nc.sbuf_tensor("WDB", [128, max(2 * FHW, 4 * TC)], BF16)).ap()
        HR = st.enter_context(nc.sbuf_tensor("HRs", [128, 4 * TC], F32)).ap()
        SCR = st.enter_context(nc.sbuf_tensor("SCR", [128, 3 * TC], F32)).ap()
        WOE = st.enter_context(nc.sbuf_tensor("WOE", [128, D], BF16)).ap()
        SSTA = st.enter_context(nc.sbuf_tensor("SSTA", [128, 4 * NT], F32)).ap()
        SSTB = st.enter_context(nc.sbuf_tensor("SSTB", [128, 4 * NT], F32)).ap()
        MXB = st.enter_context(nc.sbuf_tensor("MXB", [128, NKB + 4], F32)).ap()
        IDN = st.enter_context(nc.sbuf_tensor("IDN", [128, 128], BF16)).ap()
        ONE = st.enter_context(nc.sbuf_tensor("ONEs", [1, 128], BF16)).ap()
        BKB = st.enter_context(nc.sbuf_tensor("BKB", [128, 1], F32)).ap()
        BVR = st.enter_context(nc.sbuf_tensor("BVR", [1, 128], BF16)).ap()
        BOR = st.enter_context(nc.sbuf_tensor("BOR", [1, D], BF16)).ap()
        PS = st.enter_context(nc.psum_tensor("PS", [128, 4096], F32)).ap()

        sA = st.enter_context(nc.semaphore("sA"))
        sV = st.enter_context(nc.semaphore("sV"))
        sP = st.enter_context(nc.semaphore("sP"))
        sC = st.enter_context(nc.semaphore("sC"))
        sG = st.enter_context(nc.semaphore("sG"))
        sWs = [st.enter_context(nc.semaphore(f"sW{k}")) for k in range(4)]
        sGs = [st.enter_context(nc.semaphore(f"sG{k}")) for k in range(2)]
        sWBg = st.enter_context(nc.semaphore("sWBg"))
        sWBd = st.enter_context(nc.semaphore("sWBd"))
        cnt = {}
        for s_ in [sA, sV, sP, sC, sG, sWBg, sWBd] + sWs + sGs:
            cnt[id(s_)] = 0

        def inc(instr, sem, amt=1):
            instr.then_inc(sem, amt)
            cnt[id(sem)] += amt
            return (sem, cnt[id(sem)])

        def now(sem):
            return (sem, cnt[id(sem)])

        def wt(eng, tok):
            if isinstance(eng, Eng):
                eng = eng.eng
            if tok and tok[1] > 0:
                eng.wait_ge(tok[0], tok[1])

        def dma(eng, out, in_, sem):
            if cnt[id(sem)] > 0:
                eng.wait_ge(sem, cnt[id(sem)])
            return inc(eng.dma_start(out=out, in_=in_), sem, 16)

        class Eng:
            def __init__(self, eng, sem):
                self.eng = eng
                self.sem = sem

            def __getattr__(self, name):
                if name in ("wait_ge",):
                    return getattr(self.eng, name)
                fn = getattr(self.eng, name)

                def call(*a, **k):
                    if cnt[id(self.sem)] > 0:
                        self.eng.wait_ge(self.sem, cnt[id(self.sem)])
                    return inc(fn(*a, **k), self.sem)
                return call

        VE = Eng(nc.vector, sV)
        SE = Eng(nc.scalar, sA)

        ARAf = ARA.bitcast(F32)
        WSTb = WST.bitcast(BF16)
        HRb = HR.bitcast(BF16)
        SCRb = SCR.bitcast(BF16)
        PSb = PS.bitcast(BF16)

        def xt(t):
            return X[:, t * D:(t + 1) * D]

        def hsp(ci):
            return ARA[:, ci * TC:(ci + 1) * TC]

        def bank(b):
            return PS[:, b * 512:(b + 1) * 512]

        def bankb(b):
            return PSb[:, b * 1024:(b + 1) * 1024]

        if KCH * TC >= SFULL and 4 * KCH * 128 >= SFULL:
            KFB = LNF[:, 0:SFULL]
            VKB = WGU[:, 0:SFULL]
        else:
            KFB = st.enter_context(nc.sbuf_tensor("KFBt", [128, SFULL], BF16)).ap()
            VKB = st.enter_context(nc.sbuf_tensor("VKBt", [128, SFULL], BF16)).ap()
        QF = WSTb[:, 0:TLOC]
        AFB = WSTb[:, TLOC:2 * TLOC]
        OUTW = [ARAf[:, (ARAW // 2) - 2 * D + s * D: (ARAW // 2) - 2 * D + (s + 1) * D]
                for s in range(2)]
        out_guard_ci = (ARAW - 4 * D) // TC

        dma(nc.sync, IDN[:, :], id_d[:, :], sWs[0])
        dma(nc.sync, BKB[:, :], bk_d[:, :], sWs[1])
        dma(nc.sync, WST[0:1, 0:128], bv_d[:, :], sWs[2])
        dma(nc.sync, WST[0:1, 128:128 + D], bo_d[:, :], sWs[3])
        tok_const = now(sWs[3])

        for s_ in sWs:
            VE.wait_ge(s_, cnt[id(s_)])
        VE.memset(ONE[:, :], 1.0)
        VE.tensor_copy(out=BVR[:, :], in_=WST[0:1, 0:128])
        tok_smallcast = VE.tensor_copy(out=BOR[:, :], in_=WST[0:1, 128:128 + D])

        WOACC = ARAf[:, 0:D]
        wo_dma = []
        wo_add = []
        wt(nc.sync, tok_smallcast)
        for c in range(KCH):
            s = c % 2
            if c >= 2:
                wt(nc.sync, wo_add[c - 2])
            wo_dma.append(dma(nc.sync, WST[:, s * D:(s + 1) * D], wo_d[c], sWs[s]))
            wt(nc.vector, wo_dma[c])
            if c == 0:
                i = VE.tensor_copy(out=WOACC, in_=WST[:, 0:D])
            else:
                i = VE.tensor_add(WOACC, WOACC, WST[:, s * D:(s + 1) * D])
            wo_add.append(i)
        tok_woe = VE.tensor_copy(out=WOE[:, :], in_=WOACC)

        def xstg(s):
            return ARAf[:, (1 + s) * D:(2 + s) * D]

        SQS = ARAf[:, 3 * D:4 * D]
        xcast = []
        for t in range(NTILE):
            s = t % 2
            if t >= 2:
                wt(nc.gpsimd, xcast[2 * (t - 2)])
            tok = dma(nc.gpsimd, xstg(s), x_d[t * 128:(t + 1) * 128, :], sGs[s])
            wt(nc.scalar, tok)
            if t == 0:
                wt(nc.scalar, tok_woe)
            i = SE.activation(out=xt(t), in_=xstg(s), func=AF.Copy,
                                     accum_out=SSTA[:, t:t + 1])
            xcast.append(i)
            i = SE.activation(out=SQS, in_=xt(t), func=AF.Square,
                                     accum_out=SSTA[:, NT + t:NT + t + 1])
            xcast.append(i)
        tok_x_loaded = now(sA)

        def ln_finalize(SST, wait_tok):
            mean = SST[:, 0:NT]; ssq = SST[:, NT:2 * NT]
            rr = SST[:, 2 * NT:3 * NT]; nmr = SST[:, 3 * NT:4 * NT]
            wt(nc.vector, wait_tok)
            VE.tensor_scalar_mul(mean, mean, 1.0 / D)
            VE.tensor_scalar_mul(ssq, ssq, 1.0 / D)
            VE.tensor_mul(rr, mean, mean)
            VE.tensor_sub(ssq, ssq, rr)
            VE.tensor_scalar_add(ssq, ssq, eps)
            tv = VE.reciprocal(rr, ssq)
            wt(nc.scalar, tv)
            ta = SE.activation(out=rr, in_=rr, func=AF.Sqrt)
            wt(nc.vector, ta)
            VE.tensor_mul(nmr, mean, rr)
            return VE.tensor_scalar_mul(nmr, nmr, -1.0)

        tok_ln1 = ln_finalize(SSTA, tok_x_loaded)

        pipe = dict(out_tok=None)

        def emit_ffn(SST, stats_tok, write_out, hr_free_tok, ara_free_tok,
                     wst_free_tok=None, fill_first=False):
            LNF3 = LNF.rearrange("p (k t) -> p k t", k=KCH)
            spars_last = hr_free_tok
            ara_tok = ara_free_tok
            ev2_last = None
            if wst_free_tok:
                pipe['wst_free'] = wst_free_tok
            for tc in range(NTC):
                fill = fill_first and tc == 0
                tiles = [tc * TPT + j for j in range(TPT)]
                lts = []
                tp = {}
                ev = {}
                fills = [(j, k0) for j in range(TPT) for k0 in range(0, KCH, 4)]
                for fi, (j, k0) in enumerate(fills):
                    t = tiles[j]
                    s = j % 2
                    kw = min(4, KCH - k0)
                    b = fi % 2
                    if k0 == 0:
                        if j == 0:
                            wt(nc.vector, stats_tok)
                            wt(nc.vector, spars_last)
                        if j >= 2:
                            wt(nc.vector, tp[(j - 2, 'last')])
                        i = VE.tensor_scalar(
                            out=HRb[:, s * D:(s + 1) * D], in0=xt(t),
                            scalar1=SST[:, 2 * NT + t:2 * NT + t + 1],
                            scalar2=SST[:, 3 * NT + t:3 * NT + t + 1],
                            op0=OP.mult, op1=OP.add)
                        lts.append(i)
                    wt(nc.tensor, lts[j])
                    if fi >= 2:
                        wt(nc.tensor, ev[fi - 2])
                    elif ev2_last:
                        wt(nc.tensor, ev2_last)
                    for kk in range(kw):
                        i = nc.tensor.transpose(
                            out=bankb(b)[:, kk * 128:(kk + 1) * 128],
                            in_=HRb[:, s * D + (k0 + kk) * 128: s * D + (k0 + kk + 1) * 128],
                            identity=IDN[:, :])
                    tok = inc(i, sP)
                    tp[fi] = tok
                    tp[(j, 'last')] = tok
                    wt(nc.vector, tok)
                    src = bankb(b)[:, 0:kw * 128].rearrange("p (k t) -> p k t", k=kw)
                    i = VE.tensor_copy(
                        out=LNF3[:, k0:k0 + kw, j * 128:(j + 1) * 128], in_=src)
                    ev[fi] = i
                tok_lnf = now(sV)

                dma_tok = {}
                cast_tok = {}
                silu_tok = {}
                mult_tok = {}
                g_tok = {}
                u_tok = {}
                spars_tok = {}
                HW = KCH * 128 // 2
                gud = {}
                wbgu = pipe.setdefault('wbgu', {})
                for si in range(GC * 4):
                    gc, r = si // 4, si % 4
                    p = si % 2
                    if fill:
                        for it in range(4):
                            item = si * 4 + it
                            slot = item % 4
                            gu, hh = it // 2, it % 2
                            if item >= 4:
                                wt(nc.sync, cast_tok[item - 4])
                            else:
                                wt(nc.sync, pipe.get('wst_free'))
                            src = (wg_d if gu == 0 else wu_d)[r, gc]
                            dma_tok[item] = dma(nc.sync, WST[:, slot * HW:(slot + 1) * HW],
                                                src[:, hh * HW:(hh + 1) * HW], sWs[slot])
                        for it in range(4):
                            item = si * 4 + it
                            slot = item % 4
                            gu, hh = it // 2, it % 2
                            wt(nc.scalar, dma_tok[item])
                            if si >= 2 and it == 0:
                                wt(nc.scalar, u_tok[si - 2])
                            elif si < 2 and it == 0:
                                wt(nc.scalar, pipe.get('wgu_free'))
                            if hh == 0:
                                wt(nc.scalar, wbgu.get((si - 2, gu)))
                            dst = WGU[:, (gu * 2 + p) * KCH * 128:(gu * 2 + p + 1) * KCH * 128]
                            i = SE.activation(out=dst[:, hh * HW:(hh + 1) * HW],
                                                     in_=WST[:, slot * HW:(slot + 1) * HW],
                                                     func=AF.Copy)
                            cast_tok[item] = i
                            if hh == 1:
                                wt(nc.gpsimd, i)
                                wbgu[(si, gu)] = dma(
                                    nc.gpsimd, (wgc_d if gu == 0 else wuc_d)[r, gc],
                                    dst, sWBg)
                    else:
                        for gu in range(2):
                            slot = gu * 2 + p
                            if si == 0 and gu == 0:
                                wt(nc.sync, pipe.get('wbgu_done'))
                            if gu == 0:
                                if si >= 2:
                                    wt(nc.sync, u_tok[si - 2])
                                else:
                                    wt(nc.sync, pipe.get('wgu_free'))
                            dst = WGU[:, slot * KCH * 128:(slot + 1) * KCH * 128]
                            gud[(si, gu)] = dma(nc.sync, dst,
                                                (wgc_d if gu == 0 else wuc_d)[r, gc],
                                                sWs[slot])
                    if si == 0:
                        wt(nc.tensor, tok_lnf)
                    if fill:
                        wt(nc.tensor, cast_tok[si * 4 + 3])
                    else:
                        wt(nc.tensor, gud[(si, 0)])
                        wt(nc.tensor, gud[(si, 1)])
                    if si >= 2:
                        wt(nc.tensor, silu_tok[si - 2])
                        wt(nc.tensor, mult_tok[si - 2])
                    gw = WGU[:, p * KCH * 128:(p + 1) * KCH * 128]
                    uw = WGU[:, (2 + p) * KCH * 128:(2 + p + 1) * KCH * 128]
                    for k in range(KCH):
                        i = nc.tensor.matmul(out=bank(p)[:, 0:TC],
                                             lhsT=gw[:, k * 128:(k + 1) * 128],
                                             rhs=LNF[:, k * TC:(k + 1) * TC],
                                             start=(k == 0), stop=(k == KCH - 1))
                    g_tok[si] = inc(i, sP)
                    for k in range(KCH):
                        i = nc.tensor.matmul(out=bank(2 + p)[:, 0:TC],
                                             lhsT=uw[:, k * 128:(k + 1) * 128],
                                             rhs=LNF[:, k * TC:(k + 1) * TC],
                                             start=(k == 0), stop=(k == KCH - 1))
                    u_tok[si] = inc(i, sP)
                    wt(nc.scalar, g_tok[si])
                    if si >= 4:
                        wt(nc.scalar, spars_tok[si // 4 - 1])
                    i = SE.activation(out=HR[:, r * TC:(r + 1) * TC],
                                             in_=bank(p)[:, 0:TC], func=AF.Silu)
                    silu_tok[si] = i
                    wt(nc.vector, silu_tok[si])
                    wt(nc.vector, u_tok[si])
                    i = VE.tensor_mul(HR[:, r * TC:(r + 1) * TC],
                                             HR[:, r * TC:(r + 1) * TC],
                                             bank(2 + p)[:, 0:TC])
                    mult_tok[si] = i
                    if r == 3:
                        WDF = WDB.bitcast(F32)
                        wt(nc.scalar, mult_tok[si])
                        if si >= 4:
                            wt(nc.scalar, spars_tok[si // 4 - 1])
                        elif ev2_last:
                            wt(nc.scalar, ev2_last)
                        T = [PS[:, 4 * 512:4 * 512 + TC],
                             WDF[:, 0:TC],
                             PS[:, 5 * 512:5 * 512 + TC],
                             WDF[:, TC:2 * TC]]
                        for rr in range(4):
                            i = SE.activation(out=T[rr],
                                              in_=HR[:, rr * TC:(rr + 1) * TC],
                                              func=AF.Square)
                        sq_tok = i
                        wt(nc.vector, sq_tok)
                        if tc == 0 and gc == 0:
                            wt(nc.vector, ara_tok)
                        Hh = [HR[:, rr * TC:(rr + 1) * TC] for rr in range(4)]
                        S0 = SCR[:, 0:TC]; S1 = SCR[:, TC:2 * TC]; S2 = SCR[:, 2 * TC:3 * TC]
                        VE.tensor_tensor(S0, T[0], T[1], OP.max)
                        VE.tensor_tensor(S1, T[2], T[3], OP.max)
                        VE.tensor_tensor(S2, T[0], T[1], OP.min)
                        VE.tensor_tensor(S0, S0, S1, OP.min)
                        VE.tensor_tensor(S1, T[2], T[3], OP.min)
                        VE.tensor_tensor(S1, S2, S1, OP.max)
                        VE.tensor_tensor(S0, S0, S1, OP.max)
                        if write_out and (gc + 1) * 4 > out_guard_ci:
                            wt(nc.vector, pipe['out_tok'])
                            wt(nc.vector, pipe.get('out_tok2'))
                        for rr in range(4):
                            ci = gc * 4 + rr
                            VE.tensor_tensor(T[rr], T[rr], S0, OP.is_ge)
                            i = VE.tensor_mul(hsp(ci), Hh[rr], T[rr])
                        spars_tok[gc] = i
                spars_last = spars_tok[GC - 1]
                if fill:
                    pipe['wst_free'] = cast_tok[GC * 16 - 1]
                pipe['wgu_free'] = u_tok[GC * 4 - 1]

                dd = {}
                dc = {}
                pd = {}
                wbwd = {}
                DSL = min(4, max(2, (2 * KCH * 128) // FHW))
                for fh in range(FH):
                    for ci in range(ICH):
                        item = fh * ICH + ci
                        gc, r = ci // 4, ci % 4
                        if fill:
                            slot = item % DSL
                            if item >= DSL:
                                wt(nc.sync, dc[item - DSL])
                            else:
                                wt(nc.sync, pipe.get('wst_free'))
                            dd[item] = dma(nc.sync, WST[:, slot * FHW:(slot + 1) * FHW],
                                           wd_d[gc, r][:, fh * FHW:(fh + 1) * FHW], sWs[slot])
                            wt(nc.scalar, dd[item])
                            if item >= 2:
                                wt(nc.scalar, pd.get(item - 2))
                                wt(nc.scalar, wbwd[item - 2])
                            else:
                                wt(nc.scalar, pipe.get('wdb_free'))
                                wt(nc.scalar, spars_last)
                            wsl = WDB[:, (item % 2) * FHW:(item % 2 + 1) * FHW]
                            i = SE.activation(out=wsl,
                                                     in_=WST[:, slot * FHW:(slot + 1) * FHW],
                                                     func=AF.Copy)
                            dc[item] = i
                            wt(nc.gpsimd, i)
                            wbwd[item] = dma(nc.gpsimd,
                                             wdc_d[gc, r][:, fh * FHW:(fh + 1) * FHW],
                                             wsl, sWBd)
                            wt(nc.tensor, dc[item])
                        else:
                            if item == 0:
                                wt(nc.sync, pipe.get('wbwd_done'))
                                wt(nc.sync, spars_last)
                                wt(nc.sync, pipe.get('wdb_free'))
                            elif item >= 2:
                                wt(nc.sync, pd.get(item - 2))
                            dd[item] = dma(nc.sync,
                                           WDB[:, (item % 2) * FHW:(item % 2 + 1) * FHW],
                                           wdc_d[gc, r][:, fh * FHW:(fh + 1) * FHW],
                                           sWs[item % 2])
                            wt(nc.tensor, dd[item])
                        if ci == 0:
                            wt(nc.tensor, spars_last)
                            if fh > 0:
                                wt(nc.tensor, ev2_last)
                        wdb = WDB[:, (item % 2) * FHW:(item % 2 + 1) * FHW]
                        nsl = min(512, FHW)
                        GRP = FHW // nsl
                        for m in range(TPT):
                            for nn_ in range(GRP):
                                bb = (m * GRP + nn_) * 512
                                i = nc.tensor.matmul(
                                    out=PS[:, bb: bb + nsl],
                                    lhsT=hsp(ci)[:, m * 128:(m + 1) * 128],
                                    rhs=wdb[:, nn_ * nsl:(nn_ + 1) * nsl],
                                    start=(ci == 0), stop=(ci == ICH - 1))
                        pd[item] = inc(i, sP)
                    wt(nc.vector, pd[fh * ICH + ICH - 1])
                    nsl = min(512, FHW)
                    GRP = FHW // nsl
                    for m, t in enumerate(tiles):
                        for nn_ in range(GRP):
                            bb = (m * GRP + nn_) * 512
                            i = VE.scalar_tensor_tensor(
                                out=xt(t)[:, fh * FHW + nn_ * nsl: fh * FHW + (nn_ + 1) * nsl],
                                in0=PS[:, bb: bb + nsl],
                                scalar=0.5,
                                in1=xt(t)[:, fh * FHW + nn_ * nsl: fh * FHW + (nn_ + 1) * nsl],
                                op0=OP.mult, op1=OP.add)
                    ev2_last = i
                if fill:
                    pipe['wst_free'] = dc[FH * ICH - 1]
                    pipe['wbgu_done'] = now(sWBg)
                    pipe['wbwd_done'] = now(sWBd)
                pipe['wdb_free'] = pd[FH * ICH - 1]

                if write_out:
                    oc = {}
                    oo = pipe.setdefault('oo_slot', {})
                    for m, t in enumerate(tiles):
                        s = m % 2
                        wt(nc.scalar, ev2_last)
                        wt(nc.scalar, oo.get(s))
                        i = SE.activation(out=OUTW[s], in_=xt(t), func=AF.Copy)
                        oc[m] = i
                        wt(nc.gpsimd, oc[m])
                        oo[s] = dma(nc.gpsimd, out_d[t * 128:(t + 1) * 128, :],
                                    OUTW[s], sGs[s])
                    pipe['out_tok'] = oo.get(0)
                    pipe['out_tok2'] = oo.get(1)
            return spars_last, ev2_last

        spars_last, ev2_last = emit_ffn(SSTA, tok_ln1, False, None, tok_x_loaded,
                                        wst_free_tok=tok_woe, fill_first=True)

        wt(nc.scalar, ev2_last)
        SQ2 = ARAf[:, 0:D]
        for t in range(NTILE):
            ta = SE.activation(out=SQ2, in_=xt(t), func=AF.Square,
                                          accum_out=SSTB[:, NT + t:NT + t + 1])
            wt(nc.vector, ta)
            VE.reduce_sum(SSTB[:, t:t + 1], xt(t), axis=AX.X)
        tok_ln2 = ln_finalize(SSTB, now(sV))

        groups = []
        remaining = KCH
        k0 = 0
        for buf, width in [(LNF, KCH * TC), (WGU, 4 * KCH * 128),
                           (WSTb, 4 * KCH * 128), (WDB, 2 * FHW),
                           (SCRb, 6 * TC), (HRb, 8 * TC)]:
            nch = min(width // TLOC, remaining)
            if nch > 0:
                groups.append((buf, k0, nch))
                k0 += nch
                remaining -= nch
            if remaining == 0:
                break
        assert remaining == 0, (remaining, KCH)
        ln2f_views = []
        for buf, gk0, nch in groups:
            for q in range(nch):
                ln2f_views.append(buf[:, q * TLOC:(q + 1) * TLOC])

        l2t = []
        for t in range(NTILE):
            if t == 0:
                wt(nc.vector, tok_ln2)
            i = VE.tensor_scalar(
                out=ARA[:, t * D:(t + 1) * D], in0=xt(t),
                scalar1=SSTB[:, 2 * NT + t:2 * NT + t + 1],
                scalar2=SSTB[:, 3 * NT + t:3 * NT + t + 1],
                op0=OP.mult, op1=OP.add)
            l2t.append(i)

        ev3 = {}
        tile_done_tok = {}
        fi = 0
        for t in range(NTILE):
            for k0 in range(0, KCH, 4):
                kw = min(4, KCH - k0)
                b = fi % 2
                wt(nc.tensor, l2t[t])
                if fi >= 2:
                    wt(nc.tensor, ev3[fi - 2])
                elif ev2_last:
                    wt(nc.tensor, ev2_last)
                for kk in range(kw):
                    i = nc.tensor.transpose(
                        out=bankb(b)[:, kk * 128:(kk + 1) * 128],
                        in_=ARA[:, t * D + (k0 + kk) * 128: t * D + (k0 + kk + 1) * 128],
                        identity=IDN[:, :])
                tok = inc(i, sP)
                wt(nc.vector, tok)
                for kk in range(kw):
                    i = VE.tensor_copy(
                        out=ln2f_views[k0 + kk][:, t * 128:(t + 1) * 128],
                        in_=bankb(b)[:, kk * 128:(kk + 1) * 128])
                ev3[fi] = i
                fi += 1
            tile_done_tok[t] = now(sV)
        tok_ln2f = now(sV)

        WKc = ARA[:, 0:KCH * 128]
        WVc = ARA[:, KCH * 128:2 * KCH * 128]
        WKVST = ARAf[:, KCH * 128:2 * KCH * 128]
        need_tiles = (4 * KCH * 128 + D - 1) // D
        wt(nc.sync, tile_done_tok[min(need_tiles - 1, NTILE - 1)])
        tok = dma(nc.sync, WKVST, wk_d[:, :], sWs[0])
        wt(nc.scalar, tok)
        tok_wk = SE.activation(out=WKc, in_=WKVST, func=AF.Copy)
        wt(nc.sync, tok_wk)
        tok = dma(nc.sync, WKVST, wv_d[:, :], sWs[0])
        wt(nc.scalar, tok)
        tok_wv = SE.activation(out=WVc, in_=WKVST, func=AF.Copy)

        KLOC = ARA[:, 2 * KCH * 128:2 * KCH * 128 + TLOC]
        VLOC = ARA[:, 2 * KCH * 128 + TLOC:2 * KCH * 128 + 2 * TLOC]
        wt(nc.tensor, tok_wv)
        wt(nc.tensor, tok_ln2f)
        for k in range(KCH):
            for nn_ in range(TLOC // 512):
                i = nc.tensor.matmul(out=PS[:, nn_ * 512:(nn_ + 1) * 512],
                                     lhsT=WKc[:, k * 128:(k + 1) * 128],
                                     rhs=ln2f_views[k][:, nn_ * 512:(nn_ + 1) * 512],
                                     start=(k == 0), stop=(k == KCH - 1))
        tok_kmm = inc(i, sP)
        wt(nc.vector, tok_kmm)
        VE.tensor_scalar(out=KLOC, in0=PS[:, 0:TLOC],
                         scalar1=BKB[:, 0:1], scalar2=None, op0=OP.add)
        tok_kv = None
        for rnd in range((NTILE + 3) // 4):
            ms = list(range(rnd * 4, min(rnd * 4 + 4, NTILE)))
            if rnd > 0:
                wt(nc.tensor, tok_kv)
            for k in range(KCH):
                for m in ms:
                    i = nc.tensor.matmul(
                        out=PS[:, 2048 + (m % 4) * 512: 2048 + (m % 4) * 512 + 128],
                        lhsT=ln2f_views[k][:, m * 128:(m + 1) * 128],
                        rhs=WVc[:, k * 128:(k + 1) * 128],
                        start=(k == 0), stop=False)
            for m in ms:
                i = nc.tensor.matmul(
                    out=PS[:, 2048 + (m % 4) * 512: 2048 + (m % 4) * 512 + 128],
                    lhsT=ONE[0:1, :], rhs=BVR[0:1, :],
                    start=False, stop=True)
            tok_vr = inc(i, sP)
            wt(nc.vector, tok_vr)
            for m in ms:
                tok_kv = VE.tensor_copy(
                    out=VLOC[:, m * 128:(m + 1) * 128],
                    in_=PS[:, 2048 + (m % 4) * 512: 2048 + (m % 4) * 512 + 128])

        wt(nc.gpsimd, tok_kv)
        dma(nc.gpsimd, kvs_d[0], KLOC, sG)
        tok = dma(nc.gpsimd, kvs_d[1], VLOC, sG)
        wt(nc.gpsimd, tok)
        i = nc.gpsimd.collective_compute(
            "AllGather", OP.bypass, replica_groups=replica_groups,
            ins=[kvs_d.opt()], outs=[kvr_d.opt()])
        tok_coll = inc(i, sC)
        wt(nc.gpsimd, tok_coll)
        wt(nc.gpsimd, now(sP))
        for rk in range(NRANK):
            dma(nc.gpsimd, KFB[:, rk * TLOC:(rk + 1) * TLOC], kvr_d[rk, 0], sG)
            dma(nc.gpsimd, VKB[:, rk * TLOC:(rk + 1) * TLOC], kvr_d[rk, 1], sG)
        tok_kvfull = now(sG)

        qev = {}
        for m0 in range(0, NTILE, 4):
            b = (m0 // 4) % 2
            mw = min(4, NTILE - m0)
            if m0 == 0:
                wt(nc.tensor, tok_kv)
            if m0 >= 8:
                wt(nc.tensor, qev[m0 - 8])
            for mm in range(mw):
                i = nc.tensor.transpose(out=bankb(b)[:, mm * 128:(mm + 1) * 128],
                                        in_=VLOC[:, (m0 + mm) * 128:(m0 + mm + 1) * 128],
                                        identity=IDN[:, :])
            tok = inc(i, sP)
            wt(nc.vector, tok)
            i = VE.tensor_copy(out=QF[:, m0 * 128:(m0 + mw) * 128],
                                      in_=bankb(b)[:, 0:mw * 128])
            qev[m0] = i
        tok_qf = now(sV)

        SCb = ARAf[:, 0:SFULL]
        PB = ARA[:, 2 * SFULL:3 * SFULL]
        PT = ARA[:, 3 * SFULL:4 * SFULL]
        prev_cp = {}
        af_tok = {}
        for qt in range(NTILE):
            cp = {}
            for ib in range(NKB):
                b = ib % 4
                if qt == 0 and ib == 0:
                    wt(nc.tensor, tok_qf)
                    wt(nc.tensor, tok_kvfull)
                if ib >= 4:
                    wt(nc.tensor, cp[ib - 4])
                elif qt > 0:
                    wt(nc.tensor, prev_cp[NKB - 4 + ib])
                i = nc.tensor.matmul(out=bank(b),
                                     lhsT=QF[:, qt * 128:(qt + 1) * 128],
                                     rhs=KFB[:, ib * 512:(ib + 1) * 512],
                                     start=True, stop=True)
                tok = inc(i, sP)
                wt(nc.scalar, tok)
                i = SE.activation(out=SCb[:, ib * 512:(ib + 1) * 512],
                                         in_=bank(b), func=AF.Copy)
                cp[ib] = i
                wt(nc.vector, cp[ib])
                VE.reduce_max(MXB[:, ib:ib + 1],
                                     SCb[:, ib * 512:(ib + 1) * 512], axis=AX.X)
            prev_cp = cp
            VE.reduce_max(MXB[:, NKB:NKB + 1], MXB[:, 0:NKB], axis=AX.X)
            tok_mx = VE.tensor_scalar_mul(
                MXB[:, NKB:NKB + 1], MXB[:, NKB:NKB + 1], -1.0 / 128.0)
            wt(nc.scalar, tok_mx)
            i = SE.activation(out=PB, in_=SCb, func=AF.Exp,
                                     bias=MXB[:, NKB:NKB + 1], scale=1.0 / 128.0,
                                     accum_out=MXB[:, NKB + 1:NKB + 2])
            tok_exp = i
            wt(nc.vector, tok_exp)
            VE.reciprocal(MXB[:, NKB + 2:NKB + 3], MXB[:, NKB + 1:NKB + 2])
            tok_pn = VE.tensor_scalar(
                out=PB, in0=PB, scalar1=MXB[:, NKB + 2:NKB + 3],
                scalar2=None, op0=OP.mult)
            ptv = {}
            for ci0 in range(0, VCH, 4):
                b = 4 + (ci0 // 4) % 2
                if ci0 == 0:
                    wt(nc.tensor, tok_pn)
                if ci0 >= 8:
                    wt(nc.tensor, ptv[ci0 - 8])
                for cc in range(4):
                    i = nc.tensor.transpose(
                        out=bankb(b)[:, cc * 128:(cc + 1) * 128],
                        in_=PB[:, (ci0 + cc) * 128:(ci0 + cc + 1) * 128],
                        identity=IDN[:, :])
                tok = inc(i, sP)
                wt(nc.vector, tok)
                i = VE.tensor_copy(out=PT[:, ci0 * 128:(ci0 + 4) * 128],
                                          in_=bankb(b)[:, 0:512])
                ptv[ci0] = i
            tok_pt = now(sV)
            wt(nc.tensor, tok_pt)
            if qt >= 2:
                wt(nc.tensor, af_tok[qt - 2])
            bav = 6 + (qt % 2)
            for ci in range(VCH):
                i = nc.tensor.matmul(out=PS[:, bav * 512:bav * 512 + 128],
                                     lhsT=VKB[:, ci * 128:(ci + 1) * 128],
                                     rhs=PT[:, ci * 128:(ci + 1) * 128],
                                     start=(ci == 0), stop=(ci == VCH - 1))
            tok_av = inc(i, sP)
            wt(nc.vector, tok_av)
            i = VE.tensor_copy(out=AFB[:, qt * 128:(qt + 1) * 128],
                                      in_=PS[:, bav * 512:bav * 512 + 128])
            af_tok[qt] = i
        tok_af = now(sV)

        wo_mm = {}
        wo_ev = {}
        for t in range(NTILE):
            b0 = (t % 2) * 4
            if t == 0:
                wt(nc.tensor, tok_af)
            elif t >= 2:
                wt(nc.tensor, wo_ev[t - 2])
            for nn_ in range((D + 511) // 512):
                ww = min(512, D - nn_ * 512)
                i = nc.tensor.matmul(out=PS[:, b0 * 512 + nn_ * 512: b0 * 512 + nn_ * 512 + ww],
                                     lhsT=AFB[:, t * 128:(t + 1) * 128],
                                     rhs=WOE[:, nn_ * 512:nn_ * 512 + ww],
                                     start=True, stop=False)
                i = nc.tensor.matmul(out=PS[:, b0 * 512 + nn_ * 512: b0 * 512 + nn_ * 512 + ww],
                                     lhsT=ONE[0:1, :], rhs=BOR[0:1, nn_ * 512:nn_ * 512 + ww],
                                     start=False, stop=True)
            wo_mm[t] = inc(i, sP)
            wt(nc.vector, wo_mm[t])
            i = VE.tensor_add(xt(t), xt(t), PS[:, b0 * 512:b0 * 512 + D])
            wo_ev[t] = i
        tok_x2 = now(sV)

        SQ3 = ARAf[:, 0:D]
        for t in range(NTILE):
            wt(nc.scalar, wo_ev[t])
            ta = SE.activation(out=SQ3, in_=xt(t), func=AF.Square,
                                          accum_out=SSTA[:, NT + t:NT + t + 1])
            wt(nc.vector, ta)
            VE.reduce_sum(SSTA[:, t:t + 1], xt(t), axis=AX.X)
        tok_ln3 = ln_finalize(SSTA, now(sV))

        wt(nc.sync, wo_mm[NTILE - 1])
        pipe['wgu_free'] = wo_mm[NTILE - 1]
        pipe['wst_free'] = wo_mm[NTILE - 1]
        emit_ffn(SSTA, tok_ln3, True, (sV, tok_x2[1]), (sV, tok_x2[1]))

        for s_ in sGs:
            nc.gpsimd.wait_ge(s_, cnt[id(s_)])
        nc.gpsimd.wait_ge(sG, cnt[id(sG)])

    return nc


def shard_inputs(cfg, L0_attn_q, L0_attn_Wk, L0_attn_bk, L0_attn_Wv, L0_attn_bv,
                 L0_attn_Wo, L0_attn_bo, L0_ffn_w_gate, L0_ffn_w_up, L0_ffn_w_down):
    D = cfg["D"]; I = cfg["I"]; TLOC = cfg["TLOC"]; n_cores = cfg["n_cores"]
    KCH = D // 128; GC = I // 512

    x = np.ascontiguousarray(np.asarray(L0_attn_q, dtype=np.float32).reshape(-1, D))

    def layout_gu(w):
        w = np.asarray(w, dtype=np.float32)
        w5 = w.reshape(KCH, 128, GC, 128, 4)
        return np.ascontiguousarray(w5.transpose(4, 2, 1, 0, 3).reshape(4, GC, 128, KCH * 128))

    def layout_wd(w):
        w = np.asarray(w, dtype=np.float32)
        w4 = w.reshape(GC, 128, 4, D)
        return np.ascontiguousarray(w4.transpose(0, 2, 1, 3))

    def layout_wkv(w):
        w = np.asarray(w, dtype=np.float32)
        return np.ascontiguousarray(
            w.reshape(KCH, 128, 128).transpose(1, 0, 2).reshape(128, KCH * 128))

    base = {
        "wg": layout_gu(L0_ffn_w_gate),
        "wu": layout_gu(L0_ffn_w_up),
        "wd": layout_wd(L0_ffn_w_down),
        "wk": layout_wkv(L0_attn_Wk),
        "wv": layout_wkv(L0_attn_Wv),
        "wo": np.ascontiguousarray(np.asarray(L0_attn_Wo, dtype=np.float32).reshape(KCH, 128, D)),
        "bk": np.ascontiguousarray(np.asarray(L0_attn_bk, dtype=np.float32).reshape(128, 1)),
        "bv": np.ascontiguousarray(np.asarray(L0_attn_bv, dtype=np.float32).reshape(1, 128)),
        "bo": np.ascontiguousarray(np.asarray(L0_attn_bo, dtype=np.float32).reshape(1, D)),
        "ident": np.eye(128, dtype=BF),
    }
    in_maps = []
    for c in range(n_cores):
        m = dict(base)
        m["x"] = np.ascontiguousarray(x[c * TLOC:(c + 1) * TLOC])
        in_maps.append(m)
    return in_maps


def kernel(L0_attn_q, L0_attn_Wq, L0_attn_bq, L0_attn_Wk, L0_attn_bk,
           L0_attn_Wv, L0_attn_bv, L0_attn_Wo, L0_attn_bo,
           L0_ffn_w_gate, L0_ffn_w_up, L0_ffn_w_down):
    cfg = full_cfg()
    in_maps = shard_inputs(cfg, L0_attn_q, L0_attn_Wk, L0_attn_bk, L0_attn_Wv,
                           L0_attn_bv, L0_attn_Wo, L0_attn_bo,
                           L0_ffn_w_gate, L0_ffn_w_up, L0_ffn_w_down)
    nc = build_nc(cfg)
    res = run_bass_kernel_spmd(nc, in_maps, core_ids=list(range(cfg["n_cores"])))
    out = np.concatenate([res.results[c]["out"] for c in range(cfg["n_cores"])], axis=0)
    return out.reshape(2, 8192, cfg["D"]).astype(np.float32)


# version 05
# speedup vs baseline: 1.2318x; 1.0956x over previous
import numpy as np
import ml_dtypes
from contextlib import ExitStack
import concourse.bass as bass
import concourse.mybir as mybir
from concourse.bass_utils import run_bass_kernel_spmd

F32 = mybir.dt.float32
BF16 = mybir.dt.bfloat16
AF = mybir.ActivationFunctionType
OP = mybir.AluOpType
AX = mybir.AxisListType
BF = ml_dtypes.bfloat16


def full_cfg():
    return dict(D=2048, I=8192, TLOC=2048, TC=512, NRANK=4, n_cores=8, eps=1e-5)


def build_nc(cfg):
    D = cfg["D"]; I = cfg["I"]; TLOC = cfg["TLOC"]; TC = cfg["TC"]
    NRANK = cfg["NRANK"]; n_cores = cfg["n_cores"]; eps = cfg["eps"]
    KCH = D // 128
    ICH = I // 128
    GC = I // 512
    NTILE = TLOC // 128
    NTC = TLOC // TC
    TPT = TC // 128
    SFULL = NRANK * TLOC
    NKB = SFULL // 512
    VCH = SFULL // 128
    FH = max(1, (TPT * D * 4) // 16384)
    FHW = D // FH
    NT = NTILE
    ARAW = max(ICH * TC, 4 * SFULL, NTILE * D, 8 * D)

    nc = bass.Bass(trn_type="TRN2", target_bir_lowering=False)

    x_d = nc.dram_tensor("x", [TLOC, D], F32, kind="ExternalInput").ap()
    wg_d = nc.dram_tensor("wg", [4, GC, 128, KCH * 128], F32, kind="ExternalInput").ap()
    wu_d = nc.dram_tensor("wu", [4, GC, 128, KCH * 128], F32, kind="ExternalInput").ap()
    wd_d = nc.dram_tensor("wd", [GC, 4, 128, D], F32, kind="ExternalInput").ap()
    wk_d = nc.dram_tensor("wk", [128, KCH * 128], F32, kind="ExternalInput").ap()
    wv_d = nc.dram_tensor("wv", [128, KCH * 128], F32, kind="ExternalInput").ap()
    wo_d = nc.dram_tensor("wo", [KCH, 128, D], F32, kind="ExternalInput").ap()
    bk_d = nc.dram_tensor("bk", [128, 1], F32, kind="ExternalInput").ap()
    bv_d = nc.dram_tensor("bv", [1, 128], F32, kind="ExternalInput").ap()
    bo_d = nc.dram_tensor("bo", [1, D], F32, kind="ExternalInput").ap()
    id_d = nc.dram_tensor("ident", [128, 128], BF16, kind="ExternalInput").ap()
    out_d = nc.dram_tensor("out", [TLOC, D], F32, kind="ExternalOutput").ap()
    kvs_d = nc.dram_tensor("kv_send", [2, 128, TLOC], BF16).ap()
    kvrk_d = nc.dram_tensor("k_recv", [NRANK, 128, TLOC], BF16).ap()
    kvrv_d = nc.dram_tensor("v_recv", [NRANK, 128, TLOC], BF16).ap()
    wgc_d = nc.dram_tensor("wg_c", [4, GC, 128, KCH * 128], BF16).ap()
    wuc_d = nc.dram_tensor("wu_c", [4, GC, 128, KCH * 128], BF16).ap()
    wdc_d = nc.dram_tensor("wd_c", [GC, 4, 128, D], BF16).ap()

    replica_groups = [[g * NRANK + i for i in range(NRANK)]
                      for g in range(n_cores // NRANK)]

    st = ExitStack()
    with st:
        X = st.enter_context(nc.sbuf_tensor("Xs", [128, NTILE * D], BF16)).ap()
        ARA = st.enter_context(nc.sbuf_tensor("ARA", [128, ARAW], BF16)).ap()
        LNF = st.enter_context(nc.sbuf_tensor("LNF", [128, KCH * TC], BF16)).ap()
        WGU = st.enter_context(nc.sbuf_tensor("WGU", [128, 4 * KCH * 128], BF16)).ap()
        WST = st.enter_context(nc.sbuf_tensor("WST", [128, 2 * KCH * 128], F32)).ap()
        WDB = st.enter_context(nc.sbuf_tensor("WDB", [128, max(2 * FHW, 4 * TC)], BF16)).ap()
        HR = st.enter_context(nc.sbuf_tensor("HRs", [128, 4 * TC], F32)).ap()
        SCR = st.enter_context(nc.sbuf_tensor("SCR", [128, 3 * TC], F32)).ap()
        WOE = st.enter_context(nc.sbuf_tensor("WOE", [128, D], BF16)).ap()
        SSTA = st.enter_context(nc.sbuf_tensor("SSTA", [128, 4 * NT], F32)).ap()
        SSTB = st.enter_context(nc.sbuf_tensor("SSTB", [128, 4 * NT], F32)).ap()
        MXB = st.enter_context(nc.sbuf_tensor("MXB", [128, NKB + 8 + NT], F32)).ap()
        ZCB = st.enter_context(nc.sbuf_tensor("ZCB", [128, 2], BF16)).ap()
        ZROWb = st.enter_context(nc.sbuf_tensor("ZROW", [1, TLOC], BF16)).ap()
        IDN = st.enter_context(nc.sbuf_tensor("IDN", [128, 128], BF16)).ap()
        ONE = st.enter_context(nc.sbuf_tensor("ONEs", [1, 128], BF16)).ap()
        BKB = st.enter_context(nc.sbuf_tensor("BKB", [128, 1], F32)).ap()
        BVR = st.enter_context(nc.sbuf_tensor("BVR", [1, 128], BF16)).ap()
        BOR = st.enter_context(nc.sbuf_tensor("BOR", [1, D], BF16)).ap()
        PS = st.enter_context(nc.psum_tensor("PS", [128, 4096], F32)).ap()

        sA = st.enter_context(nc.semaphore("sA"))
        sV = st.enter_context(nc.semaphore("sV"))
        sP = st.enter_context(nc.semaphore("sP"))
        sC = st.enter_context(nc.semaphore("sC"))
        sG = st.enter_context(nc.semaphore("sG"))
        sWs = [st.enter_context(nc.semaphore(f"sW{k}")) for k in range(4)]
        sGs = [st.enter_context(nc.semaphore(f"sG{k}")) for k in range(2)]
        sWBg = st.enter_context(nc.semaphore("sWBg"))
        sWBd = st.enter_context(nc.semaphore("sWBd"))
        cnt = {}
        for s_ in [sA, sV, sP, sC, sG, sWBg, sWBd] + sWs + sGs:
            cnt[id(s_)] = 0

        def inc(instr, sem, amt=1):
            instr.then_inc(sem, amt)
            cnt[id(sem)] += amt
            return (sem, cnt[id(sem)])

        def now(sem):
            return (sem, cnt[id(sem)])

        def wt(eng, tok):
            if isinstance(eng, Eng):
                eng = eng.eng
            if tok and tok[1] > 0:
                eng.wait_ge(tok[0], tok[1])

        def dma(eng, out, in_, sem):
            if cnt[id(sem)] > 0:
                eng.wait_ge(sem, cnt[id(sem)])
            return inc(eng.dma_start(out=out, in_=in_), sem, 16)

        class Eng:
            def __init__(self, eng, sem):
                self.eng = eng
                self.sem = sem

            def __getattr__(self, name):
                if name in ("wait_ge",):
                    return getattr(self.eng, name)
                fn = getattr(self.eng, name)

                def call(*a, **k):
                    if cnt[id(self.sem)] > 0:
                        self.eng.wait_ge(self.sem, cnt[id(self.sem)])
                    return inc(fn(*a, **k), self.sem)
                return call

        VE = Eng(nc.vector, sV)
        SE = Eng(nc.scalar, sA)

        ARAf = ARA.bitcast(F32)
        WSTb = WST.bitcast(BF16)
        HRb = HR.bitcast(BF16)
        SCRb = SCR.bitcast(BF16)
        PSb = PS.bitcast(BF16)

        def xt(t):
            return X[:, t * D:(t + 1) * D]

        def hsp(ci):
            return ARA[:, ci * TC:(ci + 1) * TC]

        def bank(b):
            return PS[:, b * 512:(b + 1) * 512]

        def bankb(b):
            return PSb[:, b * 1024:(b + 1) * 1024]

        if KCH * TC >= SFULL and 4 * KCH * 128 >= SFULL:
            KFB = LNF[:, 0:SFULL]
            VKB = WGU[:, 0:SFULL]
        else:
            KFB = st.enter_context(nc.sbuf_tensor("KFBt", [128, SFULL], BF16)).ap()
            VKB = st.enter_context(nc.sbuf_tensor("VKBt", [128, SFULL], BF16)).ap()
        QF = WSTb[:, 0:TLOC]
        AFB = WSTb[:, TLOC:2 * TLOC]
        OUTW = [ARAf[:, (ARAW // 2) - 2 * D + s * D: (ARAW // 2) - 2 * D + (s + 1) * D]
                for s in range(2)]
        out_guard_ci = (ARAW - 4 * D) // TC

        dma(nc.sync, IDN[:, :], id_d[:, :], sWs[0])
        dma(nc.sync, BKB[:, :], bk_d[:, :], sWs[1])
        dma(nc.sync, WST[0:1, 0:128], bv_d[:, :], sWs[2])
        dma(nc.sync, WST[0:1, 128:128 + D], bo_d[:, :], sWs[3])
        tok_const = now(sWs[3])

        for s_ in sWs:
            VE.wait_ge(s_, cnt[id(s_)])
        VE.memset(ONE[:, :], 1.0)
        VE.tensor_copy(out=BVR[:, :], in_=WST[0:1, 0:128])
        tok_smallcast = VE.tensor_copy(out=BOR[:, :], in_=WST[0:1, 128:128 + D])

        WOACC = ARAf[:, 0:D]
        wo_dma = []
        wo_add = []
        wt(nc.sync, tok_smallcast)
        for c in range(KCH):
            s = c % 2
            if c >= 2:
                wt(nc.sync, wo_add[c - 2])
            wo_dma.append(dma(nc.sync, WST[:, s * D:(s + 1) * D], wo_d[c], sWs[s]))
            wt(nc.vector, wo_dma[c])
            if c == 0:
                i = VE.tensor_copy(out=WOACC, in_=WST[:, 0:D])
            else:
                i = VE.tensor_add(WOACC, WOACC, WST[:, s * D:(s + 1) * D])
            wo_add.append(i)
        tok_woe = VE.tensor_copy(out=WOE[:, :], in_=WOACC)

        def xstg(s):
            return ARAf[:, (1 + s) * D:(2 + s) * D]

        SQS = ARAf[:, 3 * D:4 * D]
        xcast = []
        for t in range(NTILE):
            s = t % 2
            if t >= 2:
                wt(nc.gpsimd, xcast[2 * (t - 2)])
            tok = dma(nc.gpsimd, xstg(s), x_d[t * 128:(t + 1) * 128, :], sGs[s])
            wt(nc.scalar, tok)
            if t == 0:
                wt(nc.scalar, tok_woe)
            i = SE.activation(out=xt(t), in_=xstg(s), func=AF.Copy,
                                     accum_out=SSTA[:, t:t + 1])
            xcast.append(i)
            i = SE.activation(out=SQS, in_=xt(t), func=AF.Square,
                                     accum_out=SSTA[:, NT + t:NT + t + 1])
            xcast.append(i)
        tok_x_loaded = now(sA)

        def ln_finalize(SST, wait_tok):
            mean = SST[:, 0:NT]; ssq = SST[:, NT:2 * NT]
            rr = SST[:, 2 * NT:3 * NT]; nmr = SST[:, 3 * NT:4 * NT]
            wt(nc.vector, wait_tok)
            VE.tensor_scalar_mul(mean, mean, 1.0 / D)
            VE.tensor_scalar_mul(ssq, ssq, 1.0 / D)
            VE.tensor_mul(rr, mean, mean)
            VE.tensor_sub(ssq, ssq, rr)
            VE.tensor_scalar_add(ssq, ssq, eps)
            tv = VE.reciprocal(rr, ssq)
            wt(nc.scalar, tv)
            ta = SE.activation(out=rr, in_=rr, func=AF.Sqrt)
            wt(nc.vector, ta)
            VE.tensor_mul(nmr, mean, rr)
            return VE.tensor_scalar_mul(nmr, nmr, -1.0)

        tok_ln1 = ln_finalize(SSTA, tok_x_loaded)

        pipe = dict(out_tok=None)

        def emit_ffn(SST, stats_tok, write_out, hr_free_tok, ara_free_tok,
                     wst_free_tok=None, fill_first=False):
            LNF3 = LNF.rearrange("p (k t) -> p k t", k=KCH)
            spars_last = hr_free_tok
            ara_tok = ara_free_tok
            ev2_last = None
            if wst_free_tok:
                pipe['wst_free'] = wst_free_tok
            for tc in range(NTC):
                fill = fill_first and tc == 0
                tiles = [tc * TPT + j for j in range(TPT)]
                lts = []
                tp = {}
                ev = {}
                fills = [(j, k0) for j in range(TPT) for k0 in range(0, KCH, 4)]
                for fi, (j, k0) in enumerate(fills):
                    t = tiles[j]
                    s = j % 2
                    kw = min(4, KCH - k0)
                    b = fi % 2
                    if k0 == 0:
                        if j == 0:
                            wt(nc.vector, stats_tok)
                            wt(nc.vector, spars_last)
                        if j >= 2:
                            wt(nc.vector, tp[(j - 2, 'last')])
                        i = VE.tensor_scalar(
                            out=HRb[:, s * D:(s + 1) * D], in0=xt(t),
                            scalar1=SST[:, 2 * NT + t:2 * NT + t + 1],
                            scalar2=SST[:, 3 * NT + t:3 * NT + t + 1],
                            op0=OP.mult, op1=OP.add)
                        lts.append(i)
                    wt(nc.tensor, lts[j])
                    if fi >= 2:
                        wt(nc.tensor, ev[fi - 2])
                    elif ev2_last:
                        wt(nc.tensor, ev2_last)
                    for kk in range(kw):
                        i = nc.tensor.transpose(
                            out=bankb(b)[:, kk * 128:(kk + 1) * 128],
                            in_=HRb[:, s * D + (k0 + kk) * 128: s * D + (k0 + kk + 1) * 128],
                            identity=IDN[:, :])
                    tok = inc(i, sP)
                    tp[fi] = tok
                    tp[(j, 'last')] = tok
                    wt(nc.vector, tok)
                    src = bankb(b)[:, 0:kw * 128].rearrange("p (k t) -> p k t", k=kw)
                    i = VE.tensor_copy(
                        out=LNF3[:, k0:k0 + kw, j * 128:(j + 1) * 128], in_=src)
                    ev[fi] = i
                tok_lnf = now(sV)

                dma_tok = {}
                cast_tok = {}
                silu_tok = {}
                mult_tok = {}
                g_tok = {}
                u_tok = {}
                spars_tok = {}
                HW = KCH * 128 // 2
                gud = {}
                wbgu = pipe.setdefault('wbgu', {})
                for si in range(GC * 4):
                    gc, r = si // 4, si % 4
                    p = si % 2
                    if fill:
                        for it in range(4):
                            item = si * 4 + it
                            slot = item % 4
                            gu, hh = it // 2, it % 2
                            if item >= 4:
                                wt(nc.sync, cast_tok[item - 4])
                            else:
                                wt(nc.sync, pipe.get('wst_free'))
                            src = (wg_d if gu == 0 else wu_d)[r, gc]
                            dma_tok[item] = dma(nc.sync, WST[:, slot * HW:(slot + 1) * HW],
                                                src[:, hh * HW:(hh + 1) * HW], sWs[slot])
                        for it in range(4):
                            item = si * 4 + it
                            slot = item % 4
                            gu, hh = it // 2, it % 2
                            wt(nc.scalar, dma_tok[item])
                            if si >= 2 and it == 0:
                                wt(nc.scalar, u_tok[si - 2])
                            elif si < 2 and it == 0:
                                wt(nc.scalar, pipe.get('wgu_free'))
                            if hh == 0:
                                wt(nc.scalar, wbgu.get((si - 2, gu)))
                            dst = WGU[:, (gu * 2 + p) * KCH * 128:(gu * 2 + p + 1) * KCH * 128]
                            i = SE.activation(out=dst[:, hh * HW:(hh + 1) * HW],
                                                     in_=WST[:, slot * HW:(slot + 1) * HW],
                                                     func=AF.Copy)
                            cast_tok[item] = i
                            if hh == 1:
                                wt(nc.gpsimd, i)
                                wbgu[(si, gu)] = dma(
                                    nc.gpsimd, (wgc_d if gu == 0 else wuc_d)[r, gc],
                                    dst, sWBg)
                    else:
                        for gu in range(2):
                            slot = gu * 2 + p
                            if si == 0 and gu == 0:
                                wt(nc.sync, pipe.get('wbgu_done'))
                            if gu == 0:
                                if si >= 2:
                                    wt(nc.sync, u_tok[si - 2])
                                else:
                                    wt(nc.sync, pipe.get('wgu_free'))
                            dst = WGU[:, slot * KCH * 128:(slot + 1) * KCH * 128]
                            gud[(si, gu)] = dma(nc.sync, dst,
                                                (wgc_d if gu == 0 else wuc_d)[r, gc],
                                                sWs[slot])
                    if si == 0:
                        wt(nc.tensor, tok_lnf)
                    if fill:
                        wt(nc.tensor, cast_tok[si * 4 + 3])
                    else:
                        wt(nc.tensor, gud[(si, 0)])
                        wt(nc.tensor, gud[(si, 1)])
                    if si >= 2:
                        wt(nc.tensor, silu_tok[si - 2])
                        wt(nc.tensor, mult_tok[si - 2])
                    gw = WGU[:, p * KCH * 128:(p + 1) * KCH * 128]
                    uw = WGU[:, (2 + p) * KCH * 128:(2 + p + 1) * KCH * 128]
                    for k in range(KCH):
                        i = nc.tensor.matmul(out=bank(p)[:, 0:TC],
                                             lhsT=gw[:, k * 128:(k + 1) * 128],
                                             rhs=LNF[:, k * TC:(k + 1) * TC],
                                             start=(k == 0), stop=(k == KCH - 1))
                    g_tok[si] = inc(i, sP)
                    for k in range(KCH):
                        i = nc.tensor.matmul(out=bank(2 + p)[:, 0:TC],
                                             lhsT=uw[:, k * 128:(k + 1) * 128],
                                             rhs=LNF[:, k * TC:(k + 1) * TC],
                                             start=(k == 0), stop=(k == KCH - 1))
                    u_tok[si] = inc(i, sP)
                    wt(nc.scalar, g_tok[si])
                    if si >= 4:
                        wt(nc.scalar, spars_tok[si // 4 - 1])
                    i = SE.activation(out=HR[:, r * TC:(r + 1) * TC],
                                             in_=bank(p)[:, 0:TC], func=AF.Silu)
                    silu_tok[si] = i
                    wt(nc.vector, silu_tok[si])
                    wt(nc.vector, u_tok[si])
                    i = VE.tensor_mul(HR[:, r * TC:(r + 1) * TC],
                                             HR[:, r * TC:(r + 1) * TC],
                                             bank(2 + p)[:, 0:TC])
                    mult_tok[si] = i
                    if r == 3:
                        WDF = WDB.bitcast(F32)
                        wt(nc.scalar, mult_tok[si])
                        if si >= 4:
                            wt(nc.scalar, spars_tok[si // 4 - 1])
                        elif ev2_last:
                            wt(nc.scalar, ev2_last)
                        T = [PS[:, 4 * 512:4 * 512 + TC],
                             WDF[:, 0:TC],
                             PS[:, 5 * 512:5 * 512 + TC],
                             WDF[:, TC:2 * TC]]
                        for rr in range(4):
                            i = SE.activation(out=T[rr],
                                              in_=HR[:, rr * TC:(rr + 1) * TC],
                                              func=AF.Square)
                        sq_tok = i
                        wt(nc.vector, sq_tok)
                        if tc == 0 and gc == 0:
                            wt(nc.vector, ara_tok)
                        Hh = [HR[:, rr * TC:(rr + 1) * TC] for rr in range(4)]
                        S0 = SCR[:, 0:TC]; S1 = SCR[:, TC:2 * TC]; S2 = SCR[:, 2 * TC:3 * TC]
                        VE.tensor_tensor(S0, T[0], T[1], OP.max)
                        VE.tensor_tensor(S1, T[2], T[3], OP.max)
                        VE.tensor_tensor(S2, T[0], T[1], OP.min)
                        VE.tensor_tensor(S0, S0, S1, OP.min)
                        VE.tensor_tensor(S1, T[2], T[3], OP.min)
                        VE.tensor_tensor(S1, S2, S1, OP.max)
                        VE.tensor_tensor(S0, S0, S1, OP.max)
                        if write_out and (gc + 1) * 4 > out_guard_ci:
                            wt(nc.vector, pipe['out_tok'])
                            wt(nc.vector, pipe.get('out_tok2'))
                        for rr in range(4):
                            ci = gc * 4 + rr
                            VE.tensor_tensor(T[rr], T[rr], S0, OP.is_ge)
                            i = VE.tensor_mul(hsp(ci), Hh[rr], T[rr])
                        spars_tok[gc] = i
                spars_last = spars_tok[GC - 1]
                if fill:
                    pipe['wst_free'] = cast_tok[GC * 16 - 1]
                pipe['wgu_free'] = u_tok[GC * 4 - 1]

                dd = {}
                dc = {}
                pd = {}
                wbwd = {}
                DSL = min(4, max(2, (2 * KCH * 128) // FHW))
                for fh in range(FH):
                    for ci in range(ICH):
                        item = fh * ICH + ci
                        gc, r = ci // 4, ci % 4
                        if fill:
                            slot = item % DSL
                            if item >= DSL:
                                wt(nc.sync, dc[item - DSL])
                            else:
                                wt(nc.sync, pipe.get('wst_free'))
                            dd[item] = dma(nc.sync, WST[:, slot * FHW:(slot + 1) * FHW],
                                           wd_d[gc, r][:, fh * FHW:(fh + 1) * FHW], sWs[slot])
                            wt(nc.scalar, dd[item])
                            if item >= 2:
                                wt(nc.scalar, pd.get(item - 2))
                                wt(nc.scalar, wbwd[item - 2])
                            else:
                                wt(nc.scalar, pipe.get('wdb_free'))
                                wt(nc.scalar, spars_last)
                            wsl = WDB[:, (item % 2) * FHW:(item % 2 + 1) * FHW]
                            i = SE.activation(out=wsl,
                                                     in_=WST[:, slot * FHW:(slot + 1) * FHW],
                                                     func=AF.Copy)
                            dc[item] = i
                            wt(nc.gpsimd, i)
                            wbwd[item] = dma(nc.gpsimd,
                                             wdc_d[gc, r][:, fh * FHW:(fh + 1) * FHW],
                                             wsl, sWBd)
                            wt(nc.tensor, dc[item])
                        else:
                            if item == 0:
                                wt(nc.sync, pipe.get('wbwd_done'))
                                wt(nc.sync, pipe.get('wdb_free'))
                            elif item >= 4:
                                wt(nc.sync, pd.get(item - 4))
                            dd[item] = dma(nc.sync,
                                           WSTb[:, (item % 4) * FHW:(item % 4 + 1) * FHW],
                                           wdc_d[gc, r][:, fh * FHW:(fh + 1) * FHW],
                                           sWs[item % 4])
                            wt(nc.tensor, dd[item])
                        if ci == 0:
                            wt(nc.tensor, spars_last)
                            if fh > 0:
                                wt(nc.tensor, ev2_last)
                        wdb = (WDB[:, (item % 2) * FHW:(item % 2 + 1) * FHW] if fill
                               else WSTb[:, (item % 4) * FHW:(item % 4 + 1) * FHW])
                        nsl = min(512, FHW)
                        GRP = FHW // nsl
                        for m in range(TPT):
                            for nn_ in range(GRP):
                                bb = (m * GRP + nn_) * 512
                                i = nc.tensor.matmul(
                                    out=PS[:, bb: bb + nsl],
                                    lhsT=hsp(ci)[:, m * 128:(m + 1) * 128],
                                    rhs=wdb[:, nn_ * nsl:(nn_ + 1) * nsl],
                                    start=(ci == 0), stop=(ci == ICH - 1))
                        pd[item] = inc(i, sP)
                    wt(nc.vector, pd[fh * ICH + ICH - 1])
                    nsl = min(512, FHW)
                    GRP = FHW // nsl
                    for m, t in enumerate(tiles):
                        for nn_ in range(GRP):
                            bb = (m * GRP + nn_) * 512
                            i = VE.scalar_tensor_tensor(
                                out=xt(t)[:, fh * FHW + nn_ * nsl: fh * FHW + (nn_ + 1) * nsl],
                                in0=PS[:, bb: bb + nsl],
                                scalar=0.5,
                                in1=xt(t)[:, fh * FHW + nn_ * nsl: fh * FHW + (nn_ + 1) * nsl],
                                op0=OP.mult, op1=OP.add)
                    ev2_last = i
                if fill:
                    pipe['wst_free'] = dc[FH * ICH - 1]
                    pipe['wbgu_done'] = now(sWBg)
                    pipe['wbwd_done'] = now(sWBd)
                pipe['wdb_free'] = pd[FH * ICH - 1]

                if write_out:
                    oc = {}
                    oo = pipe.setdefault('oo_slot', {})
                    for m, t in enumerate(tiles):
                        s = m % 2
                        wt(nc.scalar, ev2_last)
                        wt(nc.scalar, oo.get(s))
                        i = SE.activation(out=OUTW[s], in_=xt(t), func=AF.Copy)
                        oc[m] = i
                        wt(nc.gpsimd, oc[m])
                        oo[s] = dma(nc.gpsimd, out_d[t * 128:(t + 1) * 128, :],
                                    OUTW[s], sGs[s])
                    pipe['out_tok'] = oo.get(0)
                    pipe['out_tok2'] = oo.get(1)
            return spars_last, ev2_last

        spars_last, ev2_last = emit_ffn(SSTA, tok_ln1, False, None, tok_x_loaded,
                                        wst_free_tok=tok_woe, fill_first=True)

        wt(nc.scalar, ev2_last)
        SQ2 = ARAf[:, 0:D]
        for t in range(NTILE):
            ta = SE.activation(out=SQ2, in_=xt(t), func=AF.Square,
                                          accum_out=SSTB[:, NT + t:NT + t + 1])
            wt(nc.vector, ta)
            VE.reduce_sum(SSTB[:, t:t + 1], xt(t), axis=AX.X)
        tok_ln2 = ln_finalize(SSTB, now(sV))

        groups = []
        remaining = KCH
        k0 = 0
        for buf, width in [(LNF, KCH * TC), (WGU, 4 * KCH * 128),
                           (WSTb, 4 * KCH * 128), (WDB, 2 * FHW),
                           (SCRb, 6 * TC), (HRb, 8 * TC)]:
            nch = min(width // TLOC, remaining)
            if nch > 0:
                groups.append((buf, k0, nch))
                k0 += nch
                remaining -= nch
            if remaining == 0:
                break
        assert remaining == 0, (remaining, KCH)
        ln2f_views = []
        for buf, gk0, nch in groups:
            for q in range(nch):
                ln2f_views.append(buf[:, q * TLOC:(q + 1) * TLOC])

        l2t = []
        for t in range(NTILE):
            if t == 0:
                wt(nc.vector, tok_ln2)
            i = VE.tensor_scalar(
                out=ARA[:, t * D:(t + 1) * D], in0=xt(t),
                scalar1=SSTB[:, 2 * NT + t:2 * NT + t + 1],
                scalar2=SSTB[:, 3 * NT + t:3 * NT + t + 1],
                op0=OP.mult, op1=OP.add)
            l2t.append(i)

        ev3 = {}
        tile_done_tok = {}
        fi = 0
        for t in range(NTILE):
            for k0 in range(0, KCH, 4):
                kw = min(4, KCH - k0)
                b = fi % 2
                wt(nc.tensor, l2t[t])
                if fi >= 2:
                    wt(nc.tensor, ev3[fi - 2])
                elif ev2_last:
                    wt(nc.tensor, ev2_last)
                for kk in range(kw):
                    i = nc.tensor.transpose(
                        out=bankb(b)[:, kk * 128:(kk + 1) * 128],
                        in_=ARA[:, t * D + (k0 + kk) * 128: t * D + (k0 + kk + 1) * 128],
                        identity=IDN[:, :])
                tok = inc(i, sP)
                wt(nc.vector, tok)
                for kk in range(kw):
                    i = VE.tensor_copy(
                        out=ln2f_views[k0 + kk][:, t * 128:(t + 1) * 128],
                        in_=bankb(b)[:, kk * 128:(kk + 1) * 128])
                ev3[fi] = i
                fi += 1
            tile_done_tok[t] = now(sV)
        tok_ln2f = now(sV)

        WKc = ARA[:, 0:KCH * 128]
        WVc = ARA[:, KCH * 128:2 * KCH * 128]
        WKVST = ARAf[:, KCH * 128:2 * KCH * 128]
        need_tiles = (4 * KCH * 128 + D - 1) // D
        wt(nc.sync, tile_done_tok[min(need_tiles - 1, NTILE - 1)])
        tok = dma(nc.sync, WKVST, wk_d[:, :], sWs[0])
        wt(nc.scalar, tok)
        tok_wk = SE.activation(out=WKc, in_=WKVST, func=AF.Copy)
        wt(nc.sync, tok_wk)
        tok = dma(nc.sync, WKVST, wv_d[:, :], sWs[0])
        wt(nc.scalar, tok)
        tok_wv = SE.activation(out=WVc, in_=WKVST, func=AF.Copy)

        KLOC = ARA[:, 2 * KCH * 128:2 * KCH * 128 + TLOC]
        VLOC = ARA[:, 2 * KCH * 128 + TLOC:2 * KCH * 128 + 2 * TLOC]
        wt(nc.tensor, tok_wv)
        wt(nc.tensor, tok_ln2f)
        for k in range(KCH):
            for nn_ in range(TLOC // 512):
                i = nc.tensor.matmul(out=PS[:, nn_ * 512:(nn_ + 1) * 512],
                                     lhsT=WKc[:, k * 128:(k + 1) * 128],
                                     rhs=ln2f_views[k][:, nn_ * 512:(nn_ + 1) * 512],
                                     start=(k == 0), stop=(k == KCH - 1))
        tok_kmm = inc(i, sP)
        wt(nc.vector, tok_kmm)
        tok_kloc = VE.tensor_scalar(out=KLOC, in0=PS[:, 0:TLOC],
                                    scalar1=BKB[:, 0:1], scalar2=None, op0=OP.add)
        wt(nc.gpsimd, tok_kloc)
        tok_ksend = dma(nc.gpsimd, kvs_d[0], KLOC, sG)
        wt(nc.gpsimd, tok_ksend)
        i = nc.gpsimd.collective_compute(
            "AllGather", OP.bypass, replica_groups=replica_groups,
            ins=[kvs_d[0].opt()], outs=[kvrk_d.opt()])
        tok_agk = inc(i, sC)
        tok_kv = None
        for rnd in range((NTILE + 3) // 4):
            ms = list(range(rnd * 4, min(rnd * 4 + 4, NTILE)))
            if rnd > 0:
                wt(nc.tensor, tok_kv)
            for k in range(KCH):
                for m in ms:
                    i = nc.tensor.matmul(
                        out=PS[:, 2048 + (m % 4) * 512: 2048 + (m % 4) * 512 + 128],
                        lhsT=ln2f_views[k][:, m * 128:(m + 1) * 128],
                        rhs=WVc[:, k * 128:(k + 1) * 128],
                        start=(k == 0), stop=False)
            for m in ms:
                i = nc.tensor.matmul(
                    out=PS[:, 2048 + (m % 4) * 512: 2048 + (m % 4) * 512 + 128],
                    lhsT=ONE[0:1, :], rhs=BVR[0:1, :],
                    start=False, stop=True)
            tok_vr = inc(i, sP)
            wt(nc.vector, tok_vr)
            for m in ms:
                tok_kv = VE.tensor_copy(
                    out=VLOC[:, m * 128:(m + 1) * 128],
                    in_=PS[:, 2048 + (m % 4) * 512: 2048 + (m % 4) * 512 + 128])

        wt(nc.gpsimd, tok_kv)
        tok_vsend = dma(nc.gpsimd, kvs_d[1], VLOC, sG)
        wt(nc.gpsimd, tok_vsend)
        i = nc.gpsimd.collective_compute(
            "AllGather", OP.bypass, replica_groups=replica_groups,
            ins=[kvs_d[1].opt()], outs=[kvrv_d.opt()])
        tok_agv = inc(i, sC)
        wt(nc.gpsimd, tok_agk)
        wt(nc.gpsimd, now(sP))
        dma(nc.gpsimd, KFB[:, 0:SFULL].rearrange("p (r t) -> p r t", r=NRANK),
            kvrk_d.rearrange("r p t -> p r t"), sG)
        tok_kfull = now(sG)
        wt(nc.gpsimd, tok_agv)
        dma(nc.gpsimd, VKB[:, 0:SFULL].rearrange("p (r t) -> p r t", r=NRANK),
            kvrv_d.rearrange("r p t -> p r t"), sG)
        tok_vkbfull = now(sG)

        qev = {}
        for m0 in range(0, NTILE, 4):
            b = (m0 // 4) % 2
            mw = min(4, NTILE - m0)
            if m0 == 0:
                wt(nc.tensor, tok_kv)
            if m0 >= 8:
                wt(nc.tensor, qev[m0 - 8])
            for mm in range(mw):
                i = nc.tensor.transpose(out=bankb(b)[:, mm * 128:(mm + 1) * 128],
                                        in_=VLOC[:, (m0 + mm) * 128:(m0 + mm + 1) * 128],
                                        identity=IDN[:, :])
            tok = inc(i, sP)
            wt(nc.vector, tok)
            i = VE.tensor_copy(out=QF[:, m0 * 128:(m0 + mw) * 128],
                                      in_=bankb(b)[:, 0:mw * 128])
            qev[m0] = i
        tok_qf = now(sV)

        SCb = ARAf[:, 0:SFULL]
        PB = ARA[:, 2 * SFULL:3 * SFULL]
        PT = ARA[:, 3 * SFULL:4 * SFULL]
        IZB = MXB[:, NKB + 8:NKB + 8 + NT]
        cps = {}
        af_tok = {}
        exp_toks = {}
        ptv_last = {}
        prev_ptv = {}
        ztr_tok = {}
        zrow_ev = {}

        def score_emit(q):
            for ib in range(NKB):
                b = ib % 4
                if q == 0 and ib == 0:
                    wt(nc.tensor, tok_qf)
                    wt(nc.tensor, tok_kfull)
                if ib >= 4:
                    wt(nc.tensor, cps[(q, ib - 4)])
                elif q > 0:
                    wt(nc.tensor, cps[(q - 1, NKB - 4 + ib)])
                i = nc.tensor.matmul(out=bank(b),
                                     lhsT=QF[:, q * 128:(q + 1) * 128],
                                     rhs=KFB[:, ib * 512:(ib + 1) * 512],
                                     start=True, stop=True)
                tok = inc(i, sP)
                if ib % 2 == 0:
                    wt(nc.scalar, tok)
                    if q == 0 and ib == 0:
                        wt(nc.scalar, tok_vsend)
                    cps[(q, ib)] = SE.activation(
                        out=SCb[:, ib * 512:(ib + 1) * 512],
                        in_=bank(b), func=AF.Copy)
                else:
                    wt(nc.vector, tok)
                    if q == 0 and ib == 1:
                        wt(nc.vector, tok_vsend)
                    cps[(q, ib)] = VE.tensor_copy(
                        out=SCb[:, ib * 512:(ib + 1) * 512], in_=bank(b))

        score_emit(0)
        for qt in range(NTILE):
            mb = NKB + 4 * (qt % 2)
            if qt > 0:
                wt(nc.scalar, ptv_last[qt - 1])
            wt(nc.scalar, cps[(qt, NKB - 1)])
            HS = SFULL // 2
            e0 = SE.activation(out=PB[:, 0:HS], in_=SCb[:, 0:HS], func=AF.Exp,
                               scale=1.0 / 128.0,
                               accum_out=MXB[:, mb + 1:mb + 2])
            e1 = SE.activation(out=PB[:, HS:SFULL], in_=SCb[:, HS:SFULL],
                               func=AF.Exp, scale=1.0 / 128.0,
                               accum_out=MXB[:, mb + 2:mb + 3])
            exp_toks[qt] = (e0, e1)
            wt(nc.vector, e1)
            VE.tensor_add(MXB[:, mb + 1:mb + 2], MXB[:, mb + 1:mb + 2],
                          MXB[:, mb + 2:mb + 3])
            VE.reciprocal(IZB[:, qt:qt + 1], MXB[:, mb + 1:mb + 2])
            if qt >= 2:
                wt(nc.vector, ztr_tok[qt - 2])
            tok_zcb = VE.tensor_copy(out=ZCB[:, qt % 2:qt % 2 + 1],
                                     in_=MXB[:, mb + 1:mb + 2])
            if qt + 1 < NTILE:
                score_emit(qt + 1)
            ptv = {}
            for ci0 in range(0, VCH, 4):
                b = 4 + (ci0 // 4) % 2
                if ci0 == 0:
                    wt(nc.tensor, exp_toks[qt][0])
                    wt(nc.tensor, prev_ptv.get(VCH - 8))
                if ci0 == 4:
                    wt(nc.tensor, prev_ptv.get(VCH - 4))
                if ci0 == VCH // 2:
                    wt(nc.tensor, exp_toks[qt][1])
                if ci0 >= 8:
                    wt(nc.tensor, ptv[ci0 - 8])
                for cc in range(4):
                    i = nc.tensor.transpose(
                        out=bankb(b)[:, cc * 128:(cc + 1) * 128],
                        in_=PB[:, (ci0 + cc) * 128:(ci0 + cc + 1) * 128],
                        identity=IDN[:, :])
                tok = inc(i, sP)
                ptv_last[qt] = tok
                wt(nc.vector, tok)
                i = VE.tensor_copy(out=PT[:, ci0 * 128:(ci0 + 4) * 128],
                                          in_=bankb(b)[:, 0:512])
                ptv[ci0] = i
            prev_ptv = ptv
            tok_pt = now(sV)
            wt(nc.tensor, tok_pt)
            if qt == 0:
                wt(nc.tensor, tok_vkbfull)
            if qt >= 2:
                wt(nc.tensor, af_tok[qt - 2])
            bav = 6 + (qt % 2)
            for ci in range(VCH):
                i = nc.tensor.matmul(out=PS[:, bav * 512:bav * 512 + 128],
                                     lhsT=VKB[:, ci * 128:(ci + 1) * 128],
                                     rhs=PT[:, ci * 128:(ci + 1) * 128],
                                     start=(ci == 0), stop=(ci == VCH - 1))
            tok_av = inc(i, sP)
            bav2 = 6 + ((qt + 1) % 2)
            wt(nc.tensor, tok_zcb)
            if qt >= 2:
                wt(nc.tensor, zrow_ev[qt - 2])
            i = nc.tensor.matmul(out=PS[0:1, bav2 * 512 + 128:bav2 * 512 + 256],
                                 lhsT=ZCB[:, qt % 2:qt % 2 + 1], rhs=IDN[:, :],
                                 start=True, stop=True)
            ztr_tok[qt] = inc(i, sP)
            wt(nc.vector, tok_av)
            af_tok[qt] = VE.tensor_copy(out=AFB[:, qt * 128:(qt + 1) * 128],
                                        in_=PS[:, bav * 512:bav * 512 + 128])
            wt(nc.vector, ztr_tok[qt])
            zrow_ev[qt] = VE.tensor_copy(
                out=ZROWb[0:1, qt * 128:(qt + 1) * 128],
                in_=PS[0:1, bav2 * 512 + 128:bav2 * 512 + 256])
        tok_af = now(sV)

        wo_mm = {}
        wo_ev = {}
        for t in range(NTILE):
            b0 = (t % 2) * 4
            if t == 0:
                wt(nc.tensor, tok_af)
            elif t >= 2:
                wt(nc.tensor, wo_ev[t - 2])
            for nn_ in range((D + 511) // 512):
                ww = min(512, D - nn_ * 512)
                i = nc.tensor.matmul(out=PS[:, b0 * 512 + nn_ * 512: b0 * 512 + nn_ * 512 + ww],
                                     lhsT=AFB[:, t * 128:(t + 1) * 128],
                                     rhs=WOE[:, nn_ * 512:nn_ * 512 + ww],
                                     start=True, stop=False)
                i = nc.tensor.matmul(out=PS[:, b0 * 512 + nn_ * 512: b0 * 512 + nn_ * 512 + ww],
                                     lhsT=ZROWb[0:1, t * 128:(t + 1) * 128],
                                     rhs=BOR[0:1, nn_ * 512:nn_ * 512 + ww],
                                     start=False, stop=True)
            wo_mm[t] = inc(i, sP)
            wt(nc.vector, wo_mm[t])
            i = VE.scalar_tensor_tensor(out=xt(t), in0=PS[:, b0 * 512:b0 * 512 + D],
                                        scalar=IZB[:, t:t + 1], in1=xt(t),
                                        op0=OP.mult, op1=OP.add)
            wo_ev[t] = i
        tok_x2 = now(sV)

        SQ3 = ARAf[:, 0:D]
        for t in range(NTILE):
            wt(nc.scalar, wo_ev[t])
            ta = SE.activation(out=SQ3, in_=xt(t), func=AF.Square,
                                          accum_out=SSTA[:, NT + t:NT + t + 1])
            wt(nc.vector, ta)
            VE.reduce_sum(SSTA[:, t:t + 1], xt(t), axis=AX.X)
        tok_ln3 = ln_finalize(SSTA, now(sV))

        wt(nc.sync, wo_mm[NTILE - 1])
        pipe['wgu_free'] = wo_mm[NTILE - 1]
        pipe['wst_free'] = wo_mm[NTILE - 1]
        emit_ffn(SSTA, tok_ln3, True, (sV, tok_x2[1]), (sV, tok_x2[1]))

        for s_ in sGs:
            nc.gpsimd.wait_ge(s_, cnt[id(s_)])
        nc.gpsimd.wait_ge(sG, cnt[id(sG)])

    return nc


def shard_inputs(cfg, L0_attn_q, L0_attn_Wk, L0_attn_bk, L0_attn_Wv, L0_attn_bv,
                 L0_attn_Wo, L0_attn_bo, L0_ffn_w_gate, L0_ffn_w_up, L0_ffn_w_down):
    D = cfg["D"]; I = cfg["I"]; TLOC = cfg["TLOC"]; n_cores = cfg["n_cores"]
    KCH = D // 128; GC = I // 512

    x = np.ascontiguousarray(np.asarray(L0_attn_q, dtype=np.float32).reshape(-1, D))

    def layout_gu(w):
        w = np.asarray(w, dtype=np.float32)
        w5 = w.reshape(KCH, 128, GC, 128, 4)
        return np.ascontiguousarray(w5.transpose(4, 2, 1, 0, 3).reshape(4, GC, 128, KCH * 128))

    def layout_wd(w):
        w = np.asarray(w, dtype=np.float32)
        w4 = w.reshape(GC, 128, 4, D)
        return np.ascontiguousarray(w4.transpose(0, 2, 1, 3))

    def layout_wkv(w):
        w = np.asarray(w, dtype=np.float32)
        return np.ascontiguousarray(
            w.reshape(KCH, 128, 128).transpose(1, 0, 2).reshape(128, KCH * 128))

    base = {
        "wg": layout_gu(L0_ffn_w_gate),
        "wu": layout_gu(L0_ffn_w_up),
        "wd": layout_wd(L0_ffn_w_down),
        "wk": layout_wkv(L0_attn_Wk),
        "wv": layout_wkv(L0_attn_Wv),
        "wo": np.ascontiguousarray(np.asarray(L0_attn_Wo, dtype=np.float32).reshape(KCH, 128, D)),
        "bk": np.ascontiguousarray(np.asarray(L0_attn_bk, dtype=np.float32).reshape(128, 1)),
        "bv": np.ascontiguousarray(np.asarray(L0_attn_bv, dtype=np.float32).reshape(1, 128)),
        "bo": np.ascontiguousarray(np.asarray(L0_attn_bo, dtype=np.float32).reshape(1, D)),
        "ident": np.eye(128, dtype=BF),
    }
    in_maps = []
    for c in range(n_cores):
        m = dict(base)
        m["x"] = np.ascontiguousarray(x[c * TLOC:(c + 1) * TLOC])
        in_maps.append(m)
    return in_maps


def kernel(L0_attn_q, L0_attn_Wq, L0_attn_bq, L0_attn_Wk, L0_attn_bk,
           L0_attn_Wv, L0_attn_bv, L0_attn_Wo, L0_attn_bo,
           L0_ffn_w_gate, L0_ffn_w_up, L0_ffn_w_down):
    cfg = full_cfg()
    in_maps = shard_inputs(cfg, L0_attn_q, L0_attn_Wk, L0_attn_bk, L0_attn_Wv,
                           L0_attn_bv, L0_attn_Wo, L0_attn_bo,
                           L0_ffn_w_gate, L0_ffn_w_up, L0_ffn_w_down)
    nc = build_nc(cfg)
    res = run_bass_kernel_spmd(nc, in_maps, core_ids=list(range(cfg["n_cores"])))
    out = np.concatenate([res.results[c]["out"] for c in range(cfg["n_cores"])], axis=0)
    return out.reshape(2, 8192, cfg["D"]).astype(np.float32)


# version 06
# speedup vs baseline: 1.2730x; 1.0335x over previous
import numpy as np
import ml_dtypes
from contextlib import ExitStack
import concourse.bass as bass
import concourse.mybir as mybir
from concourse.bass_utils import run_bass_kernel_spmd

F32 = mybir.dt.float32
BF16 = mybir.dt.bfloat16
AF = mybir.ActivationFunctionType
OP = mybir.AluOpType
AX = mybir.AxisListType
BF = ml_dtypes.bfloat16


def full_cfg():
    return dict(D=2048, I=8192, TLOC=2048, TC=512, NRANK=4, n_cores=8, eps=1e-5)


def build_nc(cfg):
    D = cfg["D"]; I = cfg["I"]; TLOC = cfg["TLOC"]; TC = cfg["TC"]
    NRANK = cfg["NRANK"]; n_cores = cfg["n_cores"]; eps = cfg["eps"]
    KCH = D // 128
    ICH = I // 128
    GC = I // 512
    NTILE = TLOC // 128
    NTC = TLOC // TC
    TPT = TC // 128
    SFULL = NRANK * TLOC
    NKB = SFULL // 512
    VCH = SFULL // 128
    FH = max(1, (TPT * D * 4) // 16384)
    FHW = D // FH
    NT = NTILE
    ARAW = max(ICH * TC, 4 * SFULL, NTILE * D, 8 * D)

    nc = bass.Bass(trn_type="TRN2", target_bir_lowering=False)

    x_d = nc.dram_tensor("x", [TLOC, D], F32, kind="ExternalInput").ap()
    wg_d = nc.dram_tensor("wg", [4, GC, 128, KCH * 128], F32, kind="ExternalInput").ap()
    wu_d = nc.dram_tensor("wu", [4, GC, 128, KCH * 128], F32, kind="ExternalInput").ap()
    wd_d = nc.dram_tensor("wd", [GC, 4, 128, D], F32, kind="ExternalInput").ap()
    wk_d = nc.dram_tensor("wk", [128, KCH * 128], F32, kind="ExternalInput").ap()
    wv_d = nc.dram_tensor("wv", [128, KCH * 128], F32, kind="ExternalInput").ap()
    wo_d = nc.dram_tensor("wo", [KCH, 128, D], F32, kind="ExternalInput").ap()
    bk_d = nc.dram_tensor("bk", [128, 1], F32, kind="ExternalInput").ap()
    bv_d = nc.dram_tensor("bv", [1, 128], F32, kind="ExternalInput").ap()
    bo_d = nc.dram_tensor("bo", [1, D], F32, kind="ExternalInput").ap()
    id_d = nc.dram_tensor("ident", [128, 128], BF16, kind="ExternalInput").ap()
    out_d = nc.dram_tensor("out", [TLOC, D], F32, kind="ExternalOutput").ap()
    kvs_d = nc.dram_tensor("kv_send", [2, 128, TLOC], BF16).ap()
    kvrk_d = nc.dram_tensor("k_recv", [NRANK, 128, TLOC], BF16).ap()
    kvrv_d = nc.dram_tensor("v_recv", [NRANK, 128, TLOC], BF16).ap()
    wgc_d = nc.dram_tensor("wg_c", [4, GC, 128, KCH * 128], BF16).ap()
    wuc_d = nc.dram_tensor("wu_c", [4, GC, 128, KCH * 128], BF16).ap()
    wdc_d = nc.dram_tensor("wd_c", [GC, 4, 128, D], BF16).ap()

    replica_groups = [[g * NRANK + i for i in range(NRANK)]
                      for g in range(n_cores // NRANK)]

    st = ExitStack()
    with st:
        X = st.enter_context(nc.sbuf_tensor("Xs", [128, NTILE * D], BF16)).ap()
        ARA = st.enter_context(nc.sbuf_tensor("ARA", [128, ARAW], BF16)).ap()
        LNF = st.enter_context(nc.sbuf_tensor("LNF", [128, KCH * TC], BF16)).ap()
        WGU = st.enter_context(nc.sbuf_tensor("WGU", [128, 4 * KCH * 128], BF16)).ap()
        WST = st.enter_context(nc.sbuf_tensor("WST", [128, 2 * KCH * 128], F32)).ap()
        WDB = st.enter_context(nc.sbuf_tensor("WDB", [128, max(2 * FHW, 4 * TC)], BF16)).ap()
        HR = st.enter_context(nc.sbuf_tensor("HRs", [128, 4 * TC], F32)).ap()
        SCR = st.enter_context(nc.sbuf_tensor("SCR", [128, 3 * TC], F32)).ap()
        WOE = st.enter_context(nc.sbuf_tensor("WOE", [128, D], BF16)).ap()
        SSTA = st.enter_context(nc.sbuf_tensor("SSTA", [128, 4 * NT], F32)).ap()
        SSTB = st.enter_context(nc.sbuf_tensor("SSTB", [128, 4 * NT], F32)).ap()
        MXB = st.enter_context(nc.sbuf_tensor("MXB", [128, NKB + 8 + NT], F32)).ap()
        ZCB = st.enter_context(nc.sbuf_tensor("ZCB", [128, 2], BF16)).ap()
        ZROWb = st.enter_context(nc.sbuf_tensor("ZROW", [1, TLOC], BF16)).ap()
        IDN = st.enter_context(nc.sbuf_tensor("IDN", [128, 128], BF16)).ap()
        ONE = st.enter_context(nc.sbuf_tensor("ONEs", [1, 128], BF16)).ap()
        BKB = st.enter_context(nc.sbuf_tensor("BKB", [128, 1], F32)).ap()
        BVR = st.enter_context(nc.sbuf_tensor("BVR", [1, 128], BF16)).ap()
        BOR = st.enter_context(nc.sbuf_tensor("BOR", [1, D], BF16)).ap()
        PS = st.enter_context(nc.psum_tensor("PS", [128, 4096], F32)).ap()

        sA = st.enter_context(nc.semaphore("sA"))
        sV = st.enter_context(nc.semaphore("sV"))
        sP = st.enter_context(nc.semaphore("sP"))
        sC = st.enter_context(nc.semaphore("sC"))
        sG = st.enter_context(nc.semaphore("sG"))
        sWs = [st.enter_context(nc.semaphore(f"sW{k}")) for k in range(4)]
        sGs = [st.enter_context(nc.semaphore(f"sG{k}")) for k in range(2)]
        sWBs = [st.enter_context(nc.semaphore(f"sWB{k}")) for k in range(4)]
        sWDs = [st.enter_context(nc.semaphore(f"sWD{k}")) for k in range(2)]
        cnt = {}
        for s_ in [sA, sV, sP, sC, sG] + sWs + sGs + sWBs + sWDs:
            cnt[id(s_)] = 0

        def inc(instr, sem, amt=1):
            instr.then_inc(sem, amt)
            cnt[id(sem)] += amt
            return (sem, cnt[id(sem)])

        def now(sem):
            return (sem, cnt[id(sem)])

        def wt(eng, tok):
            if isinstance(eng, Eng):
                eng = eng.eng
            if tok and tok[1] > 0:
                eng.wait_ge(tok[0], tok[1])

        def dma(eng, out, in_, sem):
            if cnt[id(sem)] > 0:
                eng.wait_ge(sem, cnt[id(sem)])
            return inc(eng.dma_start(out=out, in_=in_), sem, 16)

        class Eng:
            def __init__(self, eng, sem):
                self.eng = eng
                self.sem = sem

            def __getattr__(self, name):
                if name in ("wait_ge",):
                    return getattr(self.eng, name)
                fn = getattr(self.eng, name)

                def call(*a, **k):
                    if cnt[id(self.sem)] > 0:
                        self.eng.wait_ge(self.sem, cnt[id(self.sem)])
                    return inc(fn(*a, **k), self.sem)
                return call

        VE = Eng(nc.vector, sV)
        SE = Eng(nc.scalar, sA)

        ARAf = ARA.bitcast(F32)
        WSTb = WST.bitcast(BF16)
        HRb = HR.bitcast(BF16)
        SCRb = SCR.bitcast(BF16)
        PSb = PS.bitcast(BF16)

        def xt(t):
            return X[:, t * D:(t + 1) * D]

        def hsp(ci):
            return ARA[:, ci * TC:(ci + 1) * TC]

        def bank(b):
            return PS[:, b * 512:(b + 1) * 512]

        def bankb(b):
            return PSb[:, b * 1024:(b + 1) * 1024]

        if KCH * TC >= SFULL and 4 * KCH * 128 >= SFULL:
            KFB = LNF[:, 0:SFULL]
            VKB = WGU[:, 0:SFULL]
        else:
            KFB = st.enter_context(nc.sbuf_tensor("KFBt", [128, SFULL], BF16)).ap()
            VKB = st.enter_context(nc.sbuf_tensor("VKBt", [128, SFULL], BF16)).ap()
        QF = WSTb[:, 0:TLOC]
        AFB = WSTb[:, TLOC:2 * TLOC]
        OUTW = [ARAf[:, (ARAW // 2) - 2 * D + s * D: (ARAW // 2) - 2 * D + (s + 1) * D]
                for s in range(2)]
        out_guard_ci = (ARAW - 4 * D) // TC

        dma(nc.sync, IDN[:, :], id_d[:, :], sWs[0])
        dma(nc.sync, BKB[:, :], bk_d[:, :], sWs[1])
        dma(nc.sync, WST[0:1, 0:128], bv_d[:, :], sWs[2])
        dma(nc.sync, WST[0:1, 128:128 + D], bo_d[:, :], sWs[3])
        tok_const = now(sWs[3])

        for s_ in sWs:
            VE.wait_ge(s_, cnt[id(s_)])
        VE.memset(ONE[:, :], 1.0)
        VE.tensor_copy(out=BVR[:, :], in_=WST[0:1, 0:128])
        tok_smallcast = VE.tensor_copy(out=BOR[:, :], in_=WST[0:1, 128:128 + D])

        WOACC = ARAf[:, 0:D]
        wo_dma = []
        wo_add = []
        wt(nc.sync, tok_smallcast)
        for c in range(KCH):
            s = c % 2
            if c >= 2:
                wt(nc.sync, wo_add[c - 2])
            wo_dma.append(dma(nc.sync, WST[:, s * D:(s + 1) * D], wo_d[c], sWs[s]))
            wt(nc.vector, wo_dma[c])
            if c == 0:
                i = VE.tensor_copy(out=WOACC, in_=WST[:, 0:D])
            else:
                i = VE.tensor_add(WOACC, WOACC, WST[:, s * D:(s + 1) * D])
            wo_add.append(i)
        tok_woe = VE.tensor_copy(out=WOE[:, :], in_=WOACC)

        def xstg(s):
            return ARAf[:, (1 + s) * D:(2 + s) * D]

        SQS = ARAf[:, 3 * D:4 * D]
        xcast = []
        for t in range(NTILE):
            s = t % 2
            if t >= 2:
                wt(nc.gpsimd, xcast[2 * (t - 2)])
            tok = dma(nc.gpsimd, xstg(s), x_d[t * 128:(t + 1) * 128, :], sGs[s])
            wt(nc.scalar, tok)
            if t == 0:
                wt(nc.scalar, tok_woe)
            i = SE.activation(out=xt(t), in_=xstg(s), func=AF.Copy,
                                     accum_out=SSTA[:, t:t + 1])
            xcast.append(i)
            i = SE.activation(out=SQS, in_=xt(t), func=AF.Square,
                                     accum_out=SSTA[:, NT + t:NT + t + 1])
            xcast.append(i)
        tok_x_loaded = now(sA)

        def ln_finalize(SST, wait_tok):
            mean = SST[:, 0:NT]; ssq = SST[:, NT:2 * NT]
            rr = SST[:, 2 * NT:3 * NT]; nmr = SST[:, 3 * NT:4 * NT]
            wt(nc.vector, wait_tok)
            VE.tensor_scalar_mul(mean, mean, 1.0 / D)
            VE.tensor_scalar_mul(ssq, ssq, 1.0 / D)
            VE.tensor_mul(rr, mean, mean)
            VE.tensor_sub(ssq, ssq, rr)
            VE.tensor_scalar_add(ssq, ssq, eps)
            tv = VE.reciprocal(rr, ssq)
            wt(nc.scalar, tv)
            ta = SE.activation(out=rr, in_=rr, func=AF.Sqrt)
            wt(nc.vector, ta)
            VE.tensor_mul(nmr, mean, rr)
            return VE.tensor_scalar_mul(nmr, nmr, -1.0)

        tok_ln1 = ln_finalize(SSTA, tok_x_loaded)

        pipe = dict(out_tok=None)

        def emit_ffn(SST, stats_tok, write_out, hr_free_tok, ara_free_tok,
                     wst_free_tok=None, fill_first=False):
            LNF3 = LNF.rearrange("p (k t) -> p k t", k=KCH)
            spars_last = hr_free_tok
            ara_tok = ara_free_tok
            ev2_last = None
            pipe['ev2_banks'] = None
            if wst_free_tok:
                pipe['wst_free'] = wst_free_tok
            for tc in range(NTC):
                fill = fill_first and tc == 0
                tiles = [tc * TPT + j for j in range(TPT)]
                lts = []
                tp = {}
                ev = {}
                fills = [(j, k0) for j in range(TPT) for k0 in range(0, KCH, 4)]
                for fi, (j, k0) in enumerate(fills):
                    t = tiles[j]
                    s = j % 2
                    kw = min(4, KCH - k0)
                    b = fi % 2
                    if k0 == 0:
                        if j == 0:
                            wt(nc.scalar, stats_tok)
                            wt(nc.scalar, spars_last)
                        if j >= 2:
                            wt(nc.scalar, tp[(j - 2, 'last')])
                        i = SE.activation(
                            out=HRb[:, s * D:(s + 1) * D], in_=xt(t),
                            func=AF.Identity,
                            scale=SST[:, 2 * NT + t:2 * NT + t + 1],
                            bias=SST[:, 3 * NT + t:3 * NT + t + 1])
                        lts.append(i)
                    wt(nc.tensor, lts[j])
                    if fi >= 2:
                        wt(nc.tensor, ev[fi - 2])
                    elif pipe.get('ev2_banks'):
                        wt(nc.tensor, pipe['ev2_banks'].get(fi))
                    elif ev2_last:
                        wt(nc.tensor, ev2_last)
                    for kk in range(kw):
                        i = nc.tensor.transpose(
                            out=bankb(b)[:, kk * 128:(kk + 1) * 128],
                            in_=HRb[:, s * D + (k0 + kk) * 128: s * D + (k0 + kk + 1) * 128],
                            identity=IDN[:, :])
                    tok = inc(i, sP)
                    tp[fi] = tok
                    tp[(j, 'last')] = tok
                    wt(nc.vector, tok)
                    src = bankb(b)[:, 0:kw * 128].rearrange("p (k t) -> p k t", k=kw)
                    i = VE.tensor_copy(
                        out=LNF3[:, k0:k0 + kw, j * 128:(j + 1) * 128], in_=src)
                    ev[fi] = i
                tok_lnf = now(sV)

                dma_tok = {}
                cast_tok = {}
                silu_tok = {}
                mult_tok = {}
                g_tok = {}
                u_tok = {}
                spars_tok = {}
                HW = KCH * 128 // 2
                gud = {}
                wbgu = pipe.setdefault('wbgu', {})
                for si in range(GC * 4):
                    gc, r = si // 4, si % 4
                    p = si % 2
                    if fill:
                        for it in range(4):
                            item = si * 4 + it
                            slot = item % 4
                            gu, hh = it // 2, it % 2
                            if item >= 4:
                                wt(nc.sync, cast_tok[item - 4])
                            else:
                                wt(nc.sync, pipe.get('wst_free'))
                            src = (wg_d if gu == 0 else wu_d)[r, gc]
                            dma_tok[item] = dma(nc.sync, WST[:, slot * HW:(slot + 1) * HW],
                                                src[:, hh * HW:(hh + 1) * HW], sWs[slot])
                        for it in range(4):
                            item = si * 4 + it
                            slot = item % 4
                            gu, hh = it // 2, it % 2
                            wt(nc.scalar, dma_tok[item])
                            if si >= 2 and it == 0:
                                wt(nc.scalar, u_tok[si - 2])
                            elif si < 2 and it == 0:
                                wt(nc.scalar, pipe.get('wgu_free'))
                            if hh == 0:
                                wt(nc.scalar, wbgu.get((si - 2, gu)))
                            dst = WGU[:, (gu * 2 + p) * KCH * 128:(gu * 2 + p + 1) * KCH * 128]
                            i = SE.activation(out=dst[:, hh * HW:(hh + 1) * HW],
                                                     in_=WST[:, slot * HW:(slot + 1) * HW],
                                                     func=AF.Copy)
                            cast_tok[item] = i
                            if hh == 1:
                                wt(nc.gpsimd, i)
                                wbgu[(si, gu)] = dma(
                                    nc.gpsimd, (wgc_d if gu == 0 else wuc_d)[r, gc],
                                    dst, sWBs[gu * 2 + p])
                    else:
                        for gu in range(2):
                            slot = gu * 2 + p
                            if si == 0 and gu == 0:
                                for tk in pipe.get('wbgu_done') or []:
                                    wt(nc.sync, tk)
                            if gu == 0:
                                if si >= 2:
                                    wt(nc.sync, u_tok[si - 2])
                                else:
                                    wt(nc.sync, pipe.get('wgu_free'))
                            dst = WGU[:, slot * KCH * 128:(slot + 1) * KCH * 128]
                            gud[(si, gu)] = dma(nc.sync, dst,
                                                (wgc_d if gu == 0 else wuc_d)[r, gc],
                                                sWs[slot])
                    if si == 0:
                        wt(nc.tensor, tok_lnf)
                    if fill:
                        wt(nc.tensor, cast_tok[si * 4 + 3])
                    else:
                        wt(nc.tensor, gud[(si, 0)])
                        wt(nc.tensor, gud[(si, 1)])
                    if si >= 2:
                        wt(nc.tensor, silu_tok[si - 2])
                        wt(nc.tensor, mult_tok[si - 2])
                    gw = WGU[:, p * KCH * 128:(p + 1) * KCH * 128]
                    uw = WGU[:, (2 + p) * KCH * 128:(2 + p + 1) * KCH * 128]
                    for k in range(KCH):
                        i = nc.tensor.matmul(out=bank(p)[:, 0:TC],
                                             lhsT=gw[:, k * 128:(k + 1) * 128],
                                             rhs=LNF[:, k * TC:(k + 1) * TC],
                                             start=(k == 0), stop=(k == KCH - 1))
                    g_tok[si] = inc(i, sP)
                    for k in range(KCH):
                        i = nc.tensor.matmul(out=bank(2 + p)[:, 0:TC],
                                             lhsT=uw[:, k * 128:(k + 1) * 128],
                                             rhs=LNF[:, k * TC:(k + 1) * TC],
                                             start=(k == 0), stop=(k == KCH - 1))
                    u_tok[si] = inc(i, sP)

                    def emit_nl(sj):
                        gcj, rj = sj // 4, sj % 4
                        pj = sj % 2
                        wt(nc.scalar, g_tok[sj])
                        if sj >= 4:
                            wt(nc.scalar, spars_tok[sj // 4 - 1])
                        i = SE.activation(out=HR[:, rj * TC:(rj + 1) * TC],
                                                 in_=bank(pj)[:, 0:TC], func=AF.Silu)
                        silu_tok[sj] = i
                        wt(nc.vector, silu_tok[sj])
                        wt(nc.vector, u_tok[sj])
                        i = VE.tensor_mul(HR[:, rj * TC:(rj + 1) * TC],
                                                 HR[:, rj * TC:(rj + 1) * TC],
                                                 bank(2 + pj)[:, 0:TC])
                        mult_tok[sj] = i
                        if rj == 3:
                            WDF = WDB.bitcast(F32)
                            wt(nc.scalar, mult_tok[sj])
                            if sj >= 4:
                                wt(nc.scalar, spars_tok[sj // 4 - 1])
                            elif ev2_last:
                                wt(nc.scalar, ev2_last)
                            T = [PS[:, 4 * 512:4 * 512 + TC],
                                 WDF[:, 0:TC],
                                 PS[:, 5 * 512:5 * 512 + TC],
                                 WDF[:, TC:2 * TC]]
                            for rr in range(4):
                                i = SE.activation(out=T[rr],
                                                  in_=HR[:, rr * TC:(rr + 1) * TC],
                                                  func=AF.Square)
                            sq_tok = i
                            wt(nc.vector, sq_tok)
                            if tc == 0 and gcj == 0:
                                wt(nc.vector, ara_tok)
                            Hh = [HR[:, rr * TC:(rr + 1) * TC] for rr in range(4)]
                            S0 = SCR[:, 0:TC]; S1 = SCR[:, TC:2 * TC]; S2 = SCR[:, 2 * TC:3 * TC]
                            VE.tensor_tensor(S0, T[0], T[1], OP.max)
                            VE.tensor_tensor(S1, T[2], T[3], OP.max)
                            VE.tensor_tensor(S2, T[0], T[1], OP.min)
                            VE.tensor_tensor(S0, S0, S1, OP.min)
                            VE.tensor_tensor(S1, T[2], T[3], OP.min)
                            VE.tensor_tensor(S1, S2, S1, OP.max)
                            VE.tensor_tensor(S0, S0, S1, OP.max)
                            if write_out and (gcj + 1) * 4 > out_guard_ci:
                                wt(nc.vector, pipe['out_tok'])
                                wt(nc.vector, pipe.get('out_tok2'))
                            for rr in range(4):
                                ci = gcj * 4 + rr
                                VE.tensor_tensor(T[rr], T[rr], S0, OP.is_ge)
                                i = VE.tensor_mul(hsp(ci), Hh[rr], T[rr])
                            spars_tok[gcj] = i

                    if si >= 1:
                        emit_nl(si - 1)
                emit_nl(GC * 4 - 1)
                spars_last = spars_tok[GC - 1]
                if fill:
                    pipe['wst_free'] = cast_tok[GC * 16 - 1]
                pipe['wgu_free'] = u_tok[GC * 4 - 1]

                dd = {}
                dc = {}
                pd = {}
                wbwd = {}
                ev2_banks = {}
                DSL = min(4, max(2, (2 * KCH * 128) // FHW))
                for fh in range(FH):
                    for ci in range(ICH):
                        item = fh * ICH + ci
                        gc, r = ci // 4, ci % 4
                        if fill:
                            slot = item % DSL
                            if item >= DSL:
                                wt(nc.sync, dc[item - DSL])
                            else:
                                wt(nc.sync, pipe.get('wst_free'))
                            dd[item] = dma(nc.sync, WST[:, slot * FHW:(slot + 1) * FHW],
                                           wd_d[gc, r][:, fh * FHW:(fh + 1) * FHW], sWs[slot])
                            wt(nc.scalar, dd[item])
                            if item >= 2:
                                wt(nc.scalar, pd.get(item - 2))
                                wt(nc.scalar, wbwd[item - 2])
                            else:
                                wt(nc.scalar, pipe.get('wdb_free'))
                                wt(nc.scalar, spars_last)
                            wsl = WDB[:, (item % 2) * FHW:(item % 2 + 1) * FHW]
                            i = SE.activation(out=wsl,
                                                     in_=WST[:, slot * FHW:(slot + 1) * FHW],
                                                     func=AF.Copy)
                            dc[item] = i
                            wt(nc.gpsimd, i)
                            wbwd[item] = dma(nc.gpsimd,
                                             wdc_d[gc, r][:, fh * FHW:(fh + 1) * FHW],
                                             wsl, sWDs[item % 2])
                            wt(nc.tensor, dc[item])
                        else:
                            if item == 0:
                                for tk in pipe.get('wbwd_done') or []:
                                    wt(nc.sync, tk)
                                wt(nc.sync, pipe.get('wdb_free'))
                            elif item >= 4:
                                wt(nc.sync, pd.get(item - 4))
                            dd[item] = dma(nc.sync,
                                           WSTb[:, (item % 4) * FHW:(item % 4 + 1) * FHW],
                                           wdc_d[gc, r][:, fh * FHW:(fh + 1) * FHW],
                                           sWs[item % 4])
                            wt(nc.tensor, dd[item])
                        if ci == 0:
                            wt(nc.tensor, spars_last)
                            if fh > 0:
                                wt(nc.tensor, ev2_last)
                        wdb = (WDB[:, (item % 2) * FHW:(item % 2 + 1) * FHW] if fill
                               else WSTb[:, (item % 4) * FHW:(item % 4 + 1) * FHW])
                        nsl = min(512, FHW)
                        GRP = FHW // nsl
                        for m in range(TPT):
                            for nn_ in range(GRP):
                                bb = (m * GRP + nn_) * 512
                                i = nc.tensor.matmul(
                                    out=PS[:, bb: bb + nsl],
                                    lhsT=hsp(ci)[:, m * 128:(m + 1) * 128],
                                    rhs=wdb[:, nn_ * nsl:(nn_ + 1) * nsl],
                                    start=(ci == 0), stop=(ci == ICH - 1))
                        pd[item] = inc(i, sP)
                    wt(nc.vector, pd[fh * ICH + ICH - 1])
                    nsl = min(512, FHW)
                    GRP = FHW // nsl
                    for m, t in enumerate(tiles):
                        for nn_ in range(GRP):
                            bb = (m * GRP + nn_) * 512
                            i = VE.scalar_tensor_tensor(
                                out=xt(t)[:, fh * FHW + nn_ * nsl: fh * FHW + (nn_ + 1) * nsl],
                                in0=PS[:, bb: bb + nsl],
                                scalar=0.5,
                                in1=xt(t)[:, fh * FHW + nn_ * nsl: fh * FHW + (nn_ + 1) * nsl],
                                op0=OP.mult, op1=OP.add)
                            if fh == FH - 1 and bb // 512 < 2:
                                ev2_banks[bb // 512] = i
                    ev2_last = i
                if fill:
                    pipe['wst_free'] = dc[FH * ICH - 1]
                    pipe['wbgu_done'] = [now(s_) for s_ in sWBs]
                    pipe['wbwd_done'] = [now(s_) for s_ in sWDs]
                pipe['wdb_free'] = pd[FH * ICH - 1]
                pipe['ev2_banks'] = ev2_banks

                if write_out:
                    oc = {}
                    oo = pipe.setdefault('oo_slot', {})
                    for m, t in enumerate(tiles):
                        s = m % 2
                        wt(nc.scalar, ev2_last)
                        wt(nc.scalar, oo.get(s))
                        i = SE.activation(out=OUTW[s], in_=xt(t), func=AF.Copy)
                        oc[m] = i
                        wt(nc.gpsimd, oc[m])
                        oo[s] = dma(nc.gpsimd, out_d[t * 128:(t + 1) * 128, :],
                                    OUTW[s], sGs[s])
                    pipe['out_tok'] = oo.get(0)
                    pipe['out_tok2'] = oo.get(1)
            return spars_last, ev2_last

        spars_last, ev2_last = emit_ffn(SSTA, tok_ln1, False, None, tok_x_loaded,
                                        wst_free_tok=tok_woe, fill_first=True)

        wt(nc.scalar, ev2_last)
        SQ2 = ARAf[:, 0:D]
        for t in range(NTILE):
            ta = SE.activation(out=SQ2, in_=xt(t), func=AF.Square,
                                          accum_out=SSTB[:, NT + t:NT + t + 1])
            wt(nc.vector, ta)
            VE.reduce_sum(SSTB[:, t:t + 1], xt(t), axis=AX.X)
        tok_ln2 = ln_finalize(SSTB, now(sV))

        groups = []
        remaining = KCH
        k0 = 0
        for buf, width in [(LNF, KCH * TC), (WGU, 4 * KCH * 128),
                           (WSTb, 4 * KCH * 128), (WDB, 2 * FHW),
                           (SCRb, 6 * TC), (HRb, 8 * TC)]:
            nch = min(width // TLOC, remaining)
            if nch > 0:
                groups.append((buf, k0, nch))
                k0 += nch
                remaining -= nch
            if remaining == 0:
                break
        assert remaining == 0, (remaining, KCH)
        ln2f_views = []
        for buf, gk0, nch in groups:
            for q in range(nch):
                ln2f_views.append(buf[:, q * TLOC:(q + 1) * TLOC])

        l2t = []
        for t in range(NTILE):
            if t == 0:
                wt(nc.vector, tok_ln2)
            i = VE.tensor_scalar(
                out=ARA[:, t * D:(t + 1) * D], in0=xt(t),
                scalar1=SSTB[:, 2 * NT + t:2 * NT + t + 1],
                scalar2=SSTB[:, 3 * NT + t:3 * NT + t + 1],
                op0=OP.mult, op1=OP.add)
            l2t.append(i)

        ev3 = {}
        tile_done_tok = {}
        fi = 0
        for t in range(NTILE):
            for k0 in range(0, KCH, 4):
                kw = min(4, KCH - k0)
                b = fi % 2
                wt(nc.tensor, l2t[t])
                if fi >= 2:
                    wt(nc.tensor, ev3[fi - 2])
                elif ev2_last:
                    wt(nc.tensor, ev2_last)
                for kk in range(kw):
                    i = nc.tensor.transpose(
                        out=bankb(b)[:, kk * 128:(kk + 1) * 128],
                        in_=ARA[:, t * D + (k0 + kk) * 128: t * D + (k0 + kk + 1) * 128],
                        identity=IDN[:, :])
                tok = inc(i, sP)
                wt(nc.vector, tok)
                for kk in range(kw):
                    i = VE.tensor_copy(
                        out=ln2f_views[k0 + kk][:, t * 128:(t + 1) * 128],
                        in_=bankb(b)[:, kk * 128:(kk + 1) * 128])
                ev3[fi] = i
                fi += 1
            tile_done_tok[t] = now(sV)
        tok_ln2f = now(sV)

        WKc = ARA[:, 0:KCH * 128]
        WVc = ARA[:, KCH * 128:2 * KCH * 128]
        WKVST = ARAf[:, KCH * 128:2 * KCH * 128]
        need_tiles = (4 * KCH * 128 + D - 1) // D
        wt(nc.sync, tile_done_tok[min(need_tiles - 1, NTILE - 1)])
        tok = dma(nc.sync, WKVST, wk_d[:, :], sWs[0])
        wt(nc.scalar, tok)
        tok_wk = SE.activation(out=WKc, in_=WKVST, func=AF.Copy)
        wt(nc.sync, tok_wk)
        tok = dma(nc.sync, WKVST, wv_d[:, :], sWs[0])
        wt(nc.scalar, tok)
        tok_wv = SE.activation(out=WVc, in_=WKVST, func=AF.Copy)

        KLOC = ARA[:, 2 * KCH * 128:2 * KCH * 128 + TLOC]
        VLOC = ARA[:, 2 * KCH * 128 + TLOC:2 * KCH * 128 + 2 * TLOC]
        wt(nc.tensor, tok_wv)
        wt(nc.tensor, tok_ln2f)
        for k in range(KCH):
            for nn_ in range(TLOC // 512):
                i = nc.tensor.matmul(out=PS[:, nn_ * 512:(nn_ + 1) * 512],
                                     lhsT=WKc[:, k * 128:(k + 1) * 128],
                                     rhs=ln2f_views[k][:, nn_ * 512:(nn_ + 1) * 512],
                                     start=(k == 0), stop=(k == KCH - 1))
        tok_kmm = inc(i, sP)
        wt(nc.vector, tok_kmm)
        tok_kloc = VE.tensor_scalar(out=KLOC, in0=PS[:, 0:TLOC],
                                    scalar1=BKB[:, 0:1], scalar2=None, op0=OP.add)
        wt(nc.gpsimd, tok_kloc)
        tok_ksend = dma(nc.gpsimd, kvs_d[0], KLOC, sG)
        wt(nc.gpsimd, tok_ksend)
        i = nc.gpsimd.collective_compute(
            "AllGather", OP.bypass, replica_groups=replica_groups,
            ins=[kvs_d[0].opt()], outs=[kvrk_d.opt()])
        tok_agk = inc(i, sC)
        tok_kv = None
        for rnd in range((NTILE + 3) // 4):
            ms = list(range(rnd * 4, min(rnd * 4 + 4, NTILE)))
            if rnd > 0:
                wt(nc.tensor, tok_kv)
            for k in range(KCH):
                for m in ms:
                    i = nc.tensor.matmul(
                        out=PS[:, 2048 + (m % 4) * 512: 2048 + (m % 4) * 512 + 128],
                        lhsT=ln2f_views[k][:, m * 128:(m + 1) * 128],
                        rhs=WVc[:, k * 128:(k + 1) * 128],
                        start=(k == 0), stop=False)
            for m in ms:
                i = nc.tensor.matmul(
                    out=PS[:, 2048 + (m % 4) * 512: 2048 + (m % 4) * 512 + 128],
                    lhsT=ONE[0:1, :], rhs=BVR[0:1, :],
                    start=False, stop=True)
            tok_vr = inc(i, sP)
            wt(nc.vector, tok_vr)
            for m in ms:
                tok_kv = VE.tensor_copy(
                    out=VLOC[:, m * 128:(m + 1) * 128],
                    in_=PS[:, 2048 + (m % 4) * 512: 2048 + (m % 4) * 512 + 128])

        wt(nc.gpsimd, tok_kv)
        tok_vsend = dma(nc.gpsimd, kvs_d[1], VLOC, sG)
        wt(nc.gpsimd, tok_vsend)
        i = nc.gpsimd.collective_compute(
            "AllGather", OP.bypass, replica_groups=replica_groups,
            ins=[kvs_d[1].opt()], outs=[kvrv_d.opt()])
        tok_agv = inc(i, sC)
        wt(nc.gpsimd, tok_agk)
        wt(nc.gpsimd, now(sP))
        dma(nc.gpsimd, KFB[:, 0:SFULL].rearrange("p (r t) -> p r t", r=NRANK),
            kvrk_d.rearrange("r p t -> p r t"), sG)
        tok_kfull = now(sG)
        wt(nc.gpsimd, tok_agv)
        dma(nc.gpsimd, VKB[:, 0:SFULL].rearrange("p (r t) -> p r t", r=NRANK),
            kvrv_d.rearrange("r p t -> p r t"), sG)
        tok_vkbfull = now(sG)

        qev = {}
        for m0 in range(0, NTILE, 4):
            b = (m0 // 4) % 2
            mw = min(4, NTILE - m0)
            if m0 == 0:
                wt(nc.tensor, tok_kv)
            if m0 >= 8:
                wt(nc.tensor, qev[m0 - 8])
            for mm in range(mw):
                i = nc.tensor.transpose(out=bankb(b)[:, mm * 128:(mm + 1) * 128],
                                        in_=VLOC[:, (m0 + mm) * 128:(m0 + mm + 1) * 128],
                                        identity=IDN[:, :])
            tok = inc(i, sP)
            wt(nc.vector, tok)
            i = VE.tensor_copy(out=QF[:, m0 * 128:(m0 + mw) * 128],
                                      in_=bankb(b)[:, 0:mw * 128])
            qev[m0] = i
        tok_qf = now(sV)

        SCb = ARAf[:, 0:SFULL]
        PB = ARA[:, 2 * SFULL:3 * SFULL]
        PT = ARA[:, 3 * SFULL:4 * SFULL]
        IZB = MXB[:, NKB + 8:NKB + 8 + NT]
        cps = {}
        af_tok = {}
        exp_toks = {}
        ptv_last = {}
        prev_ptv = {}
        ztr_tok = {}
        zrow_ev = {}

        def score_emit(q):
            for ib in range(NKB):
                b = ib % 4
                if q == 0 and ib == 0:
                    wt(nc.tensor, tok_qf)
                    wt(nc.tensor, tok_kfull)
                if ib >= 4:
                    wt(nc.tensor, cps[(q, ib - 4)])
                elif q > 0:
                    wt(nc.tensor, cps[(q - 1, NKB - 4 + ib)])
                i = nc.tensor.matmul(out=bank(b),
                                     lhsT=QF[:, q * 128:(q + 1) * 128],
                                     rhs=KFB[:, ib * 512:(ib + 1) * 512],
                                     start=True, stop=True)
                tok = inc(i, sP)
                if ib % 2 == 0:
                    wt(nc.scalar, tok)
                    if q == 0 and ib == 0:
                        wt(nc.scalar, tok_vsend)
                    cps[(q, ib)] = SE.activation(
                        out=SCb[:, ib * 512:(ib + 1) * 512],
                        in_=bank(b), func=AF.Copy)
                else:
                    wt(nc.vector, tok)
                    if q == 0 and ib == 1:
                        wt(nc.vector, tok_vsend)
                    cps[(q, ib)] = VE.tensor_copy(
                        out=SCb[:, ib * 512:(ib + 1) * 512], in_=bank(b))

        score_emit(0)
        for qt in range(NTILE):
            mb = NKB + 4 * (qt % 2)
            if qt > 0:
                wt(nc.scalar, ptv_last[qt - 1])
            wt(nc.scalar, cps[(qt, NKB - 1)])
            HS = SFULL // 2
            e0 = SE.activation(out=PB[:, 0:HS], in_=SCb[:, 0:HS], func=AF.Exp,
                               scale=1.0 / 128.0,
                               accum_out=MXB[:, mb + 1:mb + 2])
            e1 = SE.activation(out=PB[:, HS:SFULL], in_=SCb[:, HS:SFULL],
                               func=AF.Exp, scale=1.0 / 128.0,
                               accum_out=MXB[:, mb + 2:mb + 3])
            exp_toks[qt] = (e0, e1)
            wt(nc.vector, e1)
            VE.tensor_add(MXB[:, mb + 1:mb + 2], MXB[:, mb + 1:mb + 2],
                          MXB[:, mb + 2:mb + 3])
            VE.reciprocal(IZB[:, qt:qt + 1], MXB[:, mb + 1:mb + 2])
            if qt >= 2:
                wt(nc.vector, ztr_tok[qt - 2])
            tok_zcb = VE.tensor_copy(out=ZCB[:, qt % 2:qt % 2 + 1],
                                     in_=MXB[:, mb + 1:mb + 2])
            if qt + 1 < NTILE:
                score_emit(qt + 1)
            ptv = {}
            for ci0 in range(0, VCH, 4):
                b = 4 + (ci0 // 4) % 2
                if ci0 == 0:
                    wt(nc.tensor, exp_toks[qt][0])
                    wt(nc.tensor, prev_ptv.get(VCH - 8))
                if ci0 == 4:
                    wt(nc.tensor, prev_ptv.get(VCH - 4))
                if ci0 == VCH // 2:
                    wt(nc.tensor, exp_toks[qt][1])
                if ci0 >= 8:
                    wt(nc.tensor, ptv[ci0 - 8])
                for cc in range(4):
                    i = nc.tensor.transpose(
                        out=bankb(b)[:, cc * 128:(cc + 1) * 128],
                        in_=PB[:, (ci0 + cc) * 128:(ci0 + cc + 1) * 128],
                        identity=IDN[:, :])
                tok = inc(i, sP)
                ptv_last[qt] = tok
                wt(nc.vector, tok)
                i = VE.tensor_copy(out=PT[:, ci0 * 128:(ci0 + 4) * 128],
                                          in_=bankb(b)[:, 0:512])
                ptv[ci0] = i
            prev_ptv = ptv
            tok_pt = now(sV)
            wt(nc.tensor, tok_pt)
            if qt == 0:
                wt(nc.tensor, tok_vkbfull)
            if qt >= 2:
                wt(nc.tensor, af_tok[qt - 2])
            bav = 6 + (qt % 2)
            for ci in range(VCH):
                i = nc.tensor.matmul(out=PS[:, bav * 512:bav * 512 + 128],
                                     lhsT=VKB[:, ci * 128:(ci + 1) * 128],
                                     rhs=PT[:, ci * 128:(ci + 1) * 128],
                                     start=(ci == 0), stop=(ci == VCH - 1))
            tok_av = inc(i, sP)
            bav2 = 6 + ((qt + 1) % 2)
            wt(nc.tensor, tok_zcb)
            if qt >= 2:
                wt(nc.tensor, zrow_ev[qt - 2])
            i = nc.tensor.matmul(out=PS[0:1, bav2 * 512 + 128:bav2 * 512 + 256],
                                 lhsT=ZCB[:, qt % 2:qt % 2 + 1], rhs=IDN[:, :],
                                 start=True, stop=True)
            ztr_tok[qt] = inc(i, sP)
            wt(nc.vector, tok_av)
            af_tok[qt] = VE.tensor_copy(out=AFB[:, qt * 128:(qt + 1) * 128],
                                        in_=PS[:, bav * 512:bav * 512 + 128])
            wt(nc.vector, ztr_tok[qt])
            zrow_ev[qt] = VE.tensor_copy(
                out=ZROWb[0:1, qt * 128:(qt + 1) * 128],
                in_=PS[0:1, bav2 * 512 + 128:bav2 * 512 + 256])
        tok_af = now(sV)

        wo_mm = {}
        wo_ev = {}
        for t in range(NTILE):
            b0 = (t % 2) * 4
            if t == 0:
                wt(nc.tensor, tok_af)
            elif t >= 2:
                wt(nc.tensor, wo_ev[t - 2])
            for nn_ in range((D + 511) // 512):
                ww = min(512, D - nn_ * 512)
                i = nc.tensor.matmul(out=PS[:, b0 * 512 + nn_ * 512: b0 * 512 + nn_ * 512 + ww],
                                     lhsT=AFB[:, t * 128:(t + 1) * 128],
                                     rhs=WOE[:, nn_ * 512:nn_ * 512 + ww],
                                     start=True, stop=False)
                i = nc.tensor.matmul(out=PS[:, b0 * 512 + nn_ * 512: b0 * 512 + nn_ * 512 + ww],
                                     lhsT=ZROWb[0:1, t * 128:(t + 1) * 128],
                                     rhs=BOR[0:1, nn_ * 512:nn_ * 512 + ww],
                                     start=False, stop=True)
            wo_mm[t] = inc(i, sP)
            wt(nc.vector, wo_mm[t])
            i = VE.scalar_tensor_tensor(out=xt(t), in0=PS[:, b0 * 512:b0 * 512 + D],
                                        scalar=IZB[:, t:t + 1], in1=xt(t),
                                        op0=OP.mult, op1=OP.add)
            wo_ev[t] = i
        tok_x2 = now(sV)

        SQ3 = ARAf[:, 0:D]
        for t in range(NTILE):
            wt(nc.scalar, wo_ev[t])
            ta = SE.activation(out=SQ3, in_=xt(t), func=AF.Square,
                                          accum_out=SSTA[:, NT + t:NT + t + 1])
            wt(nc.vector, ta)
            VE.reduce_sum(SSTA[:, t:t + 1], xt(t), axis=AX.X)
        tok_ln3 = ln_finalize(SSTA, now(sV))

        wt(nc.sync, wo_mm[NTILE - 1])
        pipe['wgu_free'] = wo_mm[NTILE - 1]
        pipe['wst_free'] = wo_mm[NTILE - 1]
        emit_ffn(SSTA, tok_ln3, True, (sV, tok_x2[1]), (sV, tok_x2[1]))

        for s_ in sGs:
            nc.gpsimd.wait_ge(s_, cnt[id(s_)])
        nc.gpsimd.wait_ge(sG, cnt[id(sG)])

    return nc


def shard_inputs(cfg, L0_attn_q, L0_attn_Wk, L0_attn_bk, L0_attn_Wv, L0_attn_bv,
                 L0_attn_Wo, L0_attn_bo, L0_ffn_w_gate, L0_ffn_w_up, L0_ffn_w_down):
    D = cfg["D"]; I = cfg["I"]; TLOC = cfg["TLOC"]; n_cores = cfg["n_cores"]
    KCH = D // 128; GC = I // 512

    x = np.ascontiguousarray(np.asarray(L0_attn_q, dtype=np.float32).reshape(-1, D))

    def layout_gu(w):
        w = np.asarray(w, dtype=np.float32)
        w5 = w.reshape(KCH, 128, GC, 128, 4)
        return np.ascontiguousarray(w5.transpose(4, 2, 1, 0, 3).reshape(4, GC, 128, KCH * 128))

    def layout_wd(w):
        w = np.asarray(w, dtype=np.float32)
        w4 = w.reshape(GC, 128, 4, D)
        return np.ascontiguousarray(w4.transpose(0, 2, 1, 3))

    def layout_wkv(w):
        w = np.asarray(w, dtype=np.float32)
        return np.ascontiguousarray(
            w.reshape(KCH, 128, 128).transpose(1, 0, 2).reshape(128, KCH * 128))

    base = {
        "wg": layout_gu(L0_ffn_w_gate),
        "wu": layout_gu(L0_ffn_w_up),
        "wd": layout_wd(L0_ffn_w_down),
        "wk": layout_wkv(L0_attn_Wk),
        "wv": layout_wkv(L0_attn_Wv),
        "wo": np.ascontiguousarray(np.asarray(L0_attn_Wo, dtype=np.float32).reshape(KCH, 128, D)),
        "bk": np.ascontiguousarray(np.asarray(L0_attn_bk, dtype=np.float32).reshape(128, 1)),
        "bv": np.ascontiguousarray(np.asarray(L0_attn_bv, dtype=np.float32).reshape(1, 128)),
        "bo": np.ascontiguousarray(np.asarray(L0_attn_bo, dtype=np.float32).reshape(1, D)),
        "ident": np.eye(128, dtype=BF),
    }
    in_maps = []
    for c in range(n_cores):
        m = dict(base)
        m["x"] = np.ascontiguousarray(x[c * TLOC:(c + 1) * TLOC])
        in_maps.append(m)
    return in_maps


def kernel(L0_attn_q, L0_attn_Wq, L0_attn_bq, L0_attn_Wk, L0_attn_bk,
           L0_attn_Wv, L0_attn_bv, L0_attn_Wo, L0_attn_bo,
           L0_ffn_w_gate, L0_ffn_w_up, L0_ffn_w_down):
    cfg = full_cfg()
    in_maps = shard_inputs(cfg, L0_attn_q, L0_attn_Wk, L0_attn_bk, L0_attn_Wv,
                           L0_attn_bv, L0_attn_Wo, L0_attn_bo,
                           L0_ffn_w_gate, L0_ffn_w_up, L0_ffn_w_down)
    nc = build_nc(cfg)
    res = run_bass_kernel_spmd(nc, in_maps, core_ids=list(range(cfg["n_cores"])))
    out = np.concatenate([res.results[c]["out"] for c in range(cfg["n_cores"])], axis=0)
    return out.reshape(2, 8192, cfg["D"]).astype(np.float32)


# version 08
# speedup vs baseline: 1.2928x; 1.0155x over previous
import numpy as np
import ml_dtypes
from contextlib import ExitStack
import concourse.bass as bass
import concourse.mybir as mybir
from concourse.bass_utils import run_bass_kernel_spmd

F32 = mybir.dt.float32
BF16 = mybir.dt.bfloat16
AF = mybir.ActivationFunctionType
OP = mybir.AluOpType
AX = mybir.AxisListType
BF = ml_dtypes.bfloat16


def full_cfg():
    return dict(D=2048, I=8192, TLOC=2048, TC=512, NRANK=4, n_cores=8, eps=1e-5)


def build_nc(cfg):
    D = cfg["D"]; I = cfg["I"]; TLOC = cfg["TLOC"]; TC = cfg["TC"]
    NRANK = cfg["NRANK"]; n_cores = cfg["n_cores"]; eps = cfg["eps"]
    KCH = D // 128
    ICH = I // 128
    GC = I // 512
    NTILE = TLOC // 128
    NTC = TLOC // TC
    TPT = TC // 128
    SFULL = NRANK * TLOC
    NKB = SFULL // 512
    VCH = SFULL // 128
    FH = max(1, (TPT * D * 4) // 16384)
    FHW = D // FH
    NT = NTILE
    ARAW = max(ICH * TC, 4 * SFULL, NTILE * D, 8 * D)

    nc = bass.Bass(trn_type="TRN2", target_bir_lowering=False)

    x_d = nc.dram_tensor("x", [TLOC, D], F32, kind="ExternalInput").ap()
    wg_d = nc.dram_tensor("wg", [4, GC, 128, KCH * 128], F32, kind="ExternalInput").ap()
    wu_d = nc.dram_tensor("wu", [4, GC, 128, KCH * 128], F32, kind="ExternalInput").ap()
    wd_d = nc.dram_tensor("wd", [GC, 4, 128, D], F32, kind="ExternalInput").ap()
    wk_d = nc.dram_tensor("wk", [128, KCH * 128], F32, kind="ExternalInput").ap()
    wv_d = nc.dram_tensor("wv", [128, KCH * 128], F32, kind="ExternalInput").ap()
    wo_d = nc.dram_tensor("wo", [KCH, 128, D], F32, kind="ExternalInput").ap()
    bk_d = nc.dram_tensor("bk", [128, 1], F32, kind="ExternalInput").ap()
    bv_d = nc.dram_tensor("bv", [1, 128], F32, kind="ExternalInput").ap()
    bo_d = nc.dram_tensor("bo", [1, D], F32, kind="ExternalInput").ap()
    id_d = nc.dram_tensor("ident", [128, 128], BF16, kind="ExternalInput").ap()
    out_d = nc.dram_tensor("out", [TLOC, D], F32, kind="ExternalOutput").ap()
    kvs_d = nc.dram_tensor("kv_send", [2, 128, TLOC], BF16).ap()
    kvrk_d = nc.dram_tensor("k_recv", [NRANK, 128, TLOC], BF16).ap()
    kvrv_d = nc.dram_tensor("v_recv", [NRANK, 128, TLOC], BF16).ap()
    wgc_d = nc.dram_tensor("wg_c", [4, GC, 128, KCH * 128], BF16).ap()
    wuc_d = nc.dram_tensor("wu_c", [4, GC, 128, KCH * 128], BF16).ap()
    wdc_d = nc.dram_tensor("wd_c", [GC, 4, 128, D], BF16).ap()

    replica_groups = [[g * NRANK + i for i in range(NRANK)]
                      for g in range(n_cores // NRANK)]

    st = ExitStack()
    with st:
        X = st.enter_context(nc.sbuf_tensor("Xs", [128, NTILE * D], BF16)).ap()
        ARA = st.enter_context(nc.sbuf_tensor("ARA", [128, ARAW], BF16)).ap()
        LNF = st.enter_context(nc.sbuf_tensor("LNF", [128, KCH * TC], BF16)).ap()
        WGU = st.enter_context(nc.sbuf_tensor("WGU", [128, 4 * KCH * 128], BF16)).ap()
        WST = st.enter_context(nc.sbuf_tensor("WST", [128, 2 * KCH * 128], F32)).ap()
        WDB = st.enter_context(nc.sbuf_tensor("WDB", [128, max(2 * FHW, 4 * TC)], BF16)).ap()
        HR = st.enter_context(nc.sbuf_tensor("HRs", [128, 4 * TC], F32)).ap()
        SCR = st.enter_context(nc.sbuf_tensor("SCR", [128, 3 * TC], F32)).ap()
        WOE = st.enter_context(nc.sbuf_tensor("WOE", [128, D], BF16)).ap()
        SSTA = st.enter_context(nc.sbuf_tensor("SSTA", [128, 4 * NT], F32)).ap()
        SSTB = st.enter_context(nc.sbuf_tensor("SSTB", [128, 4 * NT], F32)).ap()
        MXB = st.enter_context(nc.sbuf_tensor("MXB", [128, NKB + 8 + NT], F32)).ap()
        ZCB = st.enter_context(nc.sbuf_tensor("ZCB", [128, 2], BF16)).ap()
        ZROWb = st.enter_context(nc.sbuf_tensor("ZROW", [1, TLOC], BF16)).ap()
        IDN = st.enter_context(nc.sbuf_tensor("IDN", [128, 128], BF16)).ap()
        ONE = st.enter_context(nc.sbuf_tensor("ONEs", [1, 128], BF16)).ap()
        BKB = st.enter_context(nc.sbuf_tensor("BKB", [128, 1], F32)).ap()
        BVR = st.enter_context(nc.sbuf_tensor("BVR", [1, 128], BF16)).ap()
        BOR = st.enter_context(nc.sbuf_tensor("BOR", [1, D], BF16)).ap()
        PS = st.enter_context(nc.psum_tensor("PS", [128, 4096], F32)).ap()

        sA = st.enter_context(nc.semaphore("sA"))
        sV = st.enter_context(nc.semaphore("sV"))
        sP = st.enter_context(nc.semaphore("sP"))
        sC = st.enter_context(nc.semaphore("sC"))
        sG = st.enter_context(nc.semaphore("sG"))
        sWs = [st.enter_context(nc.semaphore(f"sW{k}")) for k in range(4)]
        sGs = [st.enter_context(nc.semaphore(f"sG{k}")) for k in range(2)]
        sWBs = [st.enter_context(nc.semaphore(f"sWB{k}")) for k in range(4)]
        sWDs = [st.enter_context(nc.semaphore(f"sWD{k}")) for k in range(2)]
        cnt = {}
        for s_ in [sA, sV, sP, sC, sG] + sWs + sGs + sWBs + sWDs:
            cnt[id(s_)] = 0

        def inc(instr, sem, amt=1):
            instr.then_inc(sem, amt)
            cnt[id(sem)] += amt
            return (sem, cnt[id(sem)])

        def now(sem):
            return (sem, cnt[id(sem)])

        def wt(eng, tok):
            if isinstance(eng, Eng):
                eng = eng.eng
            if tok and tok[1] > 0:
                eng.wait_ge(tok[0], tok[1])

        def dma(eng, out, in_, sem):
            if cnt[id(sem)] > 0:
                eng.wait_ge(sem, cnt[id(sem)])
            return inc(eng.dma_start(out=out, in_=in_), sem, 16)

        class Eng:
            def __init__(self, eng, sem):
                self.eng = eng
                self.sem = sem

            def __getattr__(self, name):
                if name in ("wait_ge",):
                    return getattr(self.eng, name)
                fn = getattr(self.eng, name)

                def call(*a, **k):
                    if cnt[id(self.sem)] > 0:
                        self.eng.wait_ge(self.sem, cnt[id(self.sem)])
                    return inc(fn(*a, **k), self.sem)
                return call

        VE = Eng(nc.vector, sV)
        SE = Eng(nc.scalar, sA)

        ARAf = ARA.bitcast(F32)
        WSTb = WST.bitcast(BF16)
        HRb = HR.bitcast(BF16)
        SCRb = SCR.bitcast(BF16)
        PSb = PS.bitcast(BF16)

        def xt(t):
            return X[:, t * D:(t + 1) * D]

        def hsp(ci):
            return ARA[:, ci * TC:(ci + 1) * TC]

        def bank(b):
            return PS[:, b * 512:(b + 1) * 512]

        def bankb(b):
            return PSb[:, b * 1024:(b + 1) * 1024]

        if KCH * TC >= SFULL and 4 * KCH * 128 >= SFULL:
            KFB = LNF[:, 0:SFULL]
            VKB = WGU[:, 0:SFULL]
        else:
            KFB = st.enter_context(nc.sbuf_tensor("KFBt", [128, SFULL], BF16)).ap()
            VKB = st.enter_context(nc.sbuf_tensor("VKBt", [128, SFULL], BF16)).ap()
        QF = WSTb[:, 0:TLOC]
        AFB = WSTb[:, TLOC:2 * TLOC]
        OUTW = [ARAf[:, (ARAW // 2) - 2 * D + s * D: (ARAW // 2) - 2 * D + (s + 1) * D]
                for s in range(2)]
        out_guard_ci = (ARAW - 4 * D) // TC

        dma(nc.sync, IDN[:, :], id_d[:, :], sWs[0])
        dma(nc.sync, BKB[:, :], bk_d[:, :], sWs[1])
        dma(nc.sync, WST[0:1, 0:128], bv_d[:, :], sWs[2])
        dma(nc.sync, WST[0:1, 128:128 + D], bo_d[:, :], sWs[3])
        tok_const = now(sWs[3])

        for s_ in sWs:
            VE.wait_ge(s_, cnt[id(s_)])
        VE.memset(ONE[:, :], 1.0)
        VE.tensor_copy(out=BVR[:, :], in_=WST[0:1, 0:128])
        tok_smallcast = VE.tensor_copy(out=BOR[:, :], in_=WST[0:1, 128:128 + D])

        WOACC = ARAf[:, 0:D]
        wo_dma = []
        wo_add = []
        wt(nc.sync, tok_smallcast)
        for c in range(KCH):
            s = c % 2
            if c >= 2:
                wt(nc.sync, wo_add[c - 2])
            wo_dma.append(dma(nc.sync, WST[:, s * D:(s + 1) * D], wo_d[c], sWs[s]))
            wt(nc.vector, wo_dma[c])
            if c == 0:
                i = VE.tensor_copy(out=WOACC, in_=WST[:, 0:D])
            else:
                i = VE.tensor_add(WOACC, WOACC, WST[:, s * D:(s + 1) * D])
            wo_add.append(i)
        tok_woe = VE.tensor_copy(out=WOE[:, :], in_=WOACC)

        def xstg(s):
            return ARAf[:, (1 + s) * D:(2 + s) * D]

        SQS = ARAf[:, 3 * D:4 * D]
        xcast = []
        for t in range(NTILE):
            s = t % 2
            if t >= 2:
                wt(nc.gpsimd, xcast[2 * (t - 2)])
            tok = dma(nc.gpsimd, xstg(s), x_d[t * 128:(t + 1) * 128, :], sGs[s])
            wt(nc.scalar, tok)
            if t == 0:
                wt(nc.scalar, tok_woe)
            i = SE.activation(out=xt(t), in_=xstg(s), func=AF.Copy,
                                     accum_out=SSTA[:, t:t + 1])
            xcast.append(i)
            i = SE.activation(out=SQS, in_=xt(t), func=AF.Square,
                                     accum_out=SSTA[:, NT + t:NT + t + 1])
            xcast.append(i)
        tok_x_loaded = now(sA)

        def ln_finalize(SST, wait_tok):
            mean = SST[:, 0:NT]; ssq = SST[:, NT:2 * NT]
            rr = SST[:, 2 * NT:3 * NT]; nmr = SST[:, 3 * NT:4 * NT]
            wt(nc.vector, wait_tok)
            VE.tensor_scalar_mul(mean, mean, 1.0 / D)
            VE.tensor_scalar_mul(ssq, ssq, 1.0 / D)
            VE.tensor_mul(rr, mean, mean)
            VE.tensor_sub(ssq, ssq, rr)
            VE.tensor_scalar_add(ssq, ssq, eps)
            tv = VE.reciprocal(rr, ssq)
            wt(nc.scalar, tv)
            ta = SE.activation(out=rr, in_=rr, func=AF.Sqrt)
            wt(nc.vector, ta)
            VE.tensor_mul(nmr, mean, rr)
            return VE.tensor_scalar_mul(nmr, nmr, -1.0)

        tok_ln1 = ln_finalize(SSTA, tok_x_loaded)

        pipe = dict(out_tok=None)

        def emit_ffn(SST, stats_tok, write_out, hr_free_tok, ara_free_tok,
                     wst_free_tok=None, fill_first=False):
            LNF3 = LNF.rearrange("p (k t) -> p k t", k=KCH)
            spars_last = hr_free_tok
            ara_tok = ara_free_tok
            ev2_last = None
            pipe['ev2_banks'] = None
            if wst_free_tok:
                pipe['wst_free'] = wst_free_tok

            def emit_out():
                po = pipe.pop('pending_out', None)
                if not po:
                    return
                otiles, etiles = po
                oo = pipe.setdefault('oo_slot', {})
                for m, t in enumerate(otiles):
                    s = m % 2
                    wt(nc.scalar, etiles[m])
                    wt(nc.scalar, oo.get(s))
                    i = SE.activation(out=OUTW[s], in_=xt(t), func=AF.Copy)
                    wt(nc.gpsimd, i)
                    oo[s] = dma(nc.gpsimd, out_d[t * 128:(t + 1) * 128, :],
                                OUTW[s], sGs[s])
                pipe['out_tok'] = oo.get(0)
                pipe['out_tok2'] = oo.get(1)
            for tc in range(NTC):
                fill = fill_first and tc == 0
                tiles = [tc * TPT + j for j in range(TPT)]
                lts = []
                tp = {}
                ev = {}
                fills = [(j, k0) for j in range(TPT) for k0 in range(0, KCH, 4)]
                for fi, (j, k0) in enumerate(fills):
                    t = tiles[j]
                    s = j % 2
                    kw = min(4, KCH - k0)
                    b = fi % 2
                    if k0 == 0:
                        if j == 0:
                            wt(nc.scalar, stats_tok)
                            wt(nc.scalar, spars_last)
                        if j >= 2:
                            wt(nc.scalar, tp[(j - 2, 'last')])
                        i = SE.activation(
                            out=HRb[:, s * D:(s + 1) * D], in_=xt(t),
                            func=AF.Identity,
                            scale=SST[:, 2 * NT + t:2 * NT + t + 1],
                            bias=SST[:, 3 * NT + t:3 * NT + t + 1])
                        lts.append(i)
                    wt(nc.tensor, lts[j])
                    if fi >= 2:
                        wt(nc.tensor, ev[fi - 2])
                    elif pipe.get('ev2_banks'):
                        wt(nc.tensor, pipe['ev2_banks'].get(fi))
                    elif ev2_last:
                        wt(nc.tensor, ev2_last)
                    for kk in range(kw):
                        i = nc.tensor.transpose(
                            out=bankb(b)[:, kk * 128:(kk + 1) * 128],
                            in_=HRb[:, s * D + (k0 + kk) * 128: s * D + (k0 + kk + 1) * 128],
                            identity=IDN[:, :])
                    tok = inc(i, sP)
                    tp[fi] = tok
                    tp[(j, 'last')] = tok
                    wt(nc.vector, tok)
                    src = bankb(b)[:, 0:kw * 128].rearrange("p (k t) -> p k t", k=kw)
                    i = VE.tensor_copy(
                        out=LNF3[:, k0:k0 + kw, j * 128:(j + 1) * 128], in_=src)
                    ev[fi] = i
                tok_lnf = now(sV)
                emit_out()

                dma_tok = {}
                cast_tok = {}
                silu_tok = {}
                mult_tok = {}
                g_tok = {}
                u_tok = {}
                spars_tok = {}
                HW = KCH * 128 // 2
                gud = {}
                wbgu = pipe.setdefault('wbgu', {})
                for si in range(GC * 4):
                    gc, r = si // 4, si % 4
                    p = si % 2
                    if fill:
                        for it in range(4):
                            item = si * 4 + it
                            slot = item % 4
                            gu, hh = it // 2, it % 2
                            if item >= 4:
                                wt(nc.sync, cast_tok[item - 4])
                            else:
                                wt(nc.sync, pipe.get('wst_free'))
                            src = (wg_d if gu == 0 else wu_d)[r, gc]
                            dma_tok[item] = dma(nc.sync, WST[:, slot * HW:(slot + 1) * HW],
                                                src[:, hh * HW:(hh + 1) * HW], sWs[slot])
                        for it in range(4):
                            item = si * 4 + it
                            slot = item % 4
                            gu, hh = it // 2, it % 2
                            wt(nc.scalar, dma_tok[item])
                            if si >= 2 and it == 0:
                                wt(nc.scalar, u_tok[si - 2])
                            elif si < 2 and it == 0:
                                wt(nc.scalar, pipe.get('wgu_free'))
                            if hh == 0:
                                wt(nc.scalar, wbgu.get((si - 2, gu)))
                            dst = WGU[:, (gu * 2 + p) * KCH * 128:(gu * 2 + p + 1) * KCH * 128]
                            i = SE.activation(out=dst[:, hh * HW:(hh + 1) * HW],
                                                     in_=WST[:, slot * HW:(slot + 1) * HW],
                                                     func=AF.Copy)
                            cast_tok[item] = i
                            if hh == 1:
                                wt(nc.gpsimd, i)
                                wbgu[(si, gu)] = dma(
                                    nc.gpsimd, (wgc_d if gu == 0 else wuc_d)[r, gc],
                                    dst, sWBs[gu * 2 + p])
                    else:
                        for gu in range(2):
                            slot = gu * 2 + p
                            if si == 0 and gu == 0:
                                for tk in pipe.get('wbgu_done') or []:
                                    wt(nc.sync, tk)
                            if gu == 0:
                                if si >= 2:
                                    wt(nc.sync, u_tok[si - 2])
                                else:
                                    wt(nc.sync, pipe.get('wgu_free'))
                            dst = WGU[:, slot * KCH * 128:(slot + 1) * KCH * 128]
                            gud[(si, gu)] = dma(nc.sync, dst,
                                                (wgc_d if gu == 0 else wuc_d)[r, gc],
                                                sWs[slot])
                    if si == 0:
                        wt(nc.tensor, tok_lnf)
                    if fill:
                        wt(nc.tensor, cast_tok[si * 4 + 3])
                    else:
                        wt(nc.tensor, gud[(si, 0)])
                        wt(nc.tensor, gud[(si, 1)])
                    if si >= 2:
                        wt(nc.tensor, silu_tok[si - 2])
                        wt(nc.tensor, mult_tok[si - 2])
                    gw = WGU[:, p * KCH * 128:(p + 1) * KCH * 128]
                    uw = WGU[:, (2 + p) * KCH * 128:(2 + p + 1) * KCH * 128]
                    for k in range(KCH):
                        i = nc.tensor.matmul(out=bank(p)[:, 0:TC],
                                             lhsT=gw[:, k * 128:(k + 1) * 128],
                                             rhs=LNF[:, k * TC:(k + 1) * TC],
                                             start=(k == 0), stop=(k == KCH - 1))
                    g_tok[si] = inc(i, sP)
                    for k in range(KCH):
                        i = nc.tensor.matmul(out=bank(2 + p)[:, 0:TC],
                                             lhsT=uw[:, k * 128:(k + 1) * 128],
                                             rhs=LNF[:, k * TC:(k + 1) * TC],
                                             start=(k == 0), stop=(k == KCH - 1))
                    u_tok[si] = inc(i, sP)

                    def emit_nl(sj):
                        gcj, rj = sj // 4, sj % 4
                        pj = sj % 2
                        wt(nc.scalar, g_tok[sj])
                        if sj >= 4:
                            wt(nc.scalar, spars_tok[sj // 4 - 1])
                        i = SE.activation(out=HR[:, rj * TC:(rj + 1) * TC],
                                                 in_=bank(pj)[:, 0:TC], func=AF.Silu)
                        silu_tok[sj] = i
                        wt(nc.vector, silu_tok[sj])
                        wt(nc.vector, u_tok[sj])
                        i = VE.tensor_mul(HR[:, rj * TC:(rj + 1) * TC],
                                                 HR[:, rj * TC:(rj + 1) * TC],
                                                 bank(2 + pj)[:, 0:TC])
                        mult_tok[sj] = i
                        if rj == 3:
                            WDF = WDB.bitcast(F32)
                            LNFf = LNF.bitcast(F32)
                            final = (sj == GC * 4 - 1)
                            T = [LNFf[:, 0:TC] if final else PS[:, 4 * 512:4 * 512 + TC],
                                 WDF[:, 0:TC],
                                 LNFf[:, TC:2 * TC] if final else PS[:, 5 * 512:5 * 512 + TC],
                                 WDF[:, TC:2 * TC]]
                            for rr in range(4):
                                i = VE.tensor_mul(T[rr],
                                                  HR[:, rr * TC:(rr + 1) * TC],
                                                  HR[:, rr * TC:(rr + 1) * TC])
                            if tc == 0 and gcj == 0:
                                wt(nc.vector, ara_tok)
                            Hh = [HR[:, rr * TC:(rr + 1) * TC] for rr in range(4)]
                            S0 = SCR[:, 0:TC]; S1 = SCR[:, TC:2 * TC]; S2 = SCR[:, 2 * TC:3 * TC]
                            VE.tensor_tensor(S0, T[0], T[1], OP.max)
                            VE.tensor_tensor(S1, T[2], T[3], OP.max)
                            VE.tensor_tensor(S2, T[0], T[1], OP.min)
                            VE.tensor_tensor(S0, S0, S1, OP.min)
                            VE.tensor_tensor(S1, T[2], T[3], OP.min)
                            VE.tensor_tensor(S1, S2, S1, OP.max)
                            VE.tensor_tensor(S0, S0, S1, OP.max)
                            if write_out and (gcj + 1) * 4 > out_guard_ci:
                                wt(nc.vector, pipe['out_tok'])
                                wt(nc.vector, pipe.get('out_tok2'))
                            for rr in range(4):
                                ci = gcj * 4 + rr
                                VE.tensor_tensor(T[rr], T[rr], S0, OP.is_ge)
                                i = VE.tensor_mul(hsp(ci), Hh[rr], T[rr])
                            spars_tok[gcj] = i

                    if si >= 1:
                        emit_nl(si - 1)
                emit_nl(GC * 4 - 1)
                spars_last = spars_tok[GC - 1]
                if fill:
                    pipe['wst_free'] = cast_tok[GC * 16 - 1]
                pipe['wgu_free'] = u_tok[GC * 4 - 1]

                dd = {}
                dc = {}
                pd = {}
                wbwd = {}
                ev2_banks = {}
                ev2_tiles = {}
                DSL = min(4, max(2, (2 * KCH * 128) // FHW))
                for fh in range(FH):
                    for ci in range(ICH):
                        item = fh * ICH + ci
                        gc, r = ci // 4, ci % 4
                        if fill:
                            slot = item % DSL
                            if item >= DSL:
                                wt(nc.sync, dc[item - DSL])
                            else:
                                wt(nc.sync, pipe.get('wst_free'))
                            dd[item] = dma(nc.sync, WST[:, slot * FHW:(slot + 1) * FHW],
                                           wd_d[gc, r][:, fh * FHW:(fh + 1) * FHW], sWs[slot])
                            wt(nc.scalar, dd[item])
                            if item >= 2:
                                wt(nc.scalar, pd.get(item - 2))
                                wt(nc.scalar, wbwd[item - 2])
                            else:
                                wt(nc.scalar, pipe.get('wdb_free'))
                                wt(nc.scalar, spars_last)
                            wsl = WDB[:, (item % 2) * FHW:(item % 2 + 1) * FHW]
                            i = SE.activation(out=wsl,
                                                     in_=WST[:, slot * FHW:(slot + 1) * FHW],
                                                     func=AF.Copy)
                            dc[item] = i
                            wt(nc.gpsimd, i)
                            wbwd[item] = dma(nc.gpsimd,
                                             wdc_d[gc, r][:, fh * FHW:(fh + 1) * FHW],
                                             wsl, sWDs[item % 2])
                            wt(nc.tensor, dc[item])
                        else:
                            if item == 0:
                                for tk in pipe.get('wbwd_done') or []:
                                    wt(nc.sync, tk)
                                wt(nc.sync, pipe.get('wdb_free'))
                                wt(nc.sync, pipe.pop('wst_attn_free', None))
                            elif item >= 4:
                                wt(nc.sync, pd.get(item - 4))
                            dd[item] = dma(nc.sync,
                                           WSTb[:, (item % 4) * FHW:(item % 4 + 1) * FHW],
                                           wdc_d[gc, r][:, fh * FHW:(fh + 1) * FHW],
                                           sWs[item % 4])
                            wt(nc.tensor, dd[item])
                        if item == 0:
                            wt(nc.tensor, silu_tok[GC * 4 - 1])
                            wt(nc.tensor, mult_tok[GC * 4 - 1])
                        if fh == 0:
                            wt(nc.tensor, spars_tok[ci // 4])
                        elif ci == 0:
                            wt(nc.tensor, ev2_last)
                            wt(nc.tensor, spars_tok[GC - 1])
                        wdb = (WDB[:, (item % 2) * FHW:(item % 2 + 1) * FHW] if fill
                               else WSTb[:, (item % 4) * FHW:(item % 4 + 1) * FHW])
                        nsl = min(512, FHW)
                        GRP = FHW // nsl
                        for m in range(TPT):
                            for nn_ in range(GRP):
                                bb = (m * GRP + nn_) * 512
                                i = nc.tensor.matmul(
                                    out=PS[:, bb: bb + nsl],
                                    lhsT=hsp(ci)[:, m * 128:(m + 1) * 128],
                                    rhs=wdb[:, nn_ * nsl:(nn_ + 1) * nsl],
                                    start=(ci == 0), stop=(ci == ICH - 1))
                        pd[item] = inc(i, sP)
                    wt(nc.vector, pd[fh * ICH + ICH - 1])
                    nsl = min(512, FHW)
                    GRP = FHW // nsl
                    for m, t in enumerate(tiles):
                        for nn_ in range(GRP):
                            bb = (m * GRP + nn_) * 512
                            i = VE.scalar_tensor_tensor(
                                out=xt(t)[:, fh * FHW + nn_ * nsl: fh * FHW + (nn_ + 1) * nsl],
                                in0=PS[:, bb: bb + nsl],
                                scalar=0.5,
                                in1=xt(t)[:, fh * FHW + nn_ * nsl: fh * FHW + (nn_ + 1) * nsl],
                                op0=OP.mult, op1=OP.add)
                            if fh == FH - 1 and bb // 512 < 2:
                                ev2_banks[bb // 512] = i
                            if fh == FH - 1 and nn_ == GRP - 1:
                                ev2_tiles[m] = i
                                pipe.setdefault('tile_fin', {})[t] = i
                    ev2_last = i
                if fill:
                    pipe['wst_free'] = dc[FH * ICH - 1]
                    pipe['wbgu_done'] = [now(s_) for s_ in sWBs]
                    pipe['wbwd_done'] = [now(s_) for s_ in sWDs]
                pipe['wdb_free'] = pd[FH * ICH - 1]
                pipe['ev2_banks'] = ev2_banks

                if write_out:
                    pipe['pending_out'] = (tiles, dict(ev2_tiles))
            emit_out()
            return spars_last, ev2_last

        spars_last, ev2_last = emit_ffn(SSTA, tok_ln1, False, None, tok_x_loaded,
                                        wst_free_tok=tok_woe, fill_first=True)

        wt(nc.scalar, spars_last)
        SQ2 = LNF.bitcast(F32)[:, 0:D]
        ta = None
        for t in range(NTILE):
            wt(nc.scalar, pipe['tile_fin'][t])
            ta = SE.activation(out=SQ2, in_=xt(t), func=AF.Square,
                                          accum_out=SSTB[:, NT + t:NT + t + 1])
            VE.reduce_sum(SSTB[:, t:t + 1], xt(t), axis=AX.X)
        tok_ln2 = ln_finalize(SSTB, ta)

        groups = []
        remaining = KCH
        k0 = 0
        for buf, width in [(LNF, KCH * TC), (WGU, 4 * KCH * 128),
                           (WSTb, 4 * KCH * 128), (WDB, 2 * FHW),
                           (SCRb, 6 * TC), (HRb, 8 * TC)]:
            nch = min(width // TLOC, remaining)
            if nch > 0:
                groups.append((buf, k0, nch))
                k0 += nch
                remaining -= nch
            if remaining == 0:
                break
        assert remaining == 0, (remaining, KCH)
        ln2f_views = []
        for buf, gk0, nch in groups:
            for q in range(nch):
                ln2f_views.append(buf[:, q * TLOC:(q + 1) * TLOC])

        l2t = []
        for t in range(NTILE):
            if t == 0:
                wt(nc.vector, tok_ln2)
            i = VE.tensor_scalar(
                out=ARA[:, t * D:(t + 1) * D], in0=xt(t),
                scalar1=SSTB[:, 2 * NT + t:2 * NT + t + 1],
                scalar2=SSTB[:, 3 * NT + t:3 * NT + t + 1],
                op0=OP.mult, op1=OP.add)
            l2t.append(i)

        ev3 = {}
        tile_done_tok = {}
        fi = 0
        for t in range(NTILE):
            for k0 in range(0, KCH, 4):
                kw = min(4, KCH - k0)
                b = fi % 2
                wt(nc.tensor, l2t[t])
                if fi >= 2:
                    wt(nc.tensor, ev3[fi - 2])
                elif ev2_last:
                    wt(nc.tensor, ev2_last)
                for kk in range(kw):
                    i = nc.tensor.transpose(
                        out=bankb(b)[:, kk * 128:(kk + 1) * 128],
                        in_=ARA[:, t * D + (k0 + kk) * 128: t * D + (k0 + kk + 1) * 128],
                        identity=IDN[:, :])
                tok = inc(i, sP)
                wt(nc.vector, tok)
                for kk in range(kw):
                    i = VE.tensor_copy(
                        out=ln2f_views[k0 + kk][:, t * 128:(t + 1) * 128],
                        in_=bankb(b)[:, kk * 128:(kk + 1) * 128])
                ev3[fi] = i
                fi += 1
            tile_done_tok[t] = now(sV)
        tok_ln2f = now(sV)

        WKc = ARA[:, 0:KCH * 128]
        WVc = ARA[:, KCH * 128:2 * KCH * 128]
        WKVST = ARAf[:, KCH * 128:2 * KCH * 128]
        need_tiles = (4 * KCH * 128 + D - 1) // D
        wt(nc.sync, tile_done_tok[min(need_tiles - 1, NTILE - 1)])
        tok = dma(nc.sync, WKVST, wk_d[:, :], sWs[0])
        wt(nc.scalar, tok)
        tok_wk = SE.activation(out=WKc, in_=WKVST, func=AF.Copy)
        wt(nc.sync, tok_wk)
        tok = dma(nc.sync, WKVST, wv_d[:, :], sWs[0])
        wt(nc.scalar, tok)
        tok_wv = SE.activation(out=WVc, in_=WKVST, func=AF.Copy)

        KLOC = ARA[:, 2 * KCH * 128:2 * KCH * 128 + TLOC]
        VLOC = ARA[:, 2 * KCH * 128 + TLOC:2 * KCH * 128 + 2 * TLOC]
        wt(nc.tensor, tok_wv)
        wt(nc.tensor, tok_ln2f)
        for k in range(KCH):
            for nn_ in range(TLOC // 512):
                i = nc.tensor.matmul(out=PS[:, nn_ * 512:(nn_ + 1) * 512],
                                     lhsT=WKc[:, k * 128:(k + 1) * 128],
                                     rhs=ln2f_views[k][:, nn_ * 512:(nn_ + 1) * 512],
                                     start=(k == 0), stop=(k == KCH - 1))
        tok_kmm = inc(i, sP)
        wt(nc.vector, tok_kmm)
        tok_kloc = VE.tensor_scalar(out=KLOC, in0=PS[:, 0:TLOC],
                                    scalar1=BKB[:, 0:1], scalar2=None, op0=OP.add)
        wt(nc.gpsimd, tok_kloc)
        tok_ksend = dma(nc.gpsimd, kvs_d[0], KLOC, sG)
        wt(nc.gpsimd, tok_ksend)
        i = nc.gpsimd.collective_compute(
            "AllGather", OP.bypass, replica_groups=replica_groups,
            ins=[kvs_d[0].opt()], outs=[kvrk_d.opt()])
        tok_agk = inc(i, sC)
        tok_kv = None
        for rnd in range((NTILE + 3) // 4):
            ms = list(range(rnd * 4, min(rnd * 4 + 4, NTILE)))
            if rnd > 0:
                wt(nc.tensor, tok_kv)
            for k in range(KCH):
                for m in ms:
                    i = nc.tensor.matmul(
                        out=PS[:, 2048 + (m % 4) * 512: 2048 + (m % 4) * 512 + 128],
                        lhsT=ln2f_views[k][:, m * 128:(m + 1) * 128],
                        rhs=WVc[:, k * 128:(k + 1) * 128],
                        start=(k == 0), stop=False)
            for m in ms:
                i = nc.tensor.matmul(
                    out=PS[:, 2048 + (m % 4) * 512: 2048 + (m % 4) * 512 + 128],
                    lhsT=ONE[0:1, :], rhs=BVR[0:1, :],
                    start=False, stop=True)
            tok_vr = inc(i, sP)
            wt(nc.vector, tok_vr)
            for m in ms:
                tok_kv = VE.tensor_copy(
                    out=VLOC[:, m * 128:(m + 1) * 128],
                    in_=PS[:, 2048 + (m % 4) * 512: 2048 + (m % 4) * 512 + 128])

        wt(nc.gpsimd, tok_kv)
        tok_vsend = dma(nc.gpsimd, kvs_d[1], VLOC, sG)
        wt(nc.gpsimd, tok_vsend)
        i = nc.gpsimd.collective_compute(
            "AllGather", OP.bypass, replica_groups=replica_groups,
            ins=[kvs_d[1].opt()], outs=[kvrv_d.opt()])
        tok_agv = inc(i, sC)
        wt(nc.gpsimd, tok_agk)
        wt(nc.gpsimd, now(sP))
        dma(nc.gpsimd, KFB[:, 0:SFULL].rearrange("p (r t) -> p r t", r=NRANK),
            kvrk_d.rearrange("r p t -> p r t"), sG)
        tok_kfull = now(sG)
        wt(nc.gpsimd, tok_agv)
        dma(nc.gpsimd, VKB[:, 0:SFULL].rearrange("p (r t) -> p r t", r=NRANK),
            kvrv_d.rearrange("r p t -> p r t"), sG)
        tok_vkbfull = now(sG)

        qev = {}
        for m0 in range(0, NTILE, 4):
            b = (m0 // 4) % 2
            mw = min(4, NTILE - m0)
            if m0 == 0:
                wt(nc.tensor, tok_kv)
            if m0 >= 8:
                wt(nc.tensor, qev[m0 - 8])
            for mm in range(mw):
                i = nc.tensor.transpose(out=bankb(b)[:, mm * 128:(mm + 1) * 128],
                                        in_=VLOC[:, (m0 + mm) * 128:(m0 + mm + 1) * 128],
                                        identity=IDN[:, :])
            tok = inc(i, sP)
            wt(nc.vector, tok)
            i = VE.tensor_copy(out=QF[:, m0 * 128:(m0 + mw) * 128],
                                      in_=bankb(b)[:, 0:mw * 128])
            qev[m0] = i
        tok_qf = now(sV)

        SCb = ARAf[:, 0:SFULL]
        PB = ARA[:, 2 * SFULL:3 * SFULL]
        PT = ARA[:, 3 * SFULL:4 * SFULL]
        IZB = MXB[:, NKB + 8:NKB + 8 + NT]
        cps = {}
        af_tok = {}
        exp_toks = {}
        ptv_last = {}
        prev_ptv = {}
        ztr_tok = {}
        zrow_ev = {}

        def score_emit(q):
            for ib in range(NKB):
                b = ib % 4
                if q == 0 and ib == 0:
                    wt(nc.tensor, tok_qf)
                    wt(nc.tensor, tok_kfull)
                if ib >= 4:
                    wt(nc.tensor, cps[(q, ib - 4)])
                elif q > 0:
                    wt(nc.tensor, cps[(q - 1, NKB - 4 + ib)])
                i = nc.tensor.matmul(out=bank(b),
                                     lhsT=QF[:, q * 128:(q + 1) * 128],
                                     rhs=KFB[:, ib * 512:(ib + 1) * 512],
                                     start=True, stop=True)
                tok = inc(i, sP)
                if ib % 2 == 0:
                    wt(nc.scalar, tok)
                    if q == 0 and ib == 0:
                        wt(nc.scalar, tok_vsend)
                    cps[(q, ib)] = SE.activation(
                        out=SCb[:, ib * 512:(ib + 1) * 512],
                        in_=bank(b), func=AF.Copy)
                else:
                    wt(nc.vector, tok)
                    if q == 0 and ib == 1:
                        wt(nc.vector, tok_vsend)
                    cps[(q, ib)] = VE.tensor_copy(
                        out=SCb[:, ib * 512:(ib + 1) * 512], in_=bank(b))

        score_emit(0)
        for qt in range(NTILE):
            mb = NKB + 4 * (qt % 2)
            if qt > 0:
                wt(nc.scalar, ptv_last[qt - 1])
            wt(nc.scalar, cps[(qt, NKB - 1)])
            HS = SFULL // 2
            e0 = SE.activation(out=PB[:, 0:HS], in_=SCb[:, 0:HS], func=AF.Exp,
                               scale=1.0 / 128.0,
                               accum_out=MXB[:, mb + 1:mb + 2])
            e1 = SE.activation(out=PB[:, HS:SFULL], in_=SCb[:, HS:SFULL],
                               func=AF.Exp, scale=1.0 / 128.0,
                               accum_out=MXB[:, mb + 2:mb + 3])
            exp_toks[qt] = (e0, e1)
            wt(nc.vector, e1)
            VE.tensor_add(MXB[:, mb + 1:mb + 2], MXB[:, mb + 1:mb + 2],
                          MXB[:, mb + 2:mb + 3])
            VE.reciprocal(IZB[:, qt:qt + 1], MXB[:, mb + 1:mb + 2])
            if qt >= 2:
                wt(nc.vector, ztr_tok[qt - 2])
            tok_zcb = VE.tensor_copy(out=ZCB[:, qt % 2:qt % 2 + 1],
                                     in_=MXB[:, mb + 1:mb + 2])
            if qt + 1 < NTILE:
                score_emit(qt + 1)
            ptv = {}
            for ci0 in range(0, VCH, 4):
                b = 4 + (ci0 // 4) % 2
                if ci0 == 0:
                    wt(nc.tensor, exp_toks[qt][0])
                    wt(nc.tensor, prev_ptv.get(VCH - 8))
                if ci0 == 4:
                    wt(nc.tensor, prev_ptv.get(VCH - 4))
                if ci0 == VCH // 2:
                    wt(nc.tensor, exp_toks[qt][1])
                if ci0 >= 8:
                    wt(nc.tensor, ptv[ci0 - 8])
                for cc in range(4):
                    i = nc.tensor.transpose(
                        out=bankb(b)[:, cc * 128:(cc + 1) * 128],
                        in_=PB[:, (ci0 + cc) * 128:(ci0 + cc + 1) * 128],
                        identity=IDN[:, :])
                tok = inc(i, sP)
                ptv_last[qt] = tok
                wt(nc.vector, tok)
                i = VE.tensor_copy(out=PT[:, ci0 * 128:(ci0 + 4) * 128],
                                          in_=bankb(b)[:, 0:512])
                ptv[ci0] = i
            prev_ptv = ptv
            tok_pt = now(sV)
            wt(nc.tensor, tok_pt)
            if qt == 0:
                wt(nc.tensor, tok_vkbfull)
            if qt >= 2:
                wt(nc.tensor, af_tok[qt - 2])
            bav = 6 + (qt % 2)
            for ci in range(VCH):
                i = nc.tensor.matmul(out=PS[:, bav * 512:bav * 512 + 128],
                                     lhsT=VKB[:, ci * 128:(ci + 1) * 128],
                                     rhs=PT[:, ci * 128:(ci + 1) * 128],
                                     start=(ci == 0), stop=(ci == VCH - 1))
            tok_av = inc(i, sP)
            bav2 = 6 + ((qt + 1) % 2)
            wt(nc.tensor, tok_zcb)
            if qt >= 2:
                wt(nc.tensor, zrow_ev[qt - 2])
            i = nc.tensor.matmul(out=PS[0:1, bav2 * 512 + 128:bav2 * 512 + 256],
                                 lhsT=ZCB[:, qt % 2:qt % 2 + 1], rhs=IDN[:, :],
                                 start=True, stop=True)
            ztr_tok[qt] = inc(i, sP)
            wt(nc.vector, tok_av)
            af_tok[qt] = VE.tensor_copy(out=AFB[:, qt * 128:(qt + 1) * 128],
                                        in_=PS[:, bav * 512:bav * 512 + 128])
            wt(nc.vector, ztr_tok[qt])
            zrow_ev[qt] = VE.tensor_copy(
                out=ZROWb[0:1, qt * 128:(qt + 1) * 128],
                in_=PS[0:1, bav2 * 512 + 128:bav2 * 512 + 256])
        tok_af = now(sV)

        wo_mm = {}
        wo_ev = {}
        for t in range(NTILE):
            b0 = (t % 2) * 4
            if t == 0:
                wt(nc.tensor, tok_af)
            elif t >= 2:
                wt(nc.tensor, wo_ev[t - 2])
            for nn_ in range((D + 511) // 512):
                ww = min(512, D - nn_ * 512)
                i = nc.tensor.matmul(out=PS[:, b0 * 512 + nn_ * 512: b0 * 512 + nn_ * 512 + ww],
                                     lhsT=AFB[:, t * 128:(t + 1) * 128],
                                     rhs=WOE[:, nn_ * 512:nn_ * 512 + ww],
                                     start=True, stop=False)
                i = nc.tensor.matmul(out=PS[:, b0 * 512 + nn_ * 512: b0 * 512 + nn_ * 512 + ww],
                                     lhsT=ZROWb[0:1, t * 128:(t + 1) * 128],
                                     rhs=BOR[0:1, nn_ * 512:nn_ * 512 + ww],
                                     start=False, stop=True)
            wo_mm[t] = inc(i, sP)
            wt(nc.vector, wo_mm[t])
            i = VE.scalar_tensor_tensor(out=xt(t), in0=PS[:, b0 * 512:b0 * 512 + D],
                                        scalar=IZB[:, t:t + 1], in1=xt(t),
                                        op0=OP.mult, op1=OP.add)
            wo_ev[t] = i
        tok_x2 = now(sV)

        SQ3 = ARAf[:, 0:D]
        ta = None
        for t in range(NTILE):
            wt(nc.scalar, wo_ev[t])
            ta = SE.activation(out=SQ3, in_=xt(t), func=AF.Square,
                                          accum_out=SSTA[:, NT + t:NT + t + 1])
            VE.reduce_sum(SSTA[:, t:t + 1], xt(t), axis=AX.X)
        tok_ln3 = ln_finalize(SSTA, ta)

        wt(nc.sync, ztr_tok[NTILE - 1])
        pipe['wgu_free'] = ztr_tok[NTILE - 1]
        pipe['wst_attn_free'] = wo_mm[NTILE - 1]
        emit_ffn(SSTA, tok_ln3, True, (sV, tok_x2[1]), (sV, tok_x2[1]))

        for s_ in sGs:
            nc.gpsimd.wait_ge(s_, cnt[id(s_)])
        nc.gpsimd.wait_ge(sG, cnt[id(sG)])

    return nc


def shard_inputs(cfg, L0_attn_q, L0_attn_Wk, L0_attn_bk, L0_attn_Wv, L0_attn_bv,
                 L0_attn_Wo, L0_attn_bo, L0_ffn_w_gate, L0_ffn_w_up, L0_ffn_w_down):
    D = cfg["D"]; I = cfg["I"]; TLOC = cfg["TLOC"]; n_cores = cfg["n_cores"]
    KCH = D // 128; GC = I // 512

    x = np.ascontiguousarray(np.asarray(L0_attn_q, dtype=np.float32).reshape(-1, D))

    def layout_gu(w):
        w = np.asarray(w, dtype=np.float32)
        w5 = w.reshape(KCH, 128, GC, 128, 4)
        return np.ascontiguousarray(w5.transpose(4, 2, 1, 0, 3).reshape(4, GC, 128, KCH * 128))

    def layout_wd(w):
        w = np.asarray(w, dtype=np.float32)
        w4 = w.reshape(GC, 128, 4, D)
        return np.ascontiguousarray(w4.transpose(0, 2, 1, 3))

    def layout_wkv(w):
        w = np.asarray(w, dtype=np.float32)
        return np.ascontiguousarray(
            w.reshape(KCH, 128, 128).transpose(1, 0, 2).reshape(128, KCH * 128))

    base = {
        "wg": layout_gu(L0_ffn_w_gate),
        "wu": layout_gu(L0_ffn_w_up),
        "wd": layout_wd(L0_ffn_w_down),
        "wk": layout_wkv(L0_attn_Wk),
        "wv": layout_wkv(L0_attn_Wv),
        "wo": np.ascontiguousarray(np.asarray(L0_attn_Wo, dtype=np.float32).reshape(KCH, 128, D)),
        "bk": np.ascontiguousarray(np.asarray(L0_attn_bk, dtype=np.float32).reshape(128, 1)),
        "bv": np.ascontiguousarray(np.asarray(L0_attn_bv, dtype=np.float32).reshape(1, 128)),
        "bo": np.ascontiguousarray(np.asarray(L0_attn_bo, dtype=np.float32).reshape(1, D)),
        "ident": np.eye(128, dtype=BF),
    }
    in_maps = []
    for c in range(n_cores):
        m = dict(base)
        m["x"] = np.ascontiguousarray(x[c * TLOC:(c + 1) * TLOC])
        in_maps.append(m)
    return in_maps


def kernel(L0_attn_q, L0_attn_Wq, L0_attn_bq, L0_attn_Wk, L0_attn_bk,
           L0_attn_Wv, L0_attn_bv, L0_attn_Wo, L0_attn_bo,
           L0_ffn_w_gate, L0_ffn_w_up, L0_ffn_w_down):
    cfg = full_cfg()
    in_maps = shard_inputs(cfg, L0_attn_q, L0_attn_Wk, L0_attn_bk, L0_attn_Wv,
                           L0_attn_bv, L0_attn_Wo, L0_attn_bo,
                           L0_ffn_w_gate, L0_ffn_w_up, L0_ffn_w_down)
    nc = build_nc(cfg)
    res = run_bass_kernel_spmd(nc, in_maps, core_ids=list(range(cfg["n_cores"])))
    out = np.concatenate([res.results[c]["out"] for c in range(cfg["n_cores"])], axis=0)
    return out.reshape(2, 8192, cfg["D"]).astype(np.float32)


# version 09 (final)
# speedup vs baseline: 1.3036x; 1.0083x over previous
import numpy as np
import ml_dtypes
from contextlib import ExitStack
import concourse.bass as bass
import concourse.mybir as mybir
from concourse.bass_utils import run_bass_kernel_spmd

F32 = mybir.dt.float32
BF16 = mybir.dt.bfloat16
AF = mybir.ActivationFunctionType
OP = mybir.AluOpType
AX = mybir.AxisListType
BF = ml_dtypes.bfloat16


def full_cfg():
    return dict(D=2048, I=8192, TLOC=2048, TC=512, NRANK=4, n_cores=8, eps=1e-5)


def build_nc(cfg):
    D = cfg["D"]; I = cfg["I"]; TLOC = cfg["TLOC"]; TC = cfg["TC"]
    NRANK = cfg["NRANK"]; n_cores = cfg["n_cores"]; eps = cfg["eps"]
    KCH = D // 128
    ICH = I // 128
    GC = I // 512
    NTILE = TLOC // 128
    NTC = TLOC // TC
    TPT = TC // 128
    SFULL = NRANK * TLOC
    NKB = SFULL // 512
    VCH = SFULL // 128
    FH = max(1, (TPT * D * 4) // 16384)
    FHW = D // FH
    NT = NTILE
    ARAW = max(ICH * TC, 4 * SFULL, NTILE * D, 8 * D)

    nc = bass.Bass(trn_type="TRN2", target_bir_lowering=False)

    x_d = nc.dram_tensor("x", [TLOC, D], F32, kind="ExternalInput").ap()
    wg_d = nc.dram_tensor("wg", [4, GC, 128, KCH * 128], F32, kind="ExternalInput").ap()
    wu_d = nc.dram_tensor("wu", [4, GC, 128, KCH * 128], F32, kind="ExternalInput").ap()
    wd_d = nc.dram_tensor("wd", [GC, 4, 128, D], F32, kind="ExternalInput").ap()
    wk_d = nc.dram_tensor("wk", [128, KCH * 128], F32, kind="ExternalInput").ap()
    wv_d = nc.dram_tensor("wv", [128, KCH * 128], F32, kind="ExternalInput").ap()
    wo_d = nc.dram_tensor("wo", [KCH, 128, D], F32, kind="ExternalInput").ap()
    bk_d = nc.dram_tensor("bk", [128, 1], F32, kind="ExternalInput").ap()
    bv_d = nc.dram_tensor("bv", [1, 128], F32, kind="ExternalInput").ap()
    bo_d = nc.dram_tensor("bo", [1, D], F32, kind="ExternalInput").ap()
    id_d = nc.dram_tensor("ident", [128, 128], BF16, kind="ExternalInput").ap()
    out_d = nc.dram_tensor("out", [TLOC, D], F32, kind="ExternalOutput").ap()
    kvs_d = nc.dram_tensor("kv_send", [2, 128, TLOC], BF16).ap()
    kvrk_d = nc.dram_tensor("k_recv", [NRANK, 128, TLOC], BF16).ap()
    kvrv_d = nc.dram_tensor("v_recv", [NRANK, 128, TLOC], BF16).ap()
    wgc_d = nc.dram_tensor("wg_c", [4, GC, 128, KCH * 128], BF16).ap()
    wuc_d = nc.dram_tensor("wu_c", [4, GC, 128, KCH * 128], BF16).ap()
    wdc_d = nc.dram_tensor("wd_c", [GC, 4, 128, D], BF16).ap()

    replica_groups = [[g * NRANK + i for i in range(NRANK)]
                      for g in range(n_cores // NRANK)]

    st = ExitStack()
    with st:
        X = st.enter_context(nc.sbuf_tensor("Xs", [128, NTILE * D], BF16)).ap()
        ARA = st.enter_context(nc.sbuf_tensor("ARA", [128, ARAW], BF16)).ap()
        LNF = st.enter_context(nc.sbuf_tensor("LNF", [128, KCH * TC], BF16)).ap()
        WGU = st.enter_context(nc.sbuf_tensor("WGU", [128, 4 * KCH * 128], BF16)).ap()
        WST = st.enter_context(nc.sbuf_tensor("WST", [128, 2 * KCH * 128], F32)).ap()
        WDB = st.enter_context(nc.sbuf_tensor("WDB", [128, max(2 * FHW, 4 * TC)], BF16)).ap()
        HR = st.enter_context(nc.sbuf_tensor("HRs", [128, 4 * TC], F32)).ap()
        SCR = st.enter_context(nc.sbuf_tensor("SCR", [128, 3 * TC], F32)).ap()
        WOE = st.enter_context(nc.sbuf_tensor("WOE", [128, D], BF16)).ap()
        SSTA = st.enter_context(nc.sbuf_tensor("SSTA", [128, 4 * NT], F32)).ap()
        SSTB = st.enter_context(nc.sbuf_tensor("SSTB", [128, 4 * NT], F32)).ap()
        MXB = st.enter_context(nc.sbuf_tensor("MXB", [128, NKB + 8 + NT], F32)).ap()
        ZCB = st.enter_context(nc.sbuf_tensor("ZCB", [128, 2], BF16)).ap()
        ZROWb = st.enter_context(nc.sbuf_tensor("ZROW", [1, TLOC], BF16)).ap()
        IDN = st.enter_context(nc.sbuf_tensor("IDN", [128, 128], BF16)).ap()
        ONE = st.enter_context(nc.sbuf_tensor("ONEs", [1, 128], BF16)).ap()
        BKB = st.enter_context(nc.sbuf_tensor("BKB", [128, 1], F32)).ap()
        BVR = st.enter_context(nc.sbuf_tensor("BVR", [1, 128], BF16)).ap()
        BOR = st.enter_context(nc.sbuf_tensor("BOR", [1, D], BF16)).ap()
        PS = st.enter_context(nc.psum_tensor("PS", [128, 4096], F32)).ap()

        sA = st.enter_context(nc.semaphore("sA"))
        sV = st.enter_context(nc.semaphore("sV"))
        sP = st.enter_context(nc.semaphore("sP"))
        sC = st.enter_context(nc.semaphore("sC"))
        sG = st.enter_context(nc.semaphore("sG"))
        sWs = [st.enter_context(nc.semaphore(f"sW{k}")) for k in range(4)]
        sGs = [st.enter_context(nc.semaphore(f"sG{k}")) for k in range(2)]
        sWBs = [st.enter_context(nc.semaphore(f"sWB{k}")) for k in range(4)]
        sWDs = [st.enter_context(nc.semaphore(f"sWD{k}")) for k in range(2)]
        cnt = {}
        for s_ in [sA, sV, sP, sC, sG] + sWs + sGs + sWBs + sWDs:
            cnt[id(s_)] = 0

        def inc(instr, sem, amt=1):
            instr.then_inc(sem, amt)
            cnt[id(sem)] += amt
            return (sem, cnt[id(sem)])

        def now(sem):
            return (sem, cnt[id(sem)])

        def wt(eng, tok):
            if isinstance(eng, Eng):
                eng = eng.eng
            if tok and tok[1] > 0:
                eng.wait_ge(tok[0], tok[1])

        def dma(eng, out, in_, sem):
            if cnt[id(sem)] > 0:
                eng.wait_ge(sem, cnt[id(sem)])
            return inc(eng.dma_start(out=out, in_=in_), sem, 16)

        class Eng:
            def __init__(self, eng, sem):
                self.eng = eng
                self.sem = sem

            def __getattr__(self, name):
                if name in ("wait_ge",):
                    return getattr(self.eng, name)
                fn = getattr(self.eng, name)

                def call(*a, **k):
                    if cnt[id(self.sem)] > 0:
                        self.eng.wait_ge(self.sem, cnt[id(self.sem)])
                    return inc(fn(*a, **k), self.sem)
                return call

        VE = Eng(nc.vector, sV)
        SE = Eng(nc.scalar, sA)

        ARAf = ARA.bitcast(F32)
        WSTb = WST.bitcast(BF16)
        HRb = HR.bitcast(BF16)
        SCRb = SCR.bitcast(BF16)
        PSb = PS.bitcast(BF16)

        def xt(t):
            return X[:, t * D:(t + 1) * D]

        def hsp(ci):
            return ARA[:, ci * TC:(ci + 1) * TC]

        def bank(b):
            return PS[:, b * 512:(b + 1) * 512]

        def bankb(b):
            return PSb[:, b * 1024:(b + 1) * 1024]

        if KCH * TC >= SFULL and 4 * KCH * 128 >= SFULL:
            KFB = LNF[:, 0:SFULL]
            VKB = WGU[:, 0:SFULL]
        else:
            KFB = st.enter_context(nc.sbuf_tensor("KFBt", [128, SFULL], BF16)).ap()
            VKB = st.enter_context(nc.sbuf_tensor("VKBt", [128, SFULL], BF16)).ap()
        QF = WSTb[:, 0:TLOC]
        AFB = WSTb[:, TLOC:2 * TLOC]
        OUTW = [ARAf[:, (ARAW // 2) - 2 * D + s * D: (ARAW // 2) - 2 * D + (s + 1) * D]
                for s in range(2)]
        out_guard_ci = (ARAW - 4 * D) // TC

        dma(nc.sync, IDN[:, :], id_d[:, :], sWs[0])
        dma(nc.sync, BKB[:, :], bk_d[:, :], sWs[1])
        dma(nc.sync, WST[0:1, 0:128], bv_d[:, :], sWs[2])
        dma(nc.sync, WST[0:1, 128:128 + D], bo_d[:, :], sWs[3])
        tok_const = now(sWs[3])

        for s_ in sWs:
            VE.wait_ge(s_, cnt[id(s_)])
        VE.memset(ONE[:, :], 1.0)
        VE.tensor_copy(out=BVR[:, :], in_=WST[0:1, 0:128])
        tok_smallcast = VE.tensor_copy(out=BOR[:, :], in_=WST[0:1, 128:128 + D])

        WOACC = ARAf[:, 0:D]
        wo_dma = []
        wo_add = []
        wt(nc.sync, tok_smallcast)
        for c in range(KCH):
            s = c % 2
            if c >= 2:
                wt(nc.sync, wo_add[c - 2])
            wo_dma.append(dma(nc.sync, WST[:, s * D:(s + 1) * D], wo_d[c], sWs[s]))
            wt(nc.vector, wo_dma[c])
            if c == 0:
                i = VE.tensor_copy(out=WOACC, in_=WST[:, 0:D])
            else:
                i = VE.tensor_add(WOACC, WOACC, WST[:, s * D:(s + 1) * D])
            wo_add.append(i)
        tok_woe = VE.tensor_copy(out=WOE[:, :], in_=WOACC)

        def xstg(s):
            return ARAf[:, (1 + s) * D:(2 + s) * D]

        SQS = ARAf[:, 3 * D:4 * D]
        xcast = []
        for t in range(NTILE):
            s = t % 2
            if t >= 2:
                wt(nc.gpsimd, xcast[2 * (t - 2)])
            tok = dma(nc.gpsimd, xstg(s), x_d[t * 128:(t + 1) * 128, :], sGs[s])
            wt(nc.scalar, tok)
            if t == 0:
                wt(nc.scalar, tok_woe)
            i = SE.activation(out=xt(t), in_=xstg(s), func=AF.Copy,
                                     accum_out=SSTA[:, t:t + 1])
            xcast.append(i)
            i = SE.activation(out=SQS, in_=xt(t), func=AF.Square,
                                     accum_out=SSTA[:, NT + t:NT + t + 1])
            xcast.append(i)
        tok_x_loaded = now(sA)

        def ln_finalize(SST, wait_tok):
            mean = SST[:, 0:NT]; ssq = SST[:, NT:2 * NT]
            rr = SST[:, 2 * NT:3 * NT]; nmr = SST[:, 3 * NT:4 * NT]
            wt(nc.vector, wait_tok)
            VE.tensor_scalar_mul(mean, mean, 1.0 / D)
            VE.tensor_scalar_mul(ssq, ssq, 1.0 / D)
            VE.tensor_mul(rr, mean, mean)
            VE.tensor_sub(ssq, ssq, rr)
            VE.tensor_scalar_add(ssq, ssq, eps)
            tv = VE.reciprocal(rr, ssq)
            wt(nc.scalar, tv)
            ta = SE.activation(out=rr, in_=rr, func=AF.Sqrt)
            wt(nc.vector, ta)
            VE.tensor_mul(nmr, mean, rr)
            return VE.tensor_scalar_mul(nmr, nmr, -1.0)

        tok_ln1 = ln_finalize(SSTA, tok_x_loaded)

        pipe = dict(out_tok=None)

        def emit_ffn(SST, stats_tok, write_out, hr_free_tok, ara_free_tok,
                     wst_free_tok=None, fill_first=False):
            LNF3 = LNF.rearrange("p (k t) -> p k t", k=KCH)
            spars_last = hr_free_tok
            ara_tok = ara_free_tok
            ev2_last = None
            pipe['ev2_banks'] = None
            if wst_free_tok:
                pipe['wst_free'] = wst_free_tok

            def emit_out():
                po = pipe.pop('pending_out', None)
                if not po:
                    return
                otiles, etiles = po
                oo = pipe.setdefault('oo_slot', {})
                for m, t in enumerate(otiles):
                    s = m % 2
                    wt(nc.scalar, etiles[m])
                    wt(nc.scalar, oo.get(s))
                    i = SE.activation(out=OUTW[s], in_=xt(t), func=AF.Copy)
                    wt(nc.gpsimd, i)
                    oo[s] = dma(nc.gpsimd, out_d[t * 128:(t + 1) * 128, :],
                                OUTW[s], sGs[s])
                pipe['out_tok'] = oo.get(0)
                pipe['out_tok2'] = oo.get(1)
            for tc in range(NTC):
                fill = fill_first and tc == 0
                tiles = [tc * TPT + j for j in range(TPT)]
                lts = []
                tp = {}
                ev = {}
                fills = [(j, k0) for j in range(TPT) for k0 in range(0, KCH, 4)]
                for fi, (j, k0) in enumerate(fills):
                    t = tiles[j]
                    s = j % 2
                    kw = min(4, KCH - k0)
                    b = fi % 2
                    if k0 == 0:
                        if j == 0:
                            wt(nc.scalar, stats_tok)
                            wt(nc.scalar, spars_last)
                        if j >= 2:
                            wt(nc.scalar, tp[(j - 2, 'last')])
                        i = SE.activation(
                            out=HRb[:, s * D:(s + 1) * D], in_=xt(t),
                            func=AF.Identity,
                            scale=SST[:, 2 * NT + t:2 * NT + t + 1],
                            bias=SST[:, 3 * NT + t:3 * NT + t + 1])
                        lts.append(i)
                    wt(nc.tensor, lts[j])
                    if fi >= 2:
                        wt(nc.tensor, ev[fi - 2])
                    elif pipe.get('ev2_banks'):
                        wt(nc.tensor, pipe['ev2_banks'].get(fi))
                    elif ev2_last:
                        wt(nc.tensor, ev2_last)
                    for kk in range(kw):
                        i = nc.tensor.transpose(
                            out=bankb(b)[:, kk * 128:(kk + 1) * 128],
                            in_=HRb[:, s * D + (k0 + kk) * 128: s * D + (k0 + kk + 1) * 128],
                            identity=IDN[:, :])
                    tok = inc(i, sP)
                    tp[fi] = tok
                    tp[(j, 'last')] = tok
                    wt(nc.vector, tok)
                    src = bankb(b)[:, 0:kw * 128].rearrange("p (k t) -> p k t", k=kw)
                    i = VE.tensor_copy(
                        out=LNF3[:, k0:k0 + kw, j * 128:(j + 1) * 128], in_=src)
                    ev[fi] = i
                tok_lnf = now(sV)
                emit_out()

                dma_tok = {}
                cast_tok = {}
                silu_tok = {}
                mult_tok = {}
                g_tok = {}
                u_tok = {}
                spars_tok = {}
                HW = KCH * 128 // 2
                gud = {}
                wbgu = pipe.setdefault('wbgu', {})
                for si in range(GC * 4):
                    gc, r = si // 4, si % 4
                    p = si % 2
                    if fill:
                        for it in range(4):
                            item = si * 4 + it
                            slot = item % 4
                            gu, hh = it // 2, it % 2
                            if item >= 4:
                                wt(nc.sync, cast_tok[item - 4])
                            else:
                                wt(nc.sync, pipe.get('wst_free'))
                            src = (wg_d if gu == 0 else wu_d)[r, gc]
                            dma_tok[item] = dma(nc.sync, WST[:, slot * HW:(slot + 1) * HW],
                                                src[:, hh * HW:(hh + 1) * HW], sWs[slot])
                        for it in range(4):
                            item = si * 4 + it
                            slot = item % 4
                            gu, hh = it // 2, it % 2
                            wt(nc.scalar, dma_tok[item])
                            if si >= 2 and it == 0:
                                wt(nc.scalar, u_tok[si - 2])
                            elif si < 2 and it == 0:
                                wt(nc.scalar, pipe.get('wgu_free'))
                            if hh == 0:
                                wt(nc.scalar, wbgu.get((si - 2, gu)))
                            dst = WGU[:, (gu * 2 + p) * KCH * 128:(gu * 2 + p + 1) * KCH * 128]
                            i = SE.activation(out=dst[:, hh * HW:(hh + 1) * HW],
                                                     in_=WST[:, slot * HW:(slot + 1) * HW],
                                                     func=AF.Copy)
                            cast_tok[item] = i
                            if hh == 1:
                                wt(nc.gpsimd, i)
                                wbgu[(si, gu)] = dma(
                                    nc.gpsimd, (wgc_d if gu == 0 else wuc_d)[r, gc],
                                    dst, sWBs[gu * 2 + p])
                    else:
                        for gu in range(2):
                            slot = gu * 2 + p
                            if si == 0 and gu == 0:
                                for tk in pipe.get('wbgu_done') or []:
                                    wt(nc.sync, tk)
                            if gu == 0:
                                if si >= 2:
                                    wt(nc.sync, u_tok[si - 2])
                                else:
                                    wt(nc.sync, pipe.get('wgu_free'))
                            dst = WGU[:, slot * KCH * 128:(slot + 1) * KCH * 128]
                            gud[(si, gu)] = dma(nc.sync, dst,
                                                (wgc_d if gu == 0 else wuc_d)[r, gc],
                                                sWs[slot])
                    if si == 0:
                        wt(nc.tensor, tok_lnf)
                    if fill:
                        wt(nc.tensor, cast_tok[si * 4 + 3])
                    else:
                        wt(nc.tensor, gud[(si, 0)])
                        wt(nc.tensor, gud[(si, 1)])
                    if si >= 2:
                        wt(nc.tensor, silu_tok[si - 2])
                        wt(nc.tensor, mult_tok[si - 2])
                    gw = WGU[:, p * KCH * 128:(p + 1) * KCH * 128]
                    uw = WGU[:, (2 + p) * KCH * 128:(2 + p + 1) * KCH * 128]
                    for k in range(KCH):
                        i = nc.tensor.matmul(out=bank(p)[:, 0:TC],
                                             lhsT=gw[:, k * 128:(k + 1) * 128],
                                             rhs=LNF[:, k * TC:(k + 1) * TC],
                                             start=(k == 0), stop=(k == KCH - 1))
                    g_tok[si] = inc(i, sP)
                    for k in range(KCH):
                        i = nc.tensor.matmul(out=bank(2 + p)[:, 0:TC],
                                             lhsT=uw[:, k * 128:(k + 1) * 128],
                                             rhs=LNF[:, k * TC:(k + 1) * TC],
                                             start=(k == 0), stop=(k == KCH - 1))
                    u_tok[si] = inc(i, sP)

                    def emit_nl(sj):
                        gcj, rj = sj // 4, sj % 4
                        pj = sj % 2
                        wt(nc.scalar, g_tok[sj])
                        if sj >= 4:
                            wt(nc.scalar, spars_tok[sj // 4 - 1])
                        i = SE.activation(out=HR[:, rj * TC:(rj + 1) * TC],
                                                 in_=bank(pj)[:, 0:TC], func=AF.Silu)
                        silu_tok[sj] = i
                        wt(nc.vector, silu_tok[sj])
                        wt(nc.vector, u_tok[sj])
                        i = VE.tensor_mul(HR[:, rj * TC:(rj + 1) * TC],
                                                 HR[:, rj * TC:(rj + 1) * TC],
                                                 bank(2 + pj)[:, 0:TC])
                        mult_tok[sj] = i
                        if rj == 3:
                            WDF = WDB.bitcast(F32)
                            LNFf = LNF.bitcast(F32)
                            final = (sj == GC * 4 - 1)
                            T = [LNFf[:, 0:TC] if final else PS[:, 4 * 512:4 * 512 + TC],
                                 WDF[:, 0:TC],
                                 LNFf[:, TC:2 * TC] if final else PS[:, 5 * 512:5 * 512 + TC],
                                 WDF[:, TC:2 * TC]]
                            for rr in range(4):
                                i = VE.tensor_mul(T[rr],
                                                  HR[:, rr * TC:(rr + 1) * TC],
                                                  HR[:, rr * TC:(rr + 1) * TC])
                            if tc == 0 and gcj == 0:
                                wt(nc.vector, ara_tok)
                            Hh = [HR[:, rr * TC:(rr + 1) * TC] for rr in range(4)]
                            S0 = SCR[:, 0:TC]; S1 = SCR[:, TC:2 * TC]; S2 = SCR[:, 2 * TC:3 * TC]
                            VE.tensor_tensor(S0, T[0], T[1], OP.max)
                            VE.tensor_tensor(S1, T[2], T[3], OP.max)
                            VE.tensor_tensor(S2, T[0], T[1], OP.min)
                            VE.tensor_tensor(S0, S0, S1, OP.min)
                            VE.tensor_tensor(S1, T[2], T[3], OP.min)
                            VE.tensor_tensor(S1, S2, S1, OP.max)
                            VE.tensor_tensor(S0, S0, S1, OP.max)
                            if write_out and (gcj + 1) * 4 > out_guard_ci:
                                wt(nc.vector, pipe['out_tok'])
                                wt(nc.vector, pipe.get('out_tok2'))
                            for rr in range(4):
                                ci = gcj * 4 + rr
                                VE.tensor_tensor(T[rr], T[rr], S0, OP.is_ge)
                                i = VE.tensor_mul(hsp(ci), Hh[rr], T[rr])
                            spars_tok[gcj] = i

                    if si >= 1:
                        emit_nl(si - 1)
                emit_nl(GC * 4 - 1)
                spars_last = spars_tok[GC - 1]
                if fill:
                    pipe['wst_free'] = cast_tok[GC * 16 - 1]
                pipe['wgu_free'] = u_tok[GC * 4 - 1]

                dd = {}
                dc = {}
                pd = {}
                wbwd = {}
                ev2_banks = {}
                ev2_tiles = {}
                DSL = min(4, max(2, (2 * KCH * 128) // FHW))
                for fh in range(FH):
                    for ci in range(ICH):
                        item = fh * ICH + ci
                        gc, r = ci // 4, ci % 4
                        if fill:
                            slot = item % DSL
                            if item >= DSL:
                                wt(nc.sync, dc[item - DSL])
                            else:
                                wt(nc.sync, pipe.get('wst_free'))
                            dd[item] = dma(nc.sync, WST[:, slot * FHW:(slot + 1) * FHW],
                                           wd_d[gc, r][:, fh * FHW:(fh + 1) * FHW], sWs[slot])
                            wt(nc.scalar, dd[item])
                            if item >= 2:
                                wt(nc.scalar, pd.get(item - 2))
                                wt(nc.scalar, wbwd[item - 2])
                            else:
                                wt(nc.scalar, pipe.get('wdb_free'))
                                wt(nc.scalar, spars_last)
                            wsl = WDB[:, (item % 2) * FHW:(item % 2 + 1) * FHW]
                            i = SE.activation(out=wsl,
                                                     in_=WST[:, slot * FHW:(slot + 1) * FHW],
                                                     func=AF.Copy)
                            dc[item] = i
                            wt(nc.gpsimd, i)
                            wbwd[item] = dma(nc.gpsimd,
                                             wdc_d[gc, r][:, fh * FHW:(fh + 1) * FHW],
                                             wsl, sWDs[item % 2])
                            wt(nc.tensor, dc[item])
                        else:
                            if item == 0:
                                for tk in pipe.get('wbwd_done') or []:
                                    wt(nc.sync, tk)
                                wt(nc.sync, pipe.get('wdb_free'))
                                wt(nc.sync, pipe.pop('wst_attn_free', None))
                            elif item >= 4:
                                wt(nc.sync, pd.get(item - 4))
                            dd[item] = dma(nc.sync,
                                           WSTb[:, (item % 4) * FHW:(item % 4 + 1) * FHW],
                                           wdc_d[gc, r][:, fh * FHW:(fh + 1) * FHW],
                                           sWs[item % 4])
                            wt(nc.tensor, dd[item])
                        if item == 0:
                            wt(nc.tensor, silu_tok[GC * 4 - 1])
                            wt(nc.tensor, mult_tok[GC * 4 - 1])
                        if fh == 0:
                            wt(nc.tensor, spars_tok[ci // 4])
                        elif ci == 0:
                            wt(nc.tensor, ev2_last)
                            wt(nc.tensor, spars_tok[GC - 1])
                        wdb = (WDB[:, (item % 2) * FHW:(item % 2 + 1) * FHW] if fill
                               else WSTb[:, (item % 4) * FHW:(item % 4 + 1) * FHW])
                        nsl = min(512, FHW)
                        GRP = FHW // nsl
                        for m in range(TPT):
                            for nn_ in range(GRP):
                                bb = (m * GRP + nn_) * 512
                                i = nc.tensor.matmul(
                                    out=PS[:, bb: bb + nsl],
                                    lhsT=hsp(ci)[:, m * 128:(m + 1) * 128],
                                    rhs=wdb[:, nn_ * nsl:(nn_ + 1) * nsl],
                                    start=(ci == 0), stop=(ci == ICH - 1))
                        pd[item] = inc(i, sP)
                    wt(nc.vector, pd[fh * ICH + ICH - 1])
                    nsl = min(512, FHW)
                    GRP = FHW // nsl
                    for m, t in enumerate(tiles):
                        for nn_ in range(GRP):
                            bb = (m * GRP + nn_) * 512
                            i = VE.scalar_tensor_tensor(
                                out=xt(t)[:, fh * FHW + nn_ * nsl: fh * FHW + (nn_ + 1) * nsl],
                                in0=PS[:, bb: bb + nsl],
                                scalar=0.5,
                                in1=xt(t)[:, fh * FHW + nn_ * nsl: fh * FHW + (nn_ + 1) * nsl],
                                op0=OP.mult, op1=OP.add,
                                accum_out=MXB[:, m * 4 + fh * GRP + nn_:
                                              m * 4 + fh * GRP + nn_ + 1])
                            if fh == FH - 1 and bb // 512 < 2:
                                ev2_banks[bb // 512] = i
                            if fh == FH - 1 and nn_ == GRP - 1:
                                ev2_tiles[m] = i
                                pipe.setdefault('tile_fin', {})[t] = i
                                if not write_out:
                                    c = m * 4
                                    VE.tensor_add(MXB[:, c:c + 1],
                                                  MXB[:, c:c + 1],
                                                  MXB[:, c + 1:c + 2])
                                    VE.tensor_add(MXB[:, c + 2:c + 3],
                                                  MXB[:, c + 2:c + 3],
                                                  MXB[:, c + 3:c + 4])
                                    VE.tensor_add(SSTB[:, t:t + 1],
                                                  MXB[:, c:c + 1],
                                                  MXB[:, c + 2:c + 3])
                    ev2_last = i
                if fill:
                    pipe['wst_free'] = dc[FH * ICH - 1]
                    pipe['wbgu_done'] = [now(s_) for s_ in sWBs]
                    pipe['wbwd_done'] = [now(s_) for s_ in sWDs]
                pipe['wdb_free'] = pd[FH * ICH - 1]
                pipe['ev2_banks'] = ev2_banks

                if write_out:
                    pipe['pending_out'] = (tiles, dict(ev2_tiles))
            emit_out()
            return spars_last, ev2_last

        spars_last, ev2_last = emit_ffn(SSTA, tok_ln1, False, None, tok_x_loaded,
                                        wst_free_tok=tok_woe, fill_first=True)

        wt(nc.scalar, spars_last)
        SQ2 = LNF.bitcast(F32)[:, 0:D]
        ta = None
        for t in range(NTILE):
            wt(nc.scalar, pipe['tile_fin'][t])
            ta = SE.activation(out=SQ2, in_=xt(t), func=AF.Square,
                                          accum_out=SSTB[:, NT + t:NT + t + 1])
        tok_ln2 = ln_finalize(SSTB, ta)

        groups = []
        remaining = KCH
        k0 = 0
        for buf, width in [(LNF, KCH * TC), (WGU, 4 * KCH * 128),
                           (WSTb, 4 * KCH * 128), (WDB, 2 * FHW),
                           (SCRb, 6 * TC), (HRb, 8 * TC)]:
            nch = min(width // TLOC, remaining)
            if nch > 0:
                groups.append((buf, k0, nch))
                k0 += nch
                remaining -= nch
            if remaining == 0:
                break
        assert remaining == 0, (remaining, KCH)
        ln2f_views = []
        for buf, gk0, nch in groups:
            for q in range(nch):
                ln2f_views.append(buf[:, q * TLOC:(q + 1) * TLOC])

        l2t = []
        for t in range(NTILE):
            if t == 0:
                wt(nc.vector, tok_ln2)
            i = VE.tensor_scalar(
                out=ARA[:, t * D:(t + 1) * D], in0=xt(t),
                scalar1=SSTB[:, 2 * NT + t:2 * NT + t + 1],
                scalar2=SSTB[:, 3 * NT + t:3 * NT + t + 1],
                op0=OP.mult, op1=OP.add)
            l2t.append(i)

        ev3 = {}
        tile_done_tok = {}
        fi = 0
        for t in range(NTILE):
            for k0 in range(0, KCH, 4):
                kw = min(4, KCH - k0)
                b = fi % 2
                wt(nc.tensor, l2t[t])
                if fi >= 2:
                    wt(nc.tensor, ev3[fi - 2])
                elif ev2_last:
                    wt(nc.tensor, ev2_last)
                for kk in range(kw):
                    i = nc.tensor.transpose(
                        out=bankb(b)[:, kk * 128:(kk + 1) * 128],
                        in_=ARA[:, t * D + (k0 + kk) * 128: t * D + (k0 + kk + 1) * 128],
                        identity=IDN[:, :])
                tok = inc(i, sP)
                wt(nc.vector, tok)
                for kk in range(kw):
                    i = VE.tensor_copy(
                        out=ln2f_views[k0 + kk][:, t * 128:(t + 1) * 128],
                        in_=bankb(b)[:, kk * 128:(kk + 1) * 128])
                ev3[fi] = i
                fi += 1
            tile_done_tok[t] = now(sV)
        tok_ln2f = now(sV)

        WKc = ARA[:, 0:KCH * 128]
        WVc = ARA[:, KCH * 128:2 * KCH * 128]
        WKVST = ARAf[:, KCH * 128:2 * KCH * 128]
        need_tiles = (4 * KCH * 128 + D - 1) // D
        wt(nc.sync, tile_done_tok[min(need_tiles - 1, NTILE - 1)])
        tok = dma(nc.sync, WKVST, wk_d[:, :], sWs[0])
        wt(nc.scalar, tok)
        tok_wk = SE.activation(out=WKc, in_=WKVST, func=AF.Copy)
        wt(nc.sync, tok_wk)
        tok = dma(nc.sync, WKVST, wv_d[:, :], sWs[0])
        wt(nc.scalar, tok)
        tok_wv = SE.activation(out=WVc, in_=WKVST, func=AF.Copy)

        KLOC = ARA[:, 2 * KCH * 128:2 * KCH * 128 + TLOC]
        VLOC = ARA[:, 2 * KCH * 128 + TLOC:2 * KCH * 128 + 2 * TLOC]
        wt(nc.tensor, tok_wv)
        wt(nc.tensor, tok_ln2f)
        for k in range(KCH):
            for nn_ in range(TLOC // 512):
                i = nc.tensor.matmul(out=PS[:, nn_ * 512:(nn_ + 1) * 512],
                                     lhsT=WKc[:, k * 128:(k + 1) * 128],
                                     rhs=ln2f_views[k][:, nn_ * 512:(nn_ + 1) * 512],
                                     start=(k == 0), stop=(k == KCH - 1))
        tok_kmm = inc(i, sP)
        wt(nc.vector, tok_kmm)
        tok_kloc = VE.tensor_scalar(out=KLOC, in0=PS[:, 0:TLOC],
                                    scalar1=BKB[:, 0:1], scalar2=None, op0=OP.add)
        wt(nc.gpsimd, tok_kloc)
        tok_ksend = dma(nc.gpsimd, kvs_d[0], KLOC, sG)
        wt(nc.gpsimd, tok_ksend)
        i = nc.gpsimd.collective_compute(
            "AllGather", OP.bypass, replica_groups=replica_groups,
            ins=[kvs_d[0].opt()], outs=[kvrk_d.opt()])
        tok_agk = inc(i, sC)
        tok_kv = None
        for rnd in range((NTILE + 3) // 4):
            ms = list(range(rnd * 4, min(rnd * 4 + 4, NTILE)))
            if rnd > 0:
                wt(nc.tensor, tok_kv)
            for k in range(KCH):
                for m in ms:
                    i = nc.tensor.matmul(
                        out=PS[:, 2048 + (m % 4) * 512: 2048 + (m % 4) * 512 + 128],
                        lhsT=ln2f_views[k][:, m * 128:(m + 1) * 128],
                        rhs=WVc[:, k * 128:(k + 1) * 128],
                        start=(k == 0), stop=False)
            for m in ms:
                i = nc.tensor.matmul(
                    out=PS[:, 2048 + (m % 4) * 512: 2048 + (m % 4) * 512 + 128],
                    lhsT=ONE[0:1, :], rhs=BVR[0:1, :],
                    start=False, stop=True)
            tok_vr = inc(i, sP)
            wt(nc.vector, tok_vr)
            for m in ms:
                tok_kv = VE.tensor_copy(
                    out=VLOC[:, m * 128:(m + 1) * 128],
                    in_=PS[:, 2048 + (m % 4) * 512: 2048 + (m % 4) * 512 + 128])

        wt(nc.gpsimd, tok_kv)
        tok_vsend = dma(nc.gpsimd, kvs_d[1], VLOC, sG)
        wt(nc.gpsimd, tok_vsend)
        i = nc.gpsimd.collective_compute(
            "AllGather", OP.bypass, replica_groups=replica_groups,
            ins=[kvs_d[1].opt()], outs=[kvrv_d.opt()])
        tok_agv = inc(i, sC)
        wt(nc.gpsimd, tok_agk)
        wt(nc.gpsimd, now(sP))
        dma(nc.gpsimd, KFB[:, 0:SFULL].rearrange("p (r t) -> p r t", r=NRANK),
            kvrk_d.rearrange("r p t -> p r t"), sG)
        tok_kfull = now(sG)
        wt(nc.gpsimd, tok_agv)
        dma(nc.gpsimd, VKB[:, 0:SFULL].rearrange("p (r t) -> p r t", r=NRANK),
            kvrv_d.rearrange("r p t -> p r t"), sG)
        tok_vkbfull = now(sG)

        qev = {}
        for m0 in range(0, NTILE, 4):
            b = (m0 // 4) % 2
            mw = min(4, NTILE - m0)
            if m0 == 0:
                wt(nc.tensor, tok_kv)
            if m0 >= 8:
                wt(nc.tensor, qev[m0 - 8])
            for mm in range(mw):
                i = nc.tensor.transpose(out=bankb(b)[:, mm * 128:(mm + 1) * 128],
                                        in_=VLOC[:, (m0 + mm) * 128:(m0 + mm + 1) * 128],
                                        identity=IDN[:, :])
            tok = inc(i, sP)
            wt(nc.vector, tok)
            i = VE.tensor_copy(out=QF[:, m0 * 128:(m0 + mw) * 128],
                                      in_=bankb(b)[:, 0:mw * 128])
            qev[m0] = i
        tok_qf = now(sV)

        SCb = ARAf[:, 0:SFULL]
        PB = ARA[:, 2 * SFULL:3 * SFULL]
        PT = ARA[:, 3 * SFULL:4 * SFULL]
        IZB = MXB[:, NKB + 8:NKB + 8 + NT]
        cps = {}
        af_tok = {}
        exp_toks = {}
        ptv_last = {}
        prev_ptv = {}
        ztr_tok = {}
        zrow_ev = {}

        def score_emit(q):
            for ib in range(NKB):
                b = ib % 4
                if q == 0 and ib == 0:
                    wt(nc.tensor, tok_qf)
                    wt(nc.tensor, tok_kfull)
                if ib >= 4:
                    wt(nc.tensor, cps[(q, ib - 4)])
                elif q > 0:
                    wt(nc.tensor, cps[(q - 1, NKB - 4 + ib)])
                i = nc.tensor.matmul(out=bank(b),
                                     lhsT=QF[:, q * 128:(q + 1) * 128],
                                     rhs=KFB[:, ib * 512:(ib + 1) * 512],
                                     start=True, stop=True)
                tok = inc(i, sP)
                if ib % 2 == 0:
                    wt(nc.scalar, tok)
                    if q == 0 and ib == 0:
                        wt(nc.scalar, tok_vsend)
                    cps[(q, ib)] = SE.activation(
                        out=SCb[:, ib * 512:(ib + 1) * 512],
                        in_=bank(b), func=AF.Copy)
                else:
                    wt(nc.vector, tok)
                    if q == 0 and ib == 1:
                        wt(nc.vector, tok_vsend)
                    cps[(q, ib)] = VE.tensor_copy(
                        out=SCb[:, ib * 512:(ib + 1) * 512], in_=bank(b))

        score_emit(0)
        for qt in range(NTILE):
            mb = NKB + 4 * (qt % 2)
            if qt > 0:
                wt(nc.scalar, ptv_last[qt - 1])
            wt(nc.scalar, cps[(qt, NKB - 1)])
            HS = SFULL // 2
            e0 = SE.activation(out=PB[:, 0:HS], in_=SCb[:, 0:HS], func=AF.Exp,
                               scale=1.0 / 128.0,
                               accum_out=MXB[:, mb + 1:mb + 2])
            e1 = SE.activation(out=PB[:, HS:SFULL], in_=SCb[:, HS:SFULL],
                               func=AF.Exp, scale=1.0 / 128.0,
                               accum_out=MXB[:, mb + 2:mb + 3])
            exp_toks[qt] = (e0, e1)
            wt(nc.vector, e1)
            VE.tensor_add(MXB[:, mb + 1:mb + 2], MXB[:, mb + 1:mb + 2],
                          MXB[:, mb + 2:mb + 3])
            VE.reciprocal(IZB[:, qt:qt + 1], MXB[:, mb + 1:mb + 2])
            if qt >= 2:
                wt(nc.vector, ztr_tok[qt - 2])
            tok_zcb = VE.tensor_copy(out=ZCB[:, qt % 2:qt % 2 + 1],
                                     in_=MXB[:, mb + 1:mb + 2])
            if qt + 1 < NTILE:
                score_emit(qt + 1)
            ptv = {}
            for ci0 in range(0, VCH, 4):
                b = 4 + (ci0 // 4) % 2
                if ci0 == 0:
                    wt(nc.tensor, exp_toks[qt][0])
                    wt(nc.tensor, prev_ptv.get(VCH - 8))
                if ci0 == 4:
                    wt(nc.tensor, prev_ptv.get(VCH - 4))
                if ci0 == VCH // 2:
                    wt(nc.tensor, exp_toks[qt][1])
                if ci0 >= 8:
                    wt(nc.tensor, ptv[ci0 - 8])
                for cc in range(4):
                    i = nc.tensor.transpose(
                        out=bankb(b)[:, cc * 128:(cc + 1) * 128],
                        in_=PB[:, (ci0 + cc) * 128:(ci0 + cc + 1) * 128],
                        identity=IDN[:, :])
                tok = inc(i, sP)
                ptv_last[qt] = tok
                wt(nc.vector, tok)
                i = VE.tensor_copy(out=PT[:, ci0 * 128:(ci0 + 4) * 128],
                                          in_=bankb(b)[:, 0:512])
                ptv[ci0] = i
            prev_ptv = ptv
            tok_pt = now(sV)
            wt(nc.tensor, tok_pt)
            if qt == 0:
                wt(nc.tensor, tok_vkbfull)
            if qt >= 2:
                wt(nc.tensor, af_tok[qt - 2])
            bav = 6 + (qt % 2)
            for ci in range(VCH):
                i = nc.tensor.matmul(out=PS[:, bav * 512:bav * 512 + 128],
                                     lhsT=VKB[:, ci * 128:(ci + 1) * 128],
                                     rhs=PT[:, ci * 128:(ci + 1) * 128],
                                     start=(ci == 0), stop=(ci == VCH - 1))
            tok_av = inc(i, sP)
            bav2 = 6 + ((qt + 1) % 2)
            wt(nc.tensor, tok_zcb)
            if qt >= 2:
                wt(nc.tensor, zrow_ev[qt - 2])
            i = nc.tensor.matmul(out=PS[0:1, bav2 * 512 + 128:bav2 * 512 + 256],
                                 lhsT=ZCB[:, qt % 2:qt % 2 + 1], rhs=IDN[:, :],
                                 start=True, stop=True)
            ztr_tok[qt] = inc(i, sP)
            wt(nc.vector, tok_av)
            af_tok[qt] = VE.tensor_copy(out=AFB[:, qt * 128:(qt + 1) * 128],
                                        in_=PS[:, bav * 512:bav * 512 + 128])
            wt(nc.vector, ztr_tok[qt])
            zrow_ev[qt] = VE.tensor_copy(
                out=ZROWb[0:1, qt * 128:(qt + 1) * 128],
                in_=PS[0:1, bav2 * 512 + 128:bav2 * 512 + 256])
        tok_af = now(sV)

        wo_mm = {}
        wo_ev = {}
        for t in range(NTILE):
            b0 = (t % 2) * 4
            if t == 0:
                wt(nc.tensor, tok_af)
            elif t >= 2:
                wt(nc.tensor, wo_ev[t - 2])
            for nn_ in range((D + 511) // 512):
                ww = min(512, D - nn_ * 512)
                i = nc.tensor.matmul(out=PS[:, b0 * 512 + nn_ * 512: b0 * 512 + nn_ * 512 + ww],
                                     lhsT=AFB[:, t * 128:(t + 1) * 128],
                                     rhs=WOE[:, nn_ * 512:nn_ * 512 + ww],
                                     start=True, stop=False)
                i = nc.tensor.matmul(out=PS[:, b0 * 512 + nn_ * 512: b0 * 512 + nn_ * 512 + ww],
                                     lhsT=ZROWb[0:1, t * 128:(t + 1) * 128],
                                     rhs=BOR[0:1, nn_ * 512:nn_ * 512 + ww],
                                     start=False, stop=True)
            wo_mm[t] = inc(i, sP)
            wt(nc.vector, wo_mm[t])
            i = VE.scalar_tensor_tensor(out=xt(t), in0=PS[:, b0 * 512:b0 * 512 + D],
                                        scalar=IZB[:, t:t + 1], in1=xt(t),
                                        op0=OP.mult, op1=OP.add,
                                        accum_out=SSTA[:, t:t + 1])
            wo_ev[t] = i
        tok_x2 = now(sV)

        SQ3 = ARAf[:, 0:D]
        ta = None
        for t in range(NTILE):
            wt(nc.scalar, wo_ev[t])
            ta = SE.activation(out=SQ3, in_=xt(t), func=AF.Square,
                                          accum_out=SSTA[:, NT + t:NT + t + 1])
        tok_ln3 = ln_finalize(SSTA, ta)

        wt(nc.sync, ztr_tok[NTILE - 1])
        pipe['wgu_free'] = ztr_tok[NTILE - 1]
        pipe['wst_attn_free'] = wo_mm[NTILE - 1]
        emit_ffn(SSTA, tok_ln3, True, (sV, tok_x2[1]), (sV, tok_x2[1]))

        for s_ in sGs:
            nc.gpsimd.wait_ge(s_, cnt[id(s_)])
        nc.gpsimd.wait_ge(sG, cnt[id(sG)])

    return nc


def shard_inputs(cfg, L0_attn_q, L0_attn_Wk, L0_attn_bk, L0_attn_Wv, L0_attn_bv,
                 L0_attn_Wo, L0_attn_bo, L0_ffn_w_gate, L0_ffn_w_up, L0_ffn_w_down):
    D = cfg["D"]; I = cfg["I"]; TLOC = cfg["TLOC"]; n_cores = cfg["n_cores"]
    KCH = D // 128; GC = I // 512

    x = np.ascontiguousarray(np.asarray(L0_attn_q, dtype=np.float32).reshape(-1, D))

    def layout_gu(w):
        w = np.asarray(w, dtype=np.float32)
        w5 = w.reshape(KCH, 128, GC, 128, 4)
        return np.ascontiguousarray(w5.transpose(4, 2, 1, 0, 3).reshape(4, GC, 128, KCH * 128))

    def layout_wd(w):
        w = np.asarray(w, dtype=np.float32)
        w4 = w.reshape(GC, 128, 4, D)
        return np.ascontiguousarray(w4.transpose(0, 2, 1, 3))

    def layout_wkv(w):
        w = np.asarray(w, dtype=np.float32)
        return np.ascontiguousarray(
            w.reshape(KCH, 128, 128).transpose(1, 0, 2).reshape(128, KCH * 128))

    base = {
        "wg": layout_gu(L0_ffn_w_gate),
        "wu": layout_gu(L0_ffn_w_up),
        "wd": layout_wd(L0_ffn_w_down),
        "wk": layout_wkv(L0_attn_Wk),
        "wv": layout_wkv(L0_attn_Wv),
        "wo": np.ascontiguousarray(np.asarray(L0_attn_Wo, dtype=np.float32).reshape(KCH, 128, D)),
        "bk": np.ascontiguousarray(np.asarray(L0_attn_bk, dtype=np.float32).reshape(128, 1)),
        "bv": np.ascontiguousarray(np.asarray(L0_attn_bv, dtype=np.float32).reshape(1, 128)),
        "bo": np.ascontiguousarray(np.asarray(L0_attn_bo, dtype=np.float32).reshape(1, D)),
        "ident": np.eye(128, dtype=BF),
    }
    in_maps = []
    for c in range(n_cores):
        m = dict(base)
        m["x"] = np.ascontiguousarray(x[c * TLOC:(c + 1) * TLOC])
        in_maps.append(m)
    return in_maps


def kernel(L0_attn_q, L0_attn_Wq, L0_attn_bq, L0_attn_Wk, L0_attn_bk,
           L0_attn_Wv, L0_attn_bv, L0_attn_Wo, L0_attn_bo,
           L0_ffn_w_gate, L0_ffn_w_up, L0_ffn_w_down):
    cfg = full_cfg()
    in_maps = shard_inputs(cfg, L0_attn_q, L0_attn_Wk, L0_attn_bk, L0_attn_Wv,
                           L0_attn_bv, L0_attn_Wo, L0_attn_bo,
                           L0_ffn_w_gate, L0_ffn_w_up, L0_ffn_w_down)
    nc = build_nc(cfg)
    res = run_bass_kernel_spmd(nc, in_maps, core_ids=list(range(cfg["n_cores"])))
    out = np.concatenate([res.results[c]["out"] for c in range(cfg["n_cores"])], axis=0)
    return out.reshape(2, 8192, cfg["D"]).astype(np.float32)
